# version 28 (final)
# baseline (speedup 1.0000x reference)
import numpy as np
from contextlib import ExitStack

import concourse.bass as bass
import concourse.mybir as mybir
from concourse.bass_utils import run_bass_kernel_spmd

F32 = mybir.dt.float32
BF16 = mybir.dt.bfloat16
I32 = mybir.dt.int32
AF = mybir.ActivationFunctionType
ALU = mybir.AluOpType
AX = mybir.AxisListType

SEQ, DM, HEADS, GROUPS, DH, HID = 4096, 1024, 16, 4, 64, 3072
NTT = SEQ // 128
QTB = 512
NQTB = SEQ // QTB
DEPTH = 16
NPAIR = HEADS // 2
NCH = SEQ // 128
NU = NQTB * NPAIR
TC = 512
NTC = SEQ // TC
NHC = HID // 128
NSLAB = NHC // 2
EPS1, EPS2 = 1e-5, 1e-6
QS = 0.125
DQ2 = 1.0 / (127.0 * 127.0)

AV_, AA_, AP_ = 10, 6, 7
C_SP_BASE = AP_ * NTT


class _SkipPhase(Exception):
    pass


def build(taps=(), phases="0ABC", cmode="full"):
    nc = bass.Bass("TRN2", target_bir_lowering=False)

    x_h = nc.dram_tensor("x", [SEQ, DM], F32, kind="ExternalInput").ap()
    wq_h = nc.dram_tensor("wq", [DM, HEADS * DH], F32, kind="ExternalInput").ap()
    wkv_h = nc.dram_tensor("wkv", [DM, 2 * GROUPS * DH], F32, kind="ExternalInput").ap()
    wout_h = nc.dram_tensor("wout", [HEADS * DH, DM], F32, kind="ExternalInput").ap()
    wg_h = nc.dram_tensor("wg", [HID, DM], I32, kind="ExternalInput").ap()
    wu_h = nc.dram_tensor("wu", [HID, DM], I32, kind="ExternalInput").ap()
    wd_h = nc.dram_tensor("wd", [DM, HID], I32, kind="ExternalInput").ap()
    lnsc_h = nc.dram_tensor("ln_sc", [DM], F32, kind="ExternalInput").ap()
    lnb_h = nc.dram_tensor("ln_b", [DM], F32, kind="ExternalInput").ap()
    qnsc_h = nc.dram_tensor("qn_sc", [DH], F32, kind="ExternalInput").ap()
    qnb_h = nc.dram_tensor("qn_b", [DH], F32, kind="ExternalInput").ap()
    knsc_h = nc.dram_tensor("kn_sc", [DH], F32, kind="ExternalInput").ap()
    knb_h = nc.dram_tensor("kn_b", [DH], F32, kind="ExternalInput").ap()
    out_h = nc.dram_tensor("out", [SEQ, DM], F32, kind="ExternalOutput").ap()

    _wk = "ExternalOutput" if "wscr" in taps else "Internal"
    wgT_scr = nc.dram_tensor("wgT_scr", [128, NHC, 8, 128], BF16, kind=_wk).ap()
    wuT_scr = nc.dram_tensor("wuT_scr", [128, NHC, 8, 128], BF16, kind=_wk).ap()
    wdT_scr = nc.dram_tensor("wdT_scr", [128, NHC, 8, 128], BF16, kind=_wk).ap()
    s_scr = nc.dram_tensor("s_scr", [8, 128, SEQ], BF16,
                           kind="ExternalOutput" if "ssc" in taps else "Internal").ap()
    q_scr = nc.dram_tensor("q_scr", [NQTB, 128, NPAIR * QTB], BF16,
                           kind="ExternalOutput" if "qscr" in taps else "Internal").ap()

    tp = {}
    for name, shape, dt in [
        ("ksb", [128, GROUPS * SEQ], BF16),
        ("vsb", [128, GROUPS * NCH * 65], BF16), ("negc", [128, HEADS], F32),
        ("stats", [128, 2 * NTT], F32), ("gg", [128, 8 * TC], BF16),
        ("hb", [128, NHC * TC], BF16), ("xnf", [128, 8 * TC], BF16),
    ]:
        if name in taps:
            tp[name] = nc.dram_tensor("tap_" + name, shape, dt, kind="ExternalOutput").ap()

    with ExitStack() as top:
        E = top.enter_context
        stats_sb = E(nc.sbuf_tensor("stats_sb", [128, 2 * NTT], F32))
        negc_sb = E(nc.sbuf_tensor("negc_sb", [128, HEADS], F32))
        id_bf = E(nc.sbuf_tensor("id_bf", [128, 128], BF16))
        id_f = E(nc.sbuf_tensor("id_f", [128, 128], F32))
        ones1f = E(nc.sbuf_tensor("ones1f", [1, 128], F32))
        ones64f = E(nc.sbuf_tensor("ones64f", [64, 64], F32))
        ones64b = E(nc.sbuf_tensor("ones64b", [64, 64], BF16))
        epsv = E(nc.sbuf_tensor("epsv", [128, 3], F32))

        S = {}
        for s_ in ["dA0", "dA1", "dW", "dS", "dSp0", "dSp1", "dSr0", "dSr1",
                   "dStr", "dSl0", "dSl1", "dXC0", "dXC1", "dOUT0", "dOUT1",
                   "dQ0", "dQ1", "dQA0", "dQA1", "sMV2", "sEXP1", "sQK1",
                   "dS0", "dS1", "s0P", "s0V", "d0L0", "d0L1", "sAVC", "sXF",
                   "dXC2",
                   "sP", "sV", "sA", "sQK", "sEXP", "sAV", "sRCP", "sNRM",
                   "sAN", "sBC", "sMV", "sSE", "sWO", "sPT2",
                   "sGU", "sSI", "sH", "sDN", "sG", "sPm", "sAD",
                   "sXN", "sXNF"]:
            S[s_] = nc.alloc_semaphore(name=s_)
        nW = [0]
        nS = [0]

        ab = top.enter_context(ExitStack())
        EAB = ab.enter_context
        k_sb = EAB(nc.sbuf_tensor("k_sb", [128, GROUPS * SEQ], BF16))
        v_sb = EAB(nc.sbuf_tensor("v_sb", [128, GROUPS * NCH * 65], BF16))

        with ExitStack() as p0:
            E0 = p0.enter_context
            wq_sb = E0(nc.sbuf_tensor("wq_sb", [128, 8 * 1024], BF16))
            wkv_sb = E0(nc.sbuf_tensor("wkv_sb", [128, 8 * 512], BF16))
            vec_sb = E0(nc.sbuf_tensor("vec_sb", [1, 2 * DM + 4 * DH], F32))
            lnsc_f = E0(nc.sbuf_tensor("lnsc_f", [128, DM], F32))
            lnb_f = E0(nc.sbuf_tensor("lnb_f", [128, DM], F32))
            qnsc_f = E0(nc.sbuf_tensor("qnsc_f", [128, DM], F32))
            qnb_f = E0(nc.sbuf_tensor("qnb_f", [128, DM], F32))
            knsc_f = E0(nc.sbuf_tensor("knsc_f", [128, GROUPS * DH], F32))
            knb_f = E0(nc.sbuf_tensor("knb_f", [128, GROUPS * DH], F32))

            with nc.Block() as blk:
                @blk.gpsimd
                def _(g):
                    g.dma_start(wq_sb[:, :].rearrange("p (c n) -> p c n", c=8),
                                wq_h.rearrange("(c p) n -> p c n", p=128)).then_inc(S["dW"], 16)
                    g.dma_start(wkv_sb[:, :].rearrange("p (c n) -> p c n", c=8),
                                wkv_h.rearrange("(c p) n -> p c n", p=128)).then_inc(S["dW"], 16)
                    for i, h in enumerate([lnsc_h, lnb_h]):
                        g.dma_start(vec_sb[0:1, i * DM:(i + 1) * DM], h[None, :]
                                    ).then_inc(S["dW"], 16)
                    for i, h in enumerate([qnsc_h, qnb_h, knsc_h, knb_h]):
                        g.dma_start(vec_sb[0:1, 2 * DM + i * DH:2 * DM + (i + 1) * DH],
                                    h[None, :]).then_inc(S["dW"], 16)
                    nW[0] += 8 * 16
                    for t_ in (id_bf, id_f):
                        g.memset(t_[:, :], 0.0)
                    g.memset(ones1f[:, :], 1.0)
                    g.memset(ones64f[:, :], 1.0)
                    g.memset(ones64b[:, :], 1.0)
                    g.memset(epsv[:, 0:1], EPS1)
                    g.memset(epsv[:, 1:2], EPS2)
                    g.memset(epsv[:, 2:3], EPS1 * EPS2)
                    g.memset(v_sb[:, :].rearrange("p (c s) -> p c s", s=65)[:, :, 64:65], 1.0)
                    g.wait_ge(S["dW"], nW[0])

            with nc.Block() as blk:
                @blk.gpsimd
                def _(g):
                    for t_ in (id_bf, id_f):
                        g.affine_select(out=t_[:, :], in_=t_[:, :],
                                        compare_op=ALU.not_equal, fill=1.0,
                                        base=0, pattern=[[-1, 128]], channel_multiplier=1)

            with ExitStack() as bc:
                bps = bc.enter_context(nc.psum_tensor("bps", [128, DM], F32))
                for (dst_fill, src_lo, src_n) in [
                        ("ln", 0, DM), ("ln2", DM, DM), ("hd", 2 * DM, 4 * DH)]:
                    with nc.Block() as blk:
                        @blk.tensor
                        def _(t, lo=src_lo, n=src_n):
                            for o in range(0, n, 512):
                                w = min(512, n - o)
                                t.matmul(bps[:, o:o + w], ones1f[0:1, 0:128],
                                         vec_sb[0:1, lo + o:lo + o + w],
                                         start=True, stop=True)
                    with nc.Block() as blk:
                        @blk.vector
                        def _(v, tag=dst_fill, n=src_n):
                            if tag == "ln":
                                v.tensor_copy(lnsc_f[:, :], bps[:, 0:DM])
                            elif tag == "ln2":
                                v.tensor_copy(lnb_f[:, :], bps[:, 0:DM])
                            else:
                                for r in range(HEADS):
                                    v.tensor_scalar(qnsc_f[:, r * DH:(r + 1) * DH],
                                                    bps[:, 0:DH], QS, None,
                                                    ALU.mult, ALU.bypass)
                                    v.tensor_scalar(qnb_f[:, r * DH:(r + 1) * DH],
                                                    bps[:, DH:2 * DH], QS, None,
                                                    ALU.mult, ALU.bypass)
                                for r in range(GROUPS):
                                    v.tensor_copy(knsc_f[:, r * DH:(r + 1) * DH],
                                                  bps[:, 2 * DH:3 * DH])
                                    v.tensor_copy(knb_f[:, r * DH:(r + 1) * DH],
                                                  bps[:, 3 * DH:4 * DH])

            with ExitStack() as p0b:
                E0b = p0b.enter_context
                wint = E0b(nc.sbuf_tensor("wint", [128, 2 * HID], BF16))
                wstg = E0b(nc.sbuf_tensor("wstg", [128, 2 * HID], BF16))
                tps = E0b(nc.psum_tensor("tps", [128, 1024], BF16))
                jobs = []
                for srch, dsth in [(wg_h, wgT_scr), (wu_h, wuT_scr)]:
                    for hc in range(NHC):
                        jobs.append(("gu", srch, dsth, hc))
                for dc in range(8):
                    jobs.append(("wd", wd_h, wdT_scr, dc))
                rounds = [1 if j[0] == "gu" else 3 for j in jobs]
                rtot = [0]
                for r_ in rounds:
                    rtot.append(rtot[-1] + r_)
                NJ = len(jobs)

                with nc.Block() as blk:
                    @blk.gpsimd
                    def _(g):
                        nl = [0, 0]
                        for j, (kind, srch, dsth, idx) in enumerate(jobs):
                            pj = j % 2
                            if j >= 2:
                                g.wait_ge(S["s0P"], rtot[j - 1])
                            ncols = 1024 if kind == "gu" else HID
                            g.dma_start(wint[:, pj * HID:pj * HID + ncols],
                                        srch[idx * 128:(idx + 1) * 128, :]
                                        ).then_inc(S["d0L%d" % pj], 16)

                    @blk.tensor
                    def _(t):
                        nload = [0, 0]
                        for j, (kind, srch, dsth, idx) in enumerate(jobs):
                            pj = j % 2
                            nload[pj] += 16
                            t.wait_ge(S["d0L%d" % pj], nload[pj])
                            for r_ in range(rounds[j]):
                                if rtot[j] + r_ >= 1:
                                    t.wait_ge(S["s0V"], rtot[j] + r_)
                                for i in range(8):
                                    ins = t.transpose(
                                        tps[:, i * 128:(i + 1) * 128],
                                        wint[:, pj * HID + (r_ * 8 + i) * 128:
                                             pj * HID + (r_ * 8 + i + 1) * 128],
                                        id_bf[:, :])
                                ins.then_inc(S["s0P"], 1)

                    @blk.vector
                    def _(v):
                        nst = [0, 0]
                        for j, (kind, srch, dsth, idx) in enumerate(jobs):
                            pj = j % 2
                            if j >= 2:
                                nst[pj] += 16
                                v.wait_ge(S["dS%d" % pj], nst[pj])
                            for r_ in range(rounds[j]):
                                v.wait_ge(S["s0P"], rtot[j] + r_ + 1)
                                v.tensor_copy(wstg[:, pj * HID + r_ * 1024:
                                                   pj * HID + (r_ + 1) * 1024],
                                              tps[:, :]).then_inc(S["s0V"], 1)

                    @blk.sync
                    def _(sy):
                        for j, (kind, srch, dsth, idx) in enumerate(jobs):
                            pj = j % 2
                            sy.wait_ge(S["s0V"], rtot[j + 1])
                            if kind == "gu":
                                sy.dma_start(dsth[:, idx],
                                             wstg[:, pj * HID:pj * HID + 1024
                                                  ].rearrange("p (dc f) -> p dc f", dc=8)
                                             ).then_inc(S["dS%d" % pj], 16)
                            else:
                                sy.dma_start(dsth[:, :, idx],
                                             wstg[:, pj * HID:(pj + 1) * HID
                                                  ].rearrange("p (hc f) -> p hc f", hc=NHC)
                                             ).then_inc(S["dS%d" % pj], 16)
                        sy.wait_ge(S["dS0"], 16 * (NJ - NJ // 2))
                        sy.wait_ge(S["dS1"], 16 * (NJ // 2))

            with ExitStack() as pA:
              if "A" in phases:
                EA = pA.enter_context
                xt = EA(nc.sbuf_tensor("xt", [128, 2 * DM], F32))
                xn_t = EA(nc.sbuf_tensor("xn_t", [128, DM], F32))
                xn2_t = EA(nc.sbuf_tensor("xn2_t", [128, DM], BF16))
                xn2f_t = EA(nc.sbuf_tensor("xn2f_t", [128, DM], BF16))
                qtmp = EA(nc.sbuf_tensor("qtmp", [128, DM], F32))
                qs2 = EA(nc.sbuf_tensor("qs2", [128, DM], F32))
                ks2 = EA(nc.sbuf_tensor("ks2", [128, 256], F32))
                qaff = EA(nc.sbuf_tensor("qaff", [128, DM], F32))
                kaff = EA(nc.sbuf_tensor("kaff", [128, 256], F32))
                qsqf = EA(nc.sbuf_tensor("qsqf", [128, DM], F32))
                ksqf = EA(nc.sbuf_tensor("ksqf", [128, 256], F32))
                qhat = EA(nc.sbuf_tensor("qhat", [128, DM], BF16))
                khat = EA(nc.sbuf_tensor("khat", [128, GROUPS * DH], BF16))
                bnb = EA(nc.sbuf_tensor("bnb", [128, 12], F32))
                qstat = EA(nc.sbuf_tensor("qstat", [128, 32], F32))
                kstat = EA(nc.sbuf_tensor("kstat", [128, 8], F32))
                st1 = EA(nc.sbuf_tensor("st1", [128, 2], F32))
                st2 = EA(nc.sbuf_tensor("st2", [128, 2], F32))
                sd1 = EA(nc.sbuf_tensor("sd1", [128, 4], F32))
                qmu = EA(nc.sbuf_tensor("qmu", [128, 16], F32))
                qstd = EA(nc.sbuf_tensor("qstd", [128, 16], F32))
                kstd = EA(nc.sbuf_tensor("kstd", [128, 4], F32))
                qrr = EA(nc.sbuf_tensor("qrr", [128, 16], F32))
                kmu = EA(nc.sbuf_tensor("kmu", [128, 4], F32))
                krr = EA(nc.sbuf_tensor("krr", [128, 4], F32))
                qvar = EA(nc.sbuf_tensor("qvar", [128, 16], F32))
                kvar = EA(nc.sbuf_tensor("kvar", [128, 4], F32))
                nmr = EA(nc.sbuf_tensor("nmr", [128, 20], F32))
                ascr = EA(nc.sbuf_tensor("ascr", [128, 20], F32))
                q_stage = EA(nc.sbuf_tensor("q_stage", [128, 2 * NPAIR * 128], BF16))
                qsq = EA(nc.sbuf_tensor("qsq", [128, 16], F32))
                ksq = EA(nc.sbuf_tensor("ksq", [128, 4], F32))
                qmaxsq = EA(nc.sbuf_tensor("qmaxsq", [128, 16], F32))
                kmaxsq = EA(nc.sbuf_tensor("kmaxsq", [128, 4], F32))
                maxrow = EA(nc.sbuf_tensor("maxrow", [16, 2], F32))
                normc = EA(nc.sbuf_tensor("normc", [16, 2], F32))
                crow = EA(nc.sbuf_tensor("crow", [1, 40], F32))
                negrow = EA(nc.sbuf_tensor("negrow", [1, 16], F32))
                q_ps = EA(nc.psum_tensor("q_ps", [128, 2 * DM], F32))
                kv_ps = EA(nc.psum_tensor("kv_ps", [128, 2 * 512], F32))
                tq_ps = EA(nc.psum_tensor("tq_ps", [128, 1536], BF16))

                with nc.Block() as blk:
                    @blk.gpsimd
                    def _(g):
                        g.memset(qmaxsq[:, :], 0.0)
                        g.memset(kmaxsq[:, :], 0.0)

                sv_f = [[0] * 5 for _ in range(NTT)]
                sv_b = [dict() for _ in range(NTT)]
                sa_f = [dict() for _ in range(NTT)]
                sa_b = [dict() for _ in range(NTT)]
                cnt = 0
                acnt = 0
                for it in range(NTT + 1):
                    if it < NTT:
                        for k in (1, 2, 3, 4):
                            cnt += 1
                            sv_f[it][k] = cnt
                    if it >= 1:
                        for k in ("B5", "B6", "B7", "B8", "B9", "B10", "B12"):
                            cnt += 1
                            sv_b[it - 1][k] = cnt
                    if it < NTT:
                        for k in ("a1", "a2"):
                            acnt += 1
                            sa_f[it][k] = acnt
                    if it >= 1:
                        for k in ("a3", "a4", "a5", "a6", "a7", "a8", "a9", "a10"):
                            acnt += 1
                            sa_b[it - 1][k] = acnt
                sp_x2T = [1 if it == 0 else 7 * it - 3 for it in range(NTT)]
                sp_proj = [3 if it == 0 else 7 * it - 1 for it in range(NTT)]
                sp_qkT = [7 * bt + 10 for bt in range(NTT)]
                sp_qkT[NTT - 1] = 7 * NTT
                R64 = 1.0 / 64.0

                with nc.Block() as blk:
                    @blk.vector
                    def _(v):
                        for it in range(NTT + 1):
                            if it < NTT:
                                par = it % 2
                                xv = xt[:, par * DM:(par + 1) * DM]
                                v.wait_ge(S["dA%d" % par], 16 * (it // 2 + 1))
                                v.bn_stats(bnb[:, 0:6], xv[:, 0:512])
                                v.bn_stats(bnb[:, 6:12], xv[:, 512:1024])
                                v.drain()
                                v.bn_aggr(st1[:, :], bnb[:, 0:12])
                                v.drain()
                                v.tensor_copy(stats_sb[:, it:it + 1], st1[:, 0:1]
                                              ).then_inc(S["sV"], 1)
                                v.scalar_tensor_tensor(qtmp[:, :], xv, st1[:, 0:1],
                                                       lnsc_f[:, :], ALU.subtract, ALU.mult
                                                       ).then_inc(S["sXF"], 1)
                                v.drain()
                                v.wait_ge(S["sA"], sa_f[it]["a1"])
                                v.reciprocal(stats_sb[:, NTT + it:NTT + it + 1],
                                             sd1[:, 0:1]).then_inc(S["sV"], 1)
                                v.wait_ge(S["sA"], sa_f[it]["a2"])
                                v.reciprocal(sd1[:, 3:4], sd1[:, 1:2])
                                v.drain()
                                if it >= 1:
                                    v.wait_ge(S["sP"], sp_x2T[it - 1])
                                v.scalar_tensor_tensor(xn2_t[:, :], qtmp[:, :], sd1[:, 3:4],
                                                       lnb_f[:, :], ALU.mult, ALU.add
                                                       ).then_inc(S["sV"], 1)
                                v.wait_ge(S["sP"], sp_x2T[it])
                                v.tensor_copy(xn2f_t[:, :], tq_ps[:, 0:DM]
                                              ).then_inc(S["sV"], 1)
                            if it >= 1:
                                bt = it - 1
                                bpar = bt % 2
                                qp = q_ps[:, bpar * DM:(bpar + 1) * DM]
                                kvp = kv_ps[:, bpar * 512:(bpar + 1) * 512]
                                v.wait_ge(S["sP"], sp_proj[bt])
                                v.tensor_reduce(qstat[:, 0:16],
                                                qp.rearrange("p (s f) -> p s f", f=64),
                                                AX.X, ALU.add)
                                v.drain()
                                v.wait_ge(S["sA"], sa_b[bt]["a3"])
                                v.tensor_reduce(qstat[:, 16:32],
                                                qs2[:, :].rearrange("p (s f) -> p s f", f=64),
                                                AX.X, ALU.add)
                                v.drain()
                                v.tensor_scalar(qmu[:, :], qstat[:, 0:16], R64, None,
                                                ALU.mult, ALU.bypass)
                                v.drain()
                                v.tensor_tensor(ascr[:, 0:16], qmu[:, :], qmu[:, :], ALU.mult)
                                v.drain()
                                v.scalar_tensor_tensor(qvar[:, :], qstat[:, 16:32], R64,
                                                       ascr[:, 0:16], ALU.mult, ALU.subtract
                                                       ).then_inc(S["sV"], 1)
                                v.tensor_reduce(kstat[:, 0:4],
                                                kvp[:, 0:256].rearrange("p (s f) -> p s f", f=64),
                                                AX.X, ALU.add)
                                v.drain()
                                v.wait_ge(S["sA"], sa_b[bt]["a4"])
                                v.tensor_reduce(kstat[:, 4:8],
                                                ks2[:, :].rearrange("p (s f) -> p s f", f=64),
                                                AX.X, ALU.add)
                                v.drain()
                                v.tensor_scalar(kmu[:, :], kstat[:, 0:4], R64, None,
                                                ALU.mult, ALU.bypass)
                                v.drain()
                                v.tensor_tensor(ascr[:, 16:20], kmu[:, :], kmu[:, :], ALU.mult)
                                v.drain()
                                v.scalar_tensor_tensor(kvar[:, :], kstat[:, 4:8], R64,
                                                       ascr[:, 16:20], ALU.mult, ALU.subtract
                                                       ).then_inc(S["sV"], 1)
                                v.wait_ge(S["sA"], sa_b[bt]["a5"])
                                v.reciprocal(qrr[:, :], qstd[:, :])
                                v.drain()
                                v.scalar_tensor_tensor(nmr[:, 0:16], qmu[:, :], -1.0,
                                                       qrr[:, :], ALU.mult, ALU.mult
                                                       ).then_inc(S["sV"], 1)
                                v.wait_ge(S["sA"], sa_b[bt]["a6"])
                                v.reciprocal(krr[:, :], kstd[:, :])
                                v.drain()
                                v.scalar_tensor_tensor(nmr[:, 16:20], kmu[:, :], -1.0,
                                                       krr[:, :], ALU.mult, ALU.mult
                                                       ).then_inc(S["sV"], 1)
                                v.wait_ge(S["sA"], sa_b[bt]["a7"])
                                v.tensor_tensor(qtmp[:, :], qaff[:, :], qnsc_f[:, :], ALU.mult)
                                v.drain()
                                v.tensor_tensor(qhat[:, :], qtmp[:, :], qnb_f[:, :],
                                                ALU.add).then_inc(S["sV"], 1)
                                v.drain()
                                v.wait_ge(S["sA"], sa_b[bt]["a8"])
                                v.tensor_tensor(qtmp[:, 0:256], kaff[:, :], knsc_f[:, :],
                                                ALU.mult)
                                v.drain()
                                v.tensor_tensor(khat[:, :], qtmp[:, 0:256], knb_f[:, :],
                                                ALU.add).then_inc(S["sV"], 1)
                                v.wait_ge(S["sA"], sa_b[bt]["a9"])
                                v.tensor_reduce(qsq[:, :],
                                                qsqf[:, :].rearrange("p (s f) -> p s f", f=64),
                                                AX.X, ALU.add)
                                v.drain()
                                v.tensor_max(qmaxsq[:, :], qmaxsq[:, :], qsq[:, :])
                                v.wait_ge(S["sA"], sa_b[bt]["a10"])
                                v.tensor_reduce(ksq[:, :],
                                                ksqf[:, :].rearrange("p (s f) -> p s f", f=64),
                                                AX.X, ALU.add)
                                v.drain()
                                v.tensor_max(kmaxsq[:, :], kmaxsq[:, :], ksq[:, :])
                                v.tensor_copy(
                                    v_sb[:, :].rearrange("p (g c s) -> p g c s",
                                                         g=GROUPS, c=NCH)[:, :, bt, 0:64],
                                    kvp[:, 256:512].rearrange("p (g s) -> p g s", g=GROUPS))
                                if bt >= 2:
                                    v.wait_ge(S["dQA%d" % (bt % 2)], 16 * (bt // 2))
                                v.wait_ge(S["sP"], sp_qkT[bt])
                                v.tensor_copy(
                                    q_stage[:, (bt % 2) * NPAIR * 128:
                                            (bt % 2 + 1) * NPAIR * 128],
                                    tq_ps[:, 0:1024])
                                v.tensor_copy(
                                    k_sb[:, :].rearrange("p (g t) -> p g t",
                                                         g=GROUPS)[:, :, bt * 128:(bt + 1) * 128],
                                    tq_ps[:, 1024:1536].rearrange("p (g t) -> p g t", g=GROUPS)
                                ).then_inc(S["sV"], 1)

                    @blk.scalar
                    def _(sc):
                        sc.preload_act_table(AF.Sqrt)
                        for it in range(NTT + 1):
                            if it < NTT:
                                sc.wait_ge(S["sV"], sv_f[it][1])
                                sc.activation(sd1[:, 0:1], st1[:, 1:2], AF.Sqrt,
                                              bias=epsv[:, 0:1], scale=1.0).then_inc(S["sA"], 1)
                                sc.wait_ge(S["sV"], sv_f[it][2])
                                sc.activation(sd1[:, 1:2], st1[:, 1:2], AF.Sqrt,
                                              bias=epsv[:, 2:3],
                                              scale=1.0 + EPS2).then_inc(S["sA"], 1)
                            if it >= 1:
                                bt = it - 1
                                bpar = bt % 2
                                qp = q_ps[:, bpar * DM:(bpar + 1) * DM]
                                kvp = kv_ps[:, bpar * 512:(bpar + 1) * 512]
                                sc.wait_ge(S["sP"], sp_proj[bt])
                                sc.activation(qs2[:, :], qp, AF.Square).then_inc(S["sA"], 1)
                                sc.activation(ks2[:, :], kvp[:, 0:256], AF.Square
                                              ).then_inc(S["sA"], 1)
                                sc.wait_ge(S["sV"], sv_b[bt]["B5"])
                                sc.activation(qstd[:, :], qvar[:, :], AF.Sqrt,
                                              bias=epsv[:, 1:2], scale=1.0).then_inc(S["sA"], 1)
                                sc.wait_ge(S["sV"], sv_b[bt]["B6"])
                                sc.activation(kstd[:, :], kvar[:, :], AF.Sqrt,
                                              bias=epsv[:, 1:2], scale=1.0).then_inc(S["sA"], 1)
                                sc.wait_ge(S["sV"], sv_b[bt]["B7"])
                                for s_ in range(16):
                                    ins = sc.activation(qaff[:, 64 * s_:64 * s_ + 64],
                                                        qp[:, 64 * s_:64 * s_ + 64], AF.Identity,
                                                        bias=nmr[:, s_:s_ + 1],
                                                        scale=qrr[:, s_:s_ + 1])
                                ins.then_inc(S["sA"], 1)
                                sc.wait_ge(S["sV"], sv_b[bt]["B8"])
                                for s_ in range(4):
                                    ins = sc.activation(kaff[:, 64 * s_:64 * s_ + 64],
                                                        kvp[:, 64 * s_:64 * s_ + 64], AF.Identity,
                                                        bias=nmr[:, 16 + s_:17 + s_],
                                                        scale=krr[:, s_:s_ + 1])
                                ins.then_inc(S["sA"], 1)
                                sc.wait_ge(S["sV"], sv_b[bt]["B9"])
                                sc.activation(qsqf[:, :], qhat[:, :], AF.Square
                                              ).then_inc(S["sA"], 1)
                                sc.wait_ge(S["sV"], sv_b[bt]["B10"])
                                sc.activation(ksqf[:, :], khat[:, :], AF.Square
                                              ).then_inc(S["sA"], 1)

                    @blk.tensor
                    def _(t):
                        for it in range(NTT + 1):
                            if it < NTT:
                                par = it % 2
                                t.wait_ge(S["sV"], sv_f[it][3])
                                for dc in range(8):
                                    ins = t.transpose(tq_ps[:, dc * 128:(dc + 1) * 128],
                                                      xn2_t[:, dc * 128:(dc + 1) * 128],
                                                      id_bf[:, :])
                                ins.then_inc(S["sP"], 1)
                                t.wait_ge(S["sV"], sv_f[it][4])
                                for dc in range(8):
                                    lhs = xn2f_t[:, dc * 128:(dc + 1) * 128]
                                    t.matmul(q_ps[:, par * DM:par * DM + 512], lhs,
                                             wq_sb[:, dc * 1024:dc * 1024 + 512],
                                             start=(dc == 0), stop=(dc == 7))
                                    t.matmul(q_ps[:, par * DM + 512:par * DM + 1024], lhs,
                                             wq_sb[:, dc * 1024 + 512:(dc + 1) * 1024],
                                             start=(dc == 0), stop=(dc == 7))
                                    ins = t.matmul(kv_ps[:, par * 512:(par + 1) * 512], lhs,
                                                   wkv_sb[:, dc * 512:(dc + 1) * 512],
                                                   start=(dc == 0), stop=(dc == 7))
                                ins.then_inc(S["sP"], 2)
                            if it >= 1:
                                bt = it - 1
                                t.wait_ge(S["sV"], sv_b[bt]["B10"])
                                for r in range(NPAIR):
                                    t.transpose(tq_ps[0:64, r * 128:(r + 1) * 128],
                                                qhat[:, (2 * r) * 64:(2 * r + 1) * 64],
                                                id_bf[:, :])
                                    t.transpose(tq_ps[64:128, r * 128:(r + 1) * 128],
                                                qhat[:, (2 * r + 1) * 64:(2 * r + 2) * 64],
                                                id_bf[:, :], tile_position=(0, 64))
                                for gi in range(GROUPS):
                                    t.transpose(tq_ps[0:64, 1024 + gi * 128:1024 + (gi + 1) * 128],
                                                khat[:, gi * 64:(gi + 1) * 64], id_bf[:, :])
                                    ins = t.transpose(
                                        tq_ps[64:128, 1024 + gi * 128:1024 + (gi + 1) * 128],
                                        khat[:, gi * 64:(gi + 1) * 64], id_bf[:, :],
                                        tile_position=(0, 64))
                                ins.then_inc(S["sP"], 4)

                    @blk.sync
                    def _(sy):
                        for tt in range(NTT):
                            if tt >= 2:
                                sy.wait_ge(S["sXF"], tt - 1)
                            sy.dma_start(xt[:, (tt % 2) * DM:(tt % 2 + 1) * DM],
                                         x_h[tt * 128:(tt + 1) * 128, :]
                                         ).then_inc(S["dA%d" % (tt % 2)], 16)
                            if tt >= 2:
                                sy.wait_ge(S["sV"], sv_b[tt - 2]["B12"])
                                sy.dma_start(
                                    q_scr[(tt - 2) // 4].rearrange(
                                        "p (r q) -> p r q", r=NPAIR
                                    )[:, :, ((tt - 2) % 4) * 128:((tt - 2) % 4 + 1) * 128],
                                    q_stage[:, ((tt - 2) % 2) * NPAIR * 128:
                                            ((tt - 2) % 2 + 1) * NPAIR * 128
                                            ].rearrange("p (r f) -> p r f", r=NPAIR)
                                ).then_inc(S["dQA%d" % ((tt - 2) % 2)], 16)
                        for tt in (NTT - 2, NTT - 1):
                            sy.wait_ge(S["sV"], sv_b[tt]["B12"])
                            sy.dma_start(
                                q_scr[tt // 4].rearrange("p (r q) -> p r q", r=NPAIR
                                                         )[:, :, (tt % 4) * 128:(tt % 4 + 1) * 128],
                                q_stage[:, (tt % 2) * NPAIR * 128:(tt % 2 + 1) * NPAIR * 128
                                        ].rearrange("p (r f) -> p r f", r=NPAIR)
                            ).then_inc(S["dQA%d" % (tt % 2)], 16)

                with nc.Block() as blk:
                    @blk.tensor
                    def _(t):
                        t.transpose(kv_ps[0:16, 0:128], qmaxsq[:, :], id_f[:, :])
                        t.transpose(kv_ps[0:4, 128:256], kmaxsq[:, :], id_f[:, :])
                with nc.Block() as blk:
                    @blk.vector
                    def _(v):
                        v.reduce_max(maxrow[0:16, 0:1], kv_ps[0:16, 0:128], AX.X)
                        v.reduce_max(maxrow[0:4, 1:2], kv_ps[0:4, 128:256], AX.X)
                with nc.Block() as blk:
                    @blk.scalar
                    def _(sc):
                        sc.activation(normc[0:16, 0:1], maxrow[0:16, 0:1], AF.Sqrt)
                        sc.activation(normc[0:4, 1:2], maxrow[0:4, 1:2], AF.Sqrt)
                with nc.Block() as blk:
                    @blk.tensor
                    def _(t):
                        t.transpose(kv_ps[0:1, 256:272], normc[0:16, 0:1], id_f[0:16, 0:16])
                        t.transpose(kv_ps[0:1, 272:276], normc[0:4, 1:2], id_f[0:4, 0:4])
                with nc.Block() as blk:
                    @blk.vector
                    def _(v):
                        v.tensor_copy(crow[0:1, 0:16], kv_ps[0:1, 256:272])
                        v.tensor_copy(crow[0:1, 32:36], kv_ps[0:1, 272:276])
                with nc.Block() as blk:
                    @blk.vector
                    def _(v):
                        for h in range(HEADS):
                            v.tensor_copy(crow[0:1, 16 + h:17 + h],
                                          crow[0:1, 32 + h // 4:33 + h // 4])
                with nc.Block() as blk:
                    @blk.vector
                    def _(v):
                        v.tensor_tensor(negrow[0:1, :], crow[0:1, 0:16], crow[0:1, 16:32],
                                        ALU.mult)
                        v.drain()
                        v.tensor_scalar(negrow[0:1, :], negrow[0:1, :], -1.0, None,
                                        ALU.mult, ALU.bypass)
                with nc.Block() as blk:
                    @blk.tensor
                    def _(t):
                        t.matmul(kv_ps[:, 0:16], ones1f[0:1, 0:128], negrow[0:1, :],
                                 start=True, stop=True)
                with nc.Block() as blk:
                    @blk.vector
                    def _(v):
                        v.tensor_copy(negc_sb[:, :], kv_ps[:, 0:16])

        if any(k in tp for k in ("ksb", "vsb", "negc", "stats")):
            with nc.Block() as blk:
                @blk.sync
                def _(sy):
                    for name, t_ in [("ksb", k_sb), ("vsb", v_sb),
                                     ("negc", negc_sb), ("stats", stats_sb)]:
                        if name in tp:
                            sy.dma_start(tp[name], t_[:, :]).then_inc(S["dS"], 16)
                            nS[0] += 16
                    sy.wait_ge(S["dS"], nS[0])

        with ExitStack() as pB:
          if "B" in phases:
            EB = pB.enter_context
            wout_sb = EB(nc.sbuf_tensor("wout_sb", [128, 8 * 1024], BF16))
            pbuf = EB(nc.sbuf_tensor("pbuf", [128, 2 * DEPTH * QTB], BF16))
            qbuf = EB(nc.sbuf_tensor("qbuf", [128, 2 * NPAIR * QTB], BF16))
            att_sb = EB(nc.sbuf_tensor("att_sb", [128, NPAIR * QTB], BF16))
            s_sb = EB(nc.sbuf_tensor("s_sb", [128, 2 * 8 * QTB], BF16))
            recb = EB(nc.sbuf_tensor("recb", [64, 2 * QTB], F32))
            rec_sb = EB(nc.sbuf_tensor("rec_sb", [64, 2 * QTB], F32))
            av_sb = EB(nc.sbuf_tensor("av_sb", [128, 2 * QTB], F32))
            atmp = EB(nc.sbuf_tensor("atmp", [64, QTB], BF16))
            rscr = EB(nc.sbuf_tensor("rscr", [1, QTB], F32))
            st_ps = EB(nc.psum_tensor("st_ps", [128, 2048], F32))
            av_ps = EB(nc.psum_tensor("av_ps", [128, 2048], F32))

            with nc.Block() as blk:
                @blk.gpsimd
                def _(g):
                    if "A" in phases:
                        g.wait_ge(S["dQA0"], 16 * (NTT // 2))
                        g.wait_ge(S["dQA1"], 16 * (NTT // 2))
                    g.dma_start(wout_sb[:, :].rearrange("p (c n) -> p c n", c=8),
                                wout_h.rearrange("(c p) n -> p c n", p=128)).then_inc(S["dW"], 16)
                    nW[0] += 16
                    g.memset(recb[:, :], 0.0)
                    g.wait_ge(S["dW"], nW[0])

            def pslot(h, cc):
                c0 = (h * DEPTH + cc % DEPTH) * QTB
                return pbuf[:, c0:c0 + QTB]

            with nc.Block() as blk:
                @blk.tensor
                def _(t):
                    def emit_qk_pair(u, p):
                        qtb, pr = divmod(u, NPAIR)
                        gg = pr // 2
                        qc0 = (qtb % 2) * NPAIR * QTB + pr * QTB
                        if p == 0:
                            t.wait_ge(S["dQ%d" % (qtb % 2)], 16 * (qtb // 2 + 1))
                        for cc in (2 * p, 2 * p + 1):
                            g_ = u * NCH + cc
                            if g_ >= 2:
                                t.wait_ge(S["sEXP"], g_ // 2)
                            kc = gg * SEQ + cc * 128
                            t.matmul(st_ps[:, (cc % 2) * 512:(cc % 2) * 512 + QTB],
                                     k_sb[0:64, kc:kc + 128], qbuf[0:64, qc0:qc0 + QTB],
                                     start=True, stop=True, tile_position=(0, 0)
                                     ).then_inc(S["sQK"], 1)
                        for cc in (2 * p, 2 * p + 1):
                            g_ = u * NCH + cc
                            if g_ >= 2:
                                t.wait_ge(S["sEXP1"], g_ // 2)
                            kc = gg * SEQ + cc * 128
                            t.matmul(st_ps[:, 1024 + (cc % 2) * 512:1024 + (cc % 2) * 512 + QTB],
                                     k_sb[64:128, kc:kc + 128], qbuf[64:128, qc0:qc0 + QTB],
                                     start=True, stop=True, tile_position=(64, 0)
                                     ).then_inc(S["sQK1"], 1)

                    def norm_bc(up):
                        if up >= NPAIR and up % NPAIR == 0:
                            t.wait_ge(S["sSE"], 8 * (up // NPAIR))
                        for h in range(2):
                            t.wait_ge(S["sRCP"], 2 * up + h + 1)
                            if up >= 1 and h == 0:
                                t.wait_ge(S["sMV2"], up)
                            t.matmul(av_ps[0:64, 1024 + h * 512:1024 + h * 512 + QTB],
                                     ones64f[:, :], recb[:, h * QTB:(h + 1) * QTB],
                                     start=True, stop=True,
                                     tile_position=(0, 0)).then_inc(S["sBC"], 1)

                    def norm_id(up):
                        if up >= NPAIR and up % NPAIR == 0:
                            t.wait_ge(S["sSE"], 8 * (up // NPAIR))
                        t.wait_ge(S["sNRM"], 2 * up + 2)
                        t.matmul(av_ps[64:128, 1024:1024 + QTB], id_bf[0:64, 0:64],
                                 atmp[:, :], start=True, stop=True,
                                 tile_position=(0, 64)).then_inc(S["sMV"], 1)

                    def wout_dc(qp, dc):
                        if dc == 0:
                            t.wait_ge(S["sNRM"], 16 * (qp + 1))
                            t.wait_ge(S["sMV2"], 8 * (qp + 1))
                        if dc >= 2:
                            t.wait_ge(S["sSE"], 8 * qp + dc - 1)
                        for ic in range(8):
                            ins_w = t.matmul(
                                av_ps[:, 1024 + (dc % 2) * 512:1024 + (dc % 2) * 512 + QTB],
                                wout_sb[:, ic * 1024 + dc * 128:ic * 1024 + (dc + 1) * 128],
                                att_sb[:, ic * QTB:(ic + 1) * QTB],
                                start=(ic == 0), stop=(ic == 7))
                        ins_w.then_inc(S["sWO"], 1)

                    def wout(qp):
                        for dc in range(8):
                            wout_dc(qp, dc)

                    av_cur = [0]

                    def emit_av_upto(limit):
                        while av_cur[0] < min(limit, NU * NCH):
                            a = av_cur[0]
                            uu, cc = divmod(a, NCH)
                            gg = (uu % NPAIR) // 2
                            if uu >= 1:
                                if cc == 14:
                                    norm_bc(uu - 1)
                                elif cc == 24:
                                    norm_id(uu - 1)
                                elif (cc in (25, 27, 29, 31)
                                      and (uu - 1) % NPAIR == NPAIR - 1):
                                    wout_dc((uu - 1) // NPAIR, cc - 25)
                                    wout_dc((uu - 1) // NPAIR, cc - 24)
                            t.wait_ge(S["sEXP"], a // 2 + 1)
                            t.wait_ge(S["sEXP1"], a // 2 + 1)
                            if cc == 0:
                                if uu >= 1:
                                    t.wait_ge(S["sAVC"], uu)
                            vcol = (gg * NCH + cc) * 65
                            t.matmul(av_ps[0:65, 0:512], v_sb[:, vcol:vcol + 65],
                                     pslot(0, cc), start=(cc == 0), stop=(cc == NCH - 1))
                            t.matmul(av_ps[0:65, 512:1024], v_sb[:, vcol:vcol + 65],
                                     pslot(1, cc), start=(cc == 0), stop=(cc == NCH - 1)
                                     ).then_inc(S["sAV"], 1)
                            av_cur[0] += 1

                    for P in range(NU * NCH // 2):
                        u, p = divmod(P, NCH // 2)
                        emit_av_upto(2 * P - 14)
                        emit_qk_pair(u, p)
                    emit_av_upto(NU * NCH)
                    norm_bc(NU - 1)
                    norm_id(NU - 1)
                    wout(NQTB - 1)

                @blk.scalar
                def _(sc):
                    sc.add_instruction(mybir.InstLoadActFuncSet(
                        name=nc.get_next_instruction_name(), act_func_set_id=6,
                        ins=[], outs=[]))
                    for u in range(NU):
                        qtb, pr = divmod(u, NPAIR)
                        for p in range(NCH // 2):
                            g1 = u * NCH + 2 * p + 1
                            if g1 >= DEPTH:
                                sc.wait_ge(S["sAV"], g1 - DEPTH + 1)
                            for h in range(2):
                                sc.wait_ge(S["sQK%s" % ("" if h == 0 else "1")],
                                           u * NCH + 2 * p + 2)
                                c0 = (h * DEPTH + (2 * p) % DEPTH) * QTB
                                sc.activation(pbuf[:, c0:c0 + 2 * QTB],
                                              st_ps[:, h * 1024:(h + 1) * 1024],
                                              AF.Exp, bias=negc_sb[:, 2 * pr + h:2 * pr + h + 1],
                                              scale=1.0).then_inc(
                                    S["sEXP%s" % ("" if h == 0 else "1")], 1)

                @blk.vector
                def _(v):
                    def emit_rec(ru):
                        for h in range(2):
                            if ru >= 1:
                                v.wait_ge(S["sBC"], 2 * (ru - 1) + h + 1)
                            v.reciprocal(recb[0:1, h * QTB:(h + 1) * QTB],
                                         av_sb[64:65, h * 512:h * 512 + QTB]
                                         ).then_inc(S["sRCP"], 1)
                            v.drain()

                    for u in range(NU):
                        qtb, pr = divmod(u, NPAIR)
                        v.wait_ge(S["sAV"], NCH * (u + 1))
                        v.tensor_copy(av_sb[0:65, :], av_ps[0:65, 0:1024]
                                      ).then_inc(S["sAVC"], 1)
                        v.drain()
                        emit_rec(u)
                        for h in range(2):
                            v.wait_ge(S["sBC"], 2 * u + h + 1)
                            v.tensor_copy(rec_sb[:, h * QTB:(h + 1) * QTB],
                                          av_ps[0:64, 1024 + h * 512:1024 + h * 512 + QTB])
                        v.wait_ge(S["sWO"], 8 * qtb)
                        v.drain()
                        v.tensor_tensor(att_sb[0:64, pr * QTB:(pr + 1) * QTB],
                                        av_sb[0:64, 0:512], rec_sb[:, 0:QTB], ALU.mult
                                        ).then_inc(S["sNRM"], 1)
                        if u >= 1:
                            v.wait_ge(S["sMV"], u)
                        v.tensor_tensor(atmp[:, :], av_sb[0:64, 512:1024],
                                        rec_sb[:, QTB:2 * QTB], ALU.mult
                                        ).then_inc(S["sNRM"], 1)
                        v.wait_ge(S["sMV"], u + 1)
                        v.tensor_copy(att_sb[64:128, pr * QTB:(pr + 1) * QTB],
                                      av_ps[64:128, 1024:1024 + QTB]).then_inc(S["sMV2"], 1)
                        if pr == NPAIR - 1:
                            qp = qtb
                            par = qp % 2
                            if qp >= 2:
                                v.wait_ge(S["dSp%d" % par], 16 * (qp // 2))
                            for dc in range(8):
                                v.wait_ge(S["sWO"], 8 * qp + dc + 1)
                                v.tensor_copy(s_sb[:, (par * 8 + dc) * QTB:(par * 8 + dc + 1) * QTB],
                                              av_ps[:, 1024 + (dc % 2) * 512:
                                                    1024 + (dc % 2) * 512 + QTB]
                                              ).then_inc(S["sSE"], 1)

                @blk.sync
                def _(sy):
                    sy.wait_ge(S["dQA0"], 16 * (NTT // 2))
                    sy.wait_ge(S["dQA1"], 16 * (NTT // 2))
                    for qtb in range(NQTB):
                        if qtb >= 2:
                            sy.wait_ge(S["sQK"], NCH * NPAIR * (qtb - 1))
                            sy.wait_ge(S["sQK1"], NCH * NPAIR * (qtb - 1))
                        sy.dma_start(qbuf[:, (qtb % 2) * NPAIR * QTB:(qtb % 2 + 1) * NPAIR * QTB],
                                     q_scr[qtb]).then_inc(S["dQ%d" % (qtb % 2)], 16)
                        if qtb >= 1:
                            par = (qtb - 1) % 2
                            sy.wait_ge(S["sSE"], 8 * qtb)
                            sy.dma_start(
                                s_scr[:, :, (qtb - 1) * QTB:qtb * QTB].rearrange(
                                    "dc p q -> p dc q"),
                                s_sb[:, par * 8 * QTB:(par + 1) * 8 * QTB].rearrange(
                                    "p (dc q) -> p dc q", dc=8)).then_inc(S["dSp%d" % par], 16)
                    par = (NQTB - 1) % 2
                    sy.wait_ge(S["sSE"], 8 * NQTB)
                    sy.dma_start(
                        s_scr[:, :, (NQTB - 1) * QTB:NQTB * QTB].rearrange("dc p q -> p dc q"),
                        s_sb[:, par * 8 * QTB:(par + 1) * 8 * QTB].rearrange(
                            "p (dc q) -> p dc q", dc=8)).then_inc(S["dSp%d" % par], 16)

        ab.close()

        if "B" in phases and "C" in phases:
            with nc.Block() as blk:
                @blk.sync
                def _(sy):
                    sy.wait_ge(S["dSp0"], 16 * (NQTB // 2))
                    sy.wait_ge(S["dSp1"], 16 * (NQTB // 2))

        with ExitStack() as pC:
          if "C" in phases:
            EC = pC.enter_context
            wdT_sb = EC(nc.sbuf_tensor("wdT_sb", [128, NHC * 8 * 128], BF16))
            hbuf = EC(nc.sbuf_tensor("hbuf", [128, NHC * TC], BF16))
            xnf_c = EC(nc.sbuf_tensor("xnf_c", [128, 8 * TC], BF16))
            g_c = EC(nc.sbuf_tensor("g_c", [128, 8 * TC], BF16))
            s_c = EC(nc.sbuf_tensor("s_c", [128, 2 * 8 * TC], BF16))
            xc = EC(nc.sbuf_tensor("xc", [128, 3 * 4 * DM], F32))
            wsl = EC(nc.sbuf_tensor("wsl", [128, 2 * 2 * 2 * 8 * 128], BF16))
            sg_c = EC(nc.sbuf_tensor("sg_c", [128, 2 * TC], BF16))
            xn_c = EC(nc.sbuf_tensor("xn_c", [128, DM], BF16))
            pt_c = EC(nc.sbuf_tensor("pt_c", [128, 8 * TC], F32))
            gu_ps = EC(nc.psum_tensor("gu_ps", [128, 2 * TC], F32))
            up_ps = EC(nc.psum_tensor("up_ps", [128, 2 * TC], F32))
            dn_ps = EC(nc.psum_tensor("dn_ps", [128, 2 * TC], F32))
            xT_ps = EC(nc.psum_tensor("xT_ps", [128, DM], BF16))
            pT_ps = EC(nc.psum_tensor("pT_ps", [128, TC], F32))

            SLABW = 2 * 8 * 128

            def wslab(buf, mat, hcs, dc):
                c0 = buf * 2 * SLABW + mat * SLABW + (hcs * 8 + dc) * 128
                return wsl[:, c0:c0 + 128]

            if cmode == "gate":
                _skip_c = True
            else:
                _skip_c = False
            if not _skip_c and cmode != "full":
                with nc.Block() as blk:
                    @blk.gpsimd
                    def _(g):
                        g.memset(g_c[:, :], 1.0)

            if _skip_c:
                with nc.Block() as blk:
                    @blk.sync
                    def _(sy):
                        pass
            if not _skip_c:
              with nc.Block() as blk:
                @blk.scalar
                def _(sc):
                    sc.preload_act_table(AF.Silu)

                @blk.sync
                def _(sy):
                    if cmode == "full":
                        sy.dma_start(wdT_sb[:, :].rearrange("p (hc dc f) -> p hc dc f",
                                                            hc=NHC, dc=8),
                                     wdT_scr).then_inc(S["dStr"], 16)
                    def _emit_out(oc):
                        sy.wait_ge(S["sAD"], 8 * (oc + 1))
                        sy.dma_start(out_h[oc * TC:(oc + 1) * TC, :].rearrange(
                                         "(tt p) f -> p tt f", p=128),
                                     xc[:, (oc % 3) * 4 * DM:(oc % 3 + 1) * 4 * DM
                                        ].rearrange("p (tt f) -> p tt f", tt=4)
                                     ).then_inc(S["dOUT%d" % (oc % 2)], 16)

                    for tc in range(NTC):
                        if cmode != "cout":
                            if tc >= 3 and cmode not in ("cx",):
                                sy.wait_ge(S["dOUT%d" % ((tc - 3) % 2)],
                                           16 * ((tc - 3) // 2 + 1))
                            sy.dma_start(xc[:, (tc % 3) * 4 * DM:(tc % 3 + 1) * 4 * DM
                                            ].rearrange("p (tt f) -> p tt f", tt=4),
                                         x_h[tc * TC:(tc + 1) * TC, :].rearrange(
                                             "(tt p) f -> p tt f", p=128)
                                         ).then_inc(S["dXC%d" % (tc % 3)], 16)
                        if tc >= 2 and cmode not in ("cx",):
                            _emit_out(tc - 2)
                    if cmode not in ("cx",):
                        _emit_out(NTC - 2)
                        _emit_out(NTC - 1)
                    if cmode not in ("cx",):
                        sy.wait_ge(S["dOUT0"], 16 * (NTC // 2))
                        sy.wait_ge(S["dOUT1"], 16 * (NTC // 2))
                    else:
                        sy.wait_ge(S["dXC0"], 16 * (NTC // 2))
                        sy.wait_ge(S["dXC1"], 16 * (NTC // 2))

                @blk.gpsimd
                def _(g):
                    for tc in range(NTC if cmode != "cxout" else 0):
                        if "B" in phases:
                            g.wait_ge(S["dSp%d" % (tc % 2)], 16 * (tc // 2 + 1))
                        if tc >= 2:
                            g.wait_ge(S["sPm"], 8 * (tc - 1))
                        g.dma_start(s_c[:, (tc % 2) * 8 * TC:(tc % 2 + 1) * 8 * TC
                                        ].rearrange("p (dc q) -> p dc q", dc=8),
                                    s_scr[:, :, tc * TC:(tc + 1) * TC].rearrange(
                                        "dc p q -> p dc q")).then_inc(S["dSr%d" % (tc % 2)], 16)
                        for sl_ in range(NSLAB if cmode == "full" else 0):
                            gs = tc * NSLAB + sl_
                            if gs >= 2:
                                g.wait_ge(S["sGU"], 2 * (gs - 2) + 2)
                            buf = gs % 2
                            g.dma_start(
                                wsl[:, buf * 2 * SLABW:buf * 2 * SLABW + SLABW
                                    ].rearrange("p (hc dc f) -> p hc dc f", hc=2, dc=8),
                                wgT_scr[:, sl_ * 2:(sl_ + 1) * 2]
                                ).then_inc(S["dSl%d" % (gs % 2)], 16)
                            g.dma_start(
                                wsl[:, buf * 2 * SLABW + SLABW:buf * 2 * SLABW + 2 * SLABW
                                    ].rearrange("p (hc dc f) -> p hc dc f", hc=2, dc=8),
                                wuT_scr[:, sl_ * 2:(sl_ + 1) * 2]
                                ).then_inc(S["dSl%d" % (gs % 2)], 16)

                def _dve_combine(v, ct):
                    for dc in range(8):
                        D = 8 * ct + dc
                        if cmode in ("motion", "cxout", "cx", "cout"):
                            if cmode == "motion":
                                v.wait_ge(S["dSr%d" % (ct % 2)], 16 * (ct // 2 + 1))
                            v.engine_nop().then_inc(S["sPm"], 1)
                            v.engine_nop().then_inc(S["sAD"], 1)
                            continue
                        if cmode == "full":
                            v.wait_ge(S["sG"], D + 1)
                        v.wait_ge(S["dSr%d" % (ct % 2)], 16 * (ct // 2 + 1))
                        if D >= 2:
                            v.wait_ge(S["sPT2"], D - 1)
                        v.tensor_tensor(pt_c[:, dc * TC:(dc + 1) * TC],
                                        s_c[:, ((ct % 2) * 8 + dc) * TC:
                                            ((ct % 2) * 8 + dc + 1) * TC],
                                        g_c[:, dc * TC:(dc + 1) * TC],
                                        ALU.mult).then_inc(S["sPm"], 1)
                        v.wait_ge(S["sPT2"], D + 1)
                        v.tensor_tensor(
                            xc[:, (ct % 3) * 4 * DM:(ct % 3 + 1) * 4 * DM
                               ].rearrange("p (tt f) -> p tt f", tt=4
                                           )[:, :, dc * 128:(dc + 1) * 128],
                            xc[:, (ct % 3) * 4 * DM:(ct % 3 + 1) * 4 * DM
                               ].rearrange("p (tt f) -> p tt f", tt=4
                                           )[:, :, dc * 128:(dc + 1) * 128],
                            pT_ps[:, :].rearrange("p (tt f) -> p tt f", tt=4),
                            ALU.add).then_inc(S["sAD"], 1)

                def _dve_pt(v, ct):
                    for dc in range(8):
                        v.wait_ge(S["sG"], 8 * ct + dc + 1)
                        v.wait_ge(S["dSr%d" % (ct % 2)], 16 * (ct // 2 + 1))
                        if ct >= 1:
                            v.wait_ge(S["sPT2"], 8 * (ct - 1) + dc + 1)
                        v.tensor_tensor(pt_c[:, dc * TC:(dc + 1) * TC],
                                        s_c[:, ((ct % 2) * 8 + dc) * TC:
                                            ((ct % 2) * 8 + dc + 1) * TC],
                                        g_c[:, dc * TC:(dc + 1) * TC],
                                        ALU.mult).then_inc(S["sPm"], 1)

                def _add1(v, ct, dc):
                    v.wait_ge(S["sPT2"], 8 * ct + dc + 1)
                    v.tensor_tensor(
                        xc[:, (ct % 3) * 4 * DM:(ct % 3 + 1) * 4 * DM
                           ].rearrange("p (tt f) -> p tt f", tt=4
                                       )[:, :, dc * 128:(dc + 1) * 128],
                        xc[:, (ct % 3) * 4 * DM:(ct % 3 + 1) * 4 * DM
                           ].rearrange("p (tt f) -> p tt f", tt=4
                                       )[:, :, dc * 128:(dc + 1) * 128],
                        pT_ps[:, :].rearrange("p (tt f) -> p tt f", tt=4),
                        ALU.add).then_inc(S["sAD"], 1)

                def _dve_add(v, ct):
                    for dc in range(8):
                        _add1(v, ct, dc)

                def _xnf_ts(v, tcx, t4):
                    tt = tcx * 4 + t4
                    v.wait_ge(S["dXC%d" % (tcx % 3)], 16 * (tcx // 3 + 1))
                    if tcx >= 1 and t4 == 0:
                        v.wait_ge(S["sGU"], NHC * tcx)
                    v.tensor_scalar(xn_c[:, :],
                                    xc[:, ((tcx % 3) * 4 + t4) * DM:
                                       ((tcx % 3) * 4 + t4 + 1) * DM],
                                    stats_sb[:, tt:tt + 1],
                                    stats_sb[:, NTT + tt:NTT + tt + 1],
                                    ALU.subtract, ALU.mult).then_inc(S["sXN"], 1)

                def _xnf_copy(v, tcx, t4):
                    tt = tcx * 4 + t4
                    v.wait_ge(S["sP"], C_SP_BASE + tt + 1)
                    v.tensor_copy(xnf_c[:, :].rearrange("p (dc f) -> p dc f", dc=8
                                                        )[:, :, t4 * 128:(t4 + 1) * 128],
                                  xT_ps[:, :].rearrange("p (dc f) -> p dc f", dc=8)
                                  ).then_inc(S["sXNF"], 1)

                def _xnf_prod(v, tcx):
                    for t4 in range(4):
                        _xnf_ts(v, tcx, t4)
                        _xnf_copy(v, tcx, t4)

                @blk.vector
                def _(v):
                    for tc in range(NTC):
                        if cmode == "full":
                            if tc == 0:
                                _xnf_prod(v, 0)
                            if tc >= 1:
                                _dve_pt(v, tc - 1)
                        for hc in range(NHC if cmode == "full" else 0):
                            H = NHC * tc + hc
                            v.wait_ge(S["sSI"], H + 1)
                            if tc >= 1 and hc == 0:
                                v.wait_ge(S["sDN"], 8 * tc)
                            v.tensor_tensor(hbuf[:, hc * TC:(hc + 1) * TC],
                                            sg_c[:, (hc % 2) * TC:(hc % 2 + 1) * TC],
                                            up_ps[:, (hc % 2) * TC:(hc % 2 + 1) * TC],
                                            ALU.mult).then_inc(S["sH"], 1)
                        if cmode == "full":
                            if tc + 1 < NTC and tc >= 1:
                                _xnf_ts(v, tc + 1, 0)
                                for dc in range(4):
                                    _add1(v, tc - 1, dc)
                                _xnf_copy(v, tc + 1, 0)
                                for t4 in range(1, 4):
                                    _xnf_ts(v, tc + 1, t4)
                                    _add1(v, tc - 1, 3 + t4)
                                    _xnf_copy(v, tc + 1, t4)
                                _add1(v, tc - 1, 7)
                            elif tc + 1 < NTC:
                                _xnf_prod(v, tc + 1)
                            elif tc >= 1:
                                _dve_add(v, tc - 1)
                        elif tc >= 1:
                            _dve_combine(v, tc - 1)

                    if cmode == "full":
                        _dve_pt(v, NTC - 1)
                        _dve_add(v, NTC - 1)
                    else:
                        _dve_combine(v, NTC - 1)

                @blk.scalar
                def _(sc):
                    for tc in range(NTC if cmode == "full" else 0):
                        for hc in range(NHC):
                            H = NHC * tc + hc
                            sc.wait_ge(S["sGU"], H + 1)
                            if H >= 1:
                                sc.wait_ge(S["sH"], H - 1)
                            sc.activation(sg_c[:, (hc % 2) * TC:(hc % 2 + 1) * TC],
                                          gu_ps[:, (hc % 2) * TC:(hc % 2 + 1) * TC],
                                          AF.Silu, scale=1.0 / 127.0).then_inc(S["sSI"], 1)
                        for dc in range(8):
                            D = 8 * tc + dc
                            sc.wait_ge(S["sDN"], D + 1)
                            if tc >= 1:
                                sc.wait_ge(S["sPm"], 8 * (tc - 1) + dc + 1)
                            sc.activation(g_c[:, dc * TC:(dc + 1) * TC],
                                          dn_ps[:, (dc % 2) * TC:(dc % 2 + 1) * TC],
                                          AF.Silu, scale=DQ2).then_inc(S["sG"], 1)

                def _pe_pt(t, ct, dc):
                    if cmode == "motion":
                        return
                    D = 8 * ct + dc
                    t.wait_ge(S["sPm"], D + 1)
                    if D >= 1:
                        t.wait_ge(S["sAD"], D)
                    for t4 in range(4):
                        ins = t.transpose(pT_ps[:, t4 * 128:(t4 + 1) * 128],
                                          pt_c[:, dc * TC + t4 * 128:
                                               dc * TC + (t4 + 1) * 128],
                                          id_f[:, :])
                    ins.then_inc(S["sPT2"], 1)

                def _xt_group(t, tcx, t4):
                    tt = tcx * 4 + t4
                    t.wait_ge(S["sXN"], tt + 1)
                    if tt >= 1:
                        t.wait_ge(S["sXNF"], tt)
                    for dc in range(8):
                        ins = t.transpose(xT_ps[:, dc * 128:(dc + 1) * 128],
                                          xn_c[:, dc * 128:(dc + 1) * 128], id_bf[:, :])
                    ins.then_inc(S["sP"], 1)

                @blk.tensor
                def _(t):
                    for tc in range(NTC):
                        if cmode == "full" and tc == 0:
                            for t4 in range(4):
                                _xt_group(t, 0, t4)
                        if cmode == "full":
                            t.wait_ge(S["sXNF"], 4 * (tc + 1))
                        for hc in range(NHC if cmode == "full" else 0):
                            H = NHC * tc + hc
                            gs = tc * NSLAB + hc // 2
                            t.wait_ge(S["dSl%d" % (gs % 2)], 32 * (gs // 2 + 1))
                            if H >= 2:
                                t.wait_ge(S["sSI"], H - 1)
                                t.wait_ge(S["sH"], H - 1)
                            buf = gs % 2
                            for dc in range(8):
                                t.matmul(gu_ps[:, (hc % 2) * TC:(hc % 2 + 1) * TC],
                                         wslab(buf, 0, hc % 2, dc),
                                         xnf_c[:, dc * TC:(dc + 1) * TC],
                                         start=(dc == 0), stop=(dc == 7))
                            for dc in range(8):
                                ins = t.matmul(up_ps[:, (hc % 2) * TC:(hc % 2 + 1) * TC],
                                               wslab(buf, 1, hc % 2, dc),
                                               xnf_c[:, dc * TC:(dc + 1) * TC],
                                               start=(dc == 0), stop=(dc == 7))
                            ins.then_inc(S["sGU"], 1)
                        if cmode == "full":
                            t.wait_ge(S["sH"], NHC * (tc + 1))
                        for dc in range(8 if cmode == "full" else 0):
                            D = 8 * tc + dc
                            if D == 0:
                                t.wait_ge(S["dStr"], 16)
                            if tc >= 1:
                                _pe_pt(t, tc - 1, dc)
                            if tc + 1 < NTC and dc >= 4:
                                _xt_group(t, tc + 1, dc - 4)
                            if D >= 2:
                                t.wait_ge(S["sG"], D - 1)
                            for hc in range(NHC):
                                ins = t.matmul(dn_ps[:, (dc % 2) * TC:(dc % 2 + 1) * TC],
                                               wdT_sb[:, (hc * 8 + dc) * 128:
                                                      (hc * 8 + dc + 1) * 128],
                                               hbuf[:, hc * TC:(hc + 1) * TC],
                                               start=(hc == 0), stop=(hc == NHC - 1))
                            ins.then_inc(S["sDN"], 1)
                        if cmode != "full" and tc >= 1:
                            for dc in range(8):
                                _pe_pt(t, tc - 1, dc)
                    if cmode != "motion":
                        for dc in range(8):
                            _pe_pt(t, NTC - 1, dc)

        if "C" in phases and cmode != "gate":
            if any(k in tp for k in ("gg", "hb", "xnf")):
                with nc.Block() as blk:
                    @blk.sync
                    def _(sy):
                        for name, t_ in [("gg", g_c), ("hb", hbuf), ("xnf", xnf_c)]:
                            if name in tp:
                                sy.dma_start(tp[name], t_[:, :]).then_inc(S["dS"], 16)
                                nS[0] += 16
                        sy.wait_ge(S["dS"], nS[0])
            if cmode not in ("cx",):
                with nc.Block() as blk:
                    @blk.sync
                    def _(sy):
                        sy.wait_ge(S["dOUT0"], 16 * (NTC // 2))
                        sy.wait_ge(S["dOUT1"], 16 * (NTC // 2))

    return nc


_CACHE = {}


def kernel(L0_attn_x, L0_attn_ln_scale, L0_attn_ln_bias, L0_attn_Wq, L0_attn_Wkv,
           L0_attn_qn_scale, L0_attn_qn_bias, L0_attn_kn_scale, L0_attn_kn_bias,
           L0_attn_Wout, L0_ffn_w_gate, L0_ffn_w_up, L0_ffn_w_down):
    x = np.ascontiguousarray(np.asarray(L0_attn_x, dtype=np.float32))
    shared = {
        "wq": np.ascontiguousarray(np.asarray(L0_attn_Wq, np.float32)),
        "wkv": np.ascontiguousarray(np.asarray(L0_attn_Wkv, np.float32)),
        "wout": np.ascontiguousarray(np.asarray(L0_attn_Wout, np.float32)),
        "wg": np.ascontiguousarray(np.asarray(L0_ffn_w_gate, np.int32)),
        "wu": np.ascontiguousarray(np.asarray(L0_ffn_w_up, np.int32)),
        "wd": np.ascontiguousarray(np.asarray(L0_ffn_w_down, np.int32)),
        "ln_sc": np.ascontiguousarray(np.asarray(L0_attn_ln_scale, np.float32)),
        "ln_b": np.ascontiguousarray(np.asarray(L0_attn_ln_bias, np.float32)),
        "qn_sc": np.ascontiguousarray(np.asarray(L0_attn_qn_scale, np.float32)),
        "qn_b": np.ascontiguousarray(np.asarray(L0_attn_qn_bias, np.float32)),
        "kn_sc": np.ascontiguousarray(np.asarray(L0_attn_kn_scale, np.float32)),
        "kn_b": np.ascontiguousarray(np.asarray(L0_attn_kn_bias, np.float32)),
    }
    if "nc" not in _CACHE:
        _CACHE["nc"] = build()
    nc = _CACHE["nc"]
    in_maps = [dict(shared, x=np.ascontiguousarray(x[b])) for b in range(8)]
    res = run_bass_kernel_spmd(nc, in_maps, core_ids=list(range(8)))
    return np.stack([res.results[b]["out"] for b in range(8)], axis=0)


# version 29
# speedup vs baseline: 1.0028x; 1.0028x over previous
import numpy as np
from contextlib import ExitStack

import concourse.bass as bass
import concourse.mybir as mybir
from concourse.bass_utils import run_bass_kernel_spmd

F32 = mybir.dt.float32
BF16 = mybir.dt.bfloat16
I32 = mybir.dt.int32
AF = mybir.ActivationFunctionType
ALU = mybir.AluOpType
AX = mybir.AxisListType

SEQ, DM, HEADS, GROUPS, DH, HID = 4096, 1024, 16, 4, 64, 3072
NTT = SEQ // 128
QTB = 512
NQTB = SEQ // QTB
DEPTH = 16
NPAIR = HEADS // 2
NCH = SEQ // 128
NU = NQTB * NPAIR
TC = 512
NTC = SEQ // TC
NHC = HID // 128
NSLAB = NHC // 2
EPS1, EPS2 = 1e-5, 1e-6
QS = 0.125
DQ2 = 1.0 / (127.0 * 127.0)

AV_, AA_, AP_ = 10, 6, 7
C_SP_BASE = AP_ * NTT


class _SkipPhase(Exception):
    pass


def build(taps=(), phases="0ABC", cmode="full"):
    nc = bass.Bass("TRN2", target_bir_lowering=False)

    x_h = nc.dram_tensor("x", [SEQ, DM], F32, kind="ExternalInput").ap()
    wq_h = nc.dram_tensor("wq", [DM, HEADS * DH], F32, kind="ExternalInput").ap()
    wkv_h = nc.dram_tensor("wkv", [DM, 2 * GROUPS * DH], F32, kind="ExternalInput").ap()
    wout_h = nc.dram_tensor("wout", [HEADS * DH, DM], F32, kind="ExternalInput").ap()
    wg_h = nc.dram_tensor("wg", [HID, DM], I32, kind="ExternalInput").ap()
    wu_h = nc.dram_tensor("wu", [HID, DM], I32, kind="ExternalInput").ap()
    wd_h = nc.dram_tensor("wd", [DM, HID], I32, kind="ExternalInput").ap()
    lnsc_h = nc.dram_tensor("ln_sc", [DM], F32, kind="ExternalInput").ap()
    lnb_h = nc.dram_tensor("ln_b", [DM], F32, kind="ExternalInput").ap()
    qnsc_h = nc.dram_tensor("qn_sc", [DH], F32, kind="ExternalInput").ap()
    qnb_h = nc.dram_tensor("qn_b", [DH], F32, kind="ExternalInput").ap()
    knsc_h = nc.dram_tensor("kn_sc", [DH], F32, kind="ExternalInput").ap()
    knb_h = nc.dram_tensor("kn_b", [DH], F32, kind="ExternalInput").ap()
    out_h = nc.dram_tensor("out", [SEQ, DM], F32, kind="ExternalOutput").ap()

    _wk = "ExternalOutput" if "wscr" in taps else "Internal"
    wgT_scr = nc.dram_tensor("wgT_scr", [128, NHC, 8, 128], BF16, kind=_wk).ap()
    wuT_scr = nc.dram_tensor("wuT_scr", [128, NHC, 8, 128], BF16, kind=_wk).ap()
    wdT_scr = nc.dram_tensor("wdT_scr", [128, NHC, 8, 128], BF16, kind=_wk).ap()
    s_scr = nc.dram_tensor("s_scr", [8, 128, SEQ], BF16,
                           kind="ExternalOutput" if "ssc" in taps else "Internal").ap()
    q_scr = nc.dram_tensor("q_scr", [NQTB, 128, NPAIR * QTB], BF16,
                           kind="ExternalOutput" if "qscr" in taps else "Internal").ap()

    tp = {}
    for name, shape, dt in [
        ("ksb", [128, GROUPS * SEQ], BF16),
        ("vsb", [128, GROUPS * NCH * 65], BF16), ("negc", [128, HEADS], F32),
        ("stats", [128, 2 * NTT], F32), ("gg", [128, 8 * TC], BF16),
        ("hb", [128, NHC * TC], BF16), ("xnf", [128, 8 * TC], BF16),
    ]:
        if name in taps:
            tp[name] = nc.dram_tensor("tap_" + name, shape, dt, kind="ExternalOutput").ap()

    with ExitStack() as top:
        E = top.enter_context
        stats_sb = E(nc.sbuf_tensor("stats_sb", [128, 2 * NTT], F32))
        negc_sb = E(nc.sbuf_tensor("negc_sb", [128, HEADS], F32))
        id_bf = E(nc.sbuf_tensor("id_bf", [128, 128], BF16))
        id_f = E(nc.sbuf_tensor("id_f", [128, 128], F32))
        ones1f = E(nc.sbuf_tensor("ones1f", [1, 128], F32))
        ones64f = E(nc.sbuf_tensor("ones64f", [64, 64], F32))
        ones64b = E(nc.sbuf_tensor("ones64b", [64, 64], BF16))
        epsv = E(nc.sbuf_tensor("epsv", [128, 3], F32))

        S = {}
        for s_ in ["dA0", "dA1", "dW", "dS", "dSp0", "dSp1", "dSr0", "dSr1",
                   "dStr", "dSl0", "dSl1", "dXC0", "dXC1", "dOUT0", "dOUT1",
                   "dQ0", "dQ1", "dQA0", "dQA1", "sMV2", "sEXP1", "sQK1",
                   "dS0", "dS1", "s0P", "s0V", "d0L0", "d0L1", "sAVC", "sXF",
                   "dXC2",
                   "sP", "sV", "sA", "sQK", "sEXP", "sAV", "sRCP", "sNRM",
                   "sAN", "sBC", "sMV", "sSE", "sWO", "sPT2",
                   "sGU", "sSI", "sH", "sDN", "sG", "sPm", "sAD",
                   "sXN", "sXNF"]:
            S[s_] = nc.alloc_semaphore(name=s_)
        nW = [0]
        nS = [0]

        ab = top.enter_context(ExitStack())
        EAB = ab.enter_context
        k_sb = EAB(nc.sbuf_tensor("k_sb", [128, GROUPS * SEQ], BF16))
        v_sb = EAB(nc.sbuf_tensor("v_sb", [128, GROUPS * NCH * 65], BF16))

        with ExitStack() as p0:
            E0 = p0.enter_context
            wq_sb = E0(nc.sbuf_tensor("wq_sb", [128, 8 * 1024], BF16))
            wkv_sb = E0(nc.sbuf_tensor("wkv_sb", [128, 8 * 512], BF16))
            vec_sb = E0(nc.sbuf_tensor("vec_sb", [1, 2 * DM + 4 * DH], F32))
            lnsc_f = E0(nc.sbuf_tensor("lnsc_f", [128, DM], F32))
            lnb_f = E0(nc.sbuf_tensor("lnb_f", [128, DM], F32))
            qnsc_f = E0(nc.sbuf_tensor("qnsc_f", [128, DM], F32))
            qnb_f = E0(nc.sbuf_tensor("qnb_f", [128, DM], F32))
            knsc_f = E0(nc.sbuf_tensor("knsc_f", [128, GROUPS * DH], F32))
            knb_f = E0(nc.sbuf_tensor("knb_f", [128, GROUPS * DH], F32))

            with nc.Block() as blk:
                @blk.gpsimd
                def _(g):
                    g.dma_start(wq_sb[:, :].rearrange("p (c n) -> p c n", c=8),
                                wq_h.rearrange("(c p) n -> p c n", p=128)).then_inc(S["dW"], 16)
                    g.dma_start(wkv_sb[:, :].rearrange("p (c n) -> p c n", c=8),
                                wkv_h.rearrange("(c p) n -> p c n", p=128)).then_inc(S["dW"], 16)
                    for i, h in enumerate([lnsc_h, lnb_h]):
                        g.dma_start(vec_sb[0:1, i * DM:(i + 1) * DM], h[None, :]
                                    ).then_inc(S["dW"], 16)
                    for i, h in enumerate([qnsc_h, qnb_h, knsc_h, knb_h]):
                        g.dma_start(vec_sb[0:1, 2 * DM + i * DH:2 * DM + (i + 1) * DH],
                                    h[None, :]).then_inc(S["dW"], 16)
                    nW[0] += 8 * 16
                    for t_ in (id_bf, id_f):
                        g.memset(t_[:, :], 0.0)
                    g.memset(ones1f[:, :], 1.0)
                    g.memset(ones64f[:, :], 1.0)
                    g.memset(ones64b[:, :], 1.0)
                    g.memset(epsv[:, 0:1], EPS1)
                    g.memset(epsv[:, 1:2], EPS2)
                    g.memset(epsv[:, 2:3], EPS1 * EPS2)
                    g.memset(v_sb[:, :].rearrange("p (c s) -> p c s", s=65)[:, :, 64:65], 1.0)
                    g.wait_ge(S["dW"], nW[0])

            with nc.Block() as blk:
                @blk.gpsimd
                def _(g):
                    for t_ in (id_bf, id_f):
                        g.affine_select(out=t_[:, :], in_=t_[:, :],
                                        compare_op=ALU.not_equal, fill=1.0,
                                        base=0, pattern=[[-1, 128]], channel_multiplier=1)

            with ExitStack() as bc:
                bps = bc.enter_context(nc.psum_tensor("bps", [128, DM], F32))
                for (dst_fill, src_lo, src_n) in [
                        ("ln", 0, DM), ("ln2", DM, DM), ("hd", 2 * DM, 4 * DH)]:
                    with nc.Block() as blk:
                        @blk.tensor
                        def _(t, lo=src_lo, n=src_n):
                            for o in range(0, n, 512):
                                w = min(512, n - o)
                                t.matmul(bps[:, o:o + w], ones1f[0:1, 0:128],
                                         vec_sb[0:1, lo + o:lo + o + w],
                                         start=True, stop=True)
                    with nc.Block() as blk:
                        @blk.vector
                        def _(v, tag=dst_fill, n=src_n):
                            if tag == "ln":
                                v.tensor_copy(lnsc_f[:, :], bps[:, 0:DM])
                            elif tag == "ln2":
                                v.tensor_copy(lnb_f[:, :], bps[:, 0:DM])
                            else:
                                for r in range(HEADS):
                                    v.tensor_scalar(qnsc_f[:, r * DH:(r + 1) * DH],
                                                    bps[:, 0:DH], QS, None,
                                                    ALU.mult, ALU.bypass)
                                    v.tensor_scalar(qnb_f[:, r * DH:(r + 1) * DH],
                                                    bps[:, DH:2 * DH], QS, None,
                                                    ALU.mult, ALU.bypass)
                                for r in range(GROUPS):
                                    v.tensor_copy(knsc_f[:, r * DH:(r + 1) * DH],
                                                  bps[:, 2 * DH:3 * DH])
                                    v.tensor_copy(knb_f[:, r * DH:(r + 1) * DH],
                                                  bps[:, 3 * DH:4 * DH])

            with ExitStack() as p0b:
                E0b = p0b.enter_context
                wint = E0b(nc.sbuf_tensor("wint", [128, 2 * HID], BF16))
                wstg = E0b(nc.sbuf_tensor("wstg", [128, 2 * HID], BF16))
                tps = E0b(nc.psum_tensor("tps", [128, 1024], BF16))
                jobs = []
                for srch, dsth in [(wg_h, wgT_scr), (wu_h, wuT_scr)]:
                    for hc in range(NHC):
                        jobs.append(("gu", srch, dsth, hc))
                for dc in range(8):
                    jobs.append(("wd", wd_h, wdT_scr, dc))
                rounds = [1 if j[0] == "gu" else 3 for j in jobs]
                rtot = [0]
                for r_ in rounds:
                    rtot.append(rtot[-1] + r_)
                NJ = len(jobs)

                with nc.Block() as blk:
                    @blk.gpsimd
                    def _(g):
                        nl = [0, 0]
                        for j, (kind, srch, dsth, idx) in enumerate(jobs):
                            pj = j % 2
                            if j >= 2:
                                g.wait_ge(S["s0P"], rtot[j - 1])
                            ncols = 1024 if kind == "gu" else HID
                            g.dma_start(wint[:, pj * HID:pj * HID + ncols],
                                        srch[idx * 128:(idx + 1) * 128, :]
                                        ).then_inc(S["d0L%d" % pj], 16)

                    @blk.tensor
                    def _(t):
                        nload = [0, 0]
                        for j, (kind, srch, dsth, idx) in enumerate(jobs):
                            pj = j % 2
                            nload[pj] += 16
                            t.wait_ge(S["d0L%d" % pj], nload[pj])
                            for r_ in range(rounds[j]):
                                if rtot[j] + r_ >= 1:
                                    t.wait_ge(S["s0V"], rtot[j] + r_)
                                for i in range(8):
                                    ins = t.transpose(
                                        tps[:, i * 128:(i + 1) * 128],
                                        wint[:, pj * HID + (r_ * 8 + i) * 128:
                                             pj * HID + (r_ * 8 + i + 1) * 128],
                                        id_bf[:, :])
                                ins.then_inc(S["s0P"], 1)

                    @blk.vector
                    def _(v):
                        nst = [0, 0]
                        for j, (kind, srch, dsth, idx) in enumerate(jobs):
                            pj = j % 2
                            if j >= 2:
                                nst[pj] += 16
                                v.wait_ge(S["dS%d" % pj], nst[pj])
                            for r_ in range(rounds[j]):
                                v.wait_ge(S["s0P"], rtot[j] + r_ + 1)
                                v.tensor_copy(wstg[:, pj * HID + r_ * 1024:
                                                   pj * HID + (r_ + 1) * 1024],
                                              tps[:, :]).then_inc(S["s0V"], 1)

                    @blk.sync
                    def _(sy):
                        for j, (kind, srch, dsth, idx) in enumerate(jobs):
                            pj = j % 2
                            sy.wait_ge(S["s0V"], rtot[j + 1])
                            if kind == "gu":
                                sy.dma_start(dsth[:, idx],
                                             wstg[:, pj * HID:pj * HID + 1024
                                                  ].rearrange("p (dc f) -> p dc f", dc=8)
                                             ).then_inc(S["dS%d" % pj], 16)
                            else:
                                sy.dma_start(dsth[:, :, idx],
                                             wstg[:, pj * HID:(pj + 1) * HID
                                                  ].rearrange("p (hc f) -> p hc f", hc=NHC)
                                             ).then_inc(S["dS%d" % pj], 16)
                        sy.wait_ge(S["dS0"], 16 * (NJ - NJ // 2))
                        sy.wait_ge(S["dS1"], 16 * (NJ // 2))

            with ExitStack() as pA:
              if "A" in phases:
                EA = pA.enter_context
                xt = EA(nc.sbuf_tensor("xt", [128, 2 * DM], F32))
                xn_t = EA(nc.sbuf_tensor("xn_t", [128, DM], F32))
                xn2_t = EA(nc.sbuf_tensor("xn2_t", [128, DM], BF16))
                xn2f_t = EA(nc.sbuf_tensor("xn2f_t", [128, DM], BF16))
                qtmp = EA(nc.sbuf_tensor("qtmp", [128, DM], F32))
                qs2 = EA(nc.sbuf_tensor("qs2", [128, DM], F32))
                ks2 = EA(nc.sbuf_tensor("ks2", [128, 256], F32))
                qaff = EA(nc.sbuf_tensor("qaff", [128, DM], F32))
                kaff = EA(nc.sbuf_tensor("kaff", [128, 256], F32))
                qsqf = EA(nc.sbuf_tensor("qsqf", [128, DM], F32))
                ksqf = EA(nc.sbuf_tensor("ksqf", [128, 256], F32))
                qhat = EA(nc.sbuf_tensor("qhat", [128, DM], BF16))
                khat = EA(nc.sbuf_tensor("khat", [128, GROUPS * DH], BF16))
                bnb = EA(nc.sbuf_tensor("bnb", [128, 12], F32))
                qstat = EA(nc.sbuf_tensor("qstat", [128, 32], F32))
                kstat = EA(nc.sbuf_tensor("kstat", [128, 8], F32))
                st1 = EA(nc.sbuf_tensor("st1", [128, 2], F32))
                st2 = EA(nc.sbuf_tensor("st2", [128, 2], F32))
                sd1 = EA(nc.sbuf_tensor("sd1", [128, 4], F32))
                qmu = EA(nc.sbuf_tensor("qmu", [128, 16], F32))
                qstd = EA(nc.sbuf_tensor("qstd", [128, 16], F32))
                kstd = EA(nc.sbuf_tensor("kstd", [128, 4], F32))
                qrr = EA(nc.sbuf_tensor("qrr", [128, 16], F32))
                kmu = EA(nc.sbuf_tensor("kmu", [128, 4], F32))
                krr = EA(nc.sbuf_tensor("krr", [128, 4], F32))
                qvar = EA(nc.sbuf_tensor("qvar", [128, 16], F32))
                kvar = EA(nc.sbuf_tensor("kvar", [128, 4], F32))
                nmr = EA(nc.sbuf_tensor("nmr", [128, 20], F32))
                ascr = EA(nc.sbuf_tensor("ascr", [128, 20], F32))
                q_stage = EA(nc.sbuf_tensor("q_stage", [128, 2 * NPAIR * 128], BF16))
                qsq = EA(nc.sbuf_tensor("qsq", [128, 16], F32))
                ksq = EA(nc.sbuf_tensor("ksq", [128, 4], F32))
                qmaxsq = EA(nc.sbuf_tensor("qmaxsq", [128, 16], F32))
                kmaxsq = EA(nc.sbuf_tensor("kmaxsq", [128, 4], F32))
                maxrow = EA(nc.sbuf_tensor("maxrow", [16, 2], F32))
                normc = EA(nc.sbuf_tensor("normc", [16, 2], F32))
                crow = EA(nc.sbuf_tensor("crow", [1, 40], F32))
                negrow = EA(nc.sbuf_tensor("negrow", [1, 16], F32))
                q_ps = EA(nc.psum_tensor("q_ps", [128, 2 * DM], F32))
                kv_ps = EA(nc.psum_tensor("kv_ps", [128, 2 * 512], F32))
                tq_ps = EA(nc.psum_tensor("tq_ps", [128, 1536], BF16))

                with nc.Block() as blk:
                    @blk.gpsimd
                    def _(g):
                        g.memset(qmaxsq[:, :], 0.0)
                        g.memset(kmaxsq[:, :], 0.0)

                sv_f = [[0] * 5 for _ in range(NTT)]
                sv_b = [dict() for _ in range(NTT)]
                sa_f = [dict() for _ in range(NTT)]
                sa_b = [dict() for _ in range(NTT)]
                cnt = 0
                acnt = 0
                for it in range(NTT + 1):
                    if it < NTT:
                        for k in (1, 2, 3, 4):
                            cnt += 1
                            sv_f[it][k] = cnt
                    if it >= 1:
                        for k in ("B5", "B6", "B7", "B8", "B9", "B10", "B12"):
                            cnt += 1
                            sv_b[it - 1][k] = cnt
                    if it < NTT:
                        for k in ("a1", "a2"):
                            acnt += 1
                            sa_f[it][k] = acnt
                    if it >= 1:
                        for k in ("a3", "a4", "a5", "a6", "a7", "a8", "a9", "a10"):
                            acnt += 1
                            sa_b[it - 1][k] = acnt
                sp_x2T = [1 if it == 0 else 7 * it - 3 for it in range(NTT)]
                sp_proj = [3 if it == 0 else 7 * it - 1 for it in range(NTT)]
                sp_qkT = [7 * bt + 10 for bt in range(NTT)]
                sp_qkT[NTT - 1] = 7 * NTT
                R64 = 1.0 / 64.0

                with nc.Block() as blk:
                    @blk.vector
                    def _(v):
                        for it in range(NTT + 1):
                            if it < NTT:
                                par = it % 2
                                xv = xt[:, par * DM:(par + 1) * DM]
                                v.wait_ge(S["dA%d" % par], 16 * (it // 2 + 1))
                                v.bn_stats(bnb[:, 0:6], xv[:, 0:512])
                                v.bn_stats(bnb[:, 6:12], xv[:, 512:1024])
                                v.drain()
                                v.bn_aggr(st1[:, :], bnb[:, 0:12])
                                v.drain()
                                v.tensor_copy(stats_sb[:, it:it + 1], st1[:, 0:1]
                                              ).then_inc(S["sV"], 1)
                                v.scalar_tensor_tensor(qtmp[:, :], xv, st1[:, 0:1],
                                                       lnsc_f[:, :], ALU.subtract, ALU.mult
                                                       ).then_inc(S["sXF"], 1)
                                v.drain()
                                v.wait_ge(S["sA"], sa_f[it]["a1"])
                                v.reciprocal(stats_sb[:, NTT + it:NTT + it + 1],
                                             sd1[:, 0:1]).then_inc(S["sV"], 1)
                                v.wait_ge(S["sA"], sa_f[it]["a2"])
                                v.reciprocal(sd1[:, 3:4], sd1[:, 1:2])
                                v.drain()
                                if it >= 1:
                                    v.wait_ge(S["sP"], sp_x2T[it - 1])
                                v.scalar_tensor_tensor(xn2_t[:, :], qtmp[:, :], sd1[:, 3:4],
                                                       lnb_f[:, :], ALU.mult, ALU.add
                                                       ).then_inc(S["sV"], 1)
                                v.wait_ge(S["sP"], sp_x2T[it])
                                v.tensor_copy(xn2f_t[:, :], tq_ps[:, 0:DM]
                                              ).then_inc(S["sV"], 1)
                            if it >= 1:
                                bt = it - 1
                                bpar = bt % 2
                                qp = q_ps[:, bpar * DM:(bpar + 1) * DM]
                                kvp = kv_ps[:, bpar * 512:(bpar + 1) * 512]
                                v.wait_ge(S["sP"], sp_proj[bt])
                                v.tensor_reduce(qstat[:, 0:16],
                                                qp.rearrange("p (s f) -> p s f", f=64),
                                                AX.X, ALU.add)
                                v.drain()
                                v.wait_ge(S["sA"], sa_b[bt]["a3"])
                                v.tensor_reduce(qstat[:, 16:32],
                                                qs2[:, :].rearrange("p (s f) -> p s f", f=64),
                                                AX.X, ALU.add)
                                v.drain()
                                v.tensor_scalar(qmu[:, :], qstat[:, 0:16], R64, None,
                                                ALU.mult, ALU.bypass)
                                v.drain()
                                v.tensor_tensor(ascr[:, 0:16], qmu[:, :], qmu[:, :], ALU.mult)
                                v.drain()
                                v.scalar_tensor_tensor(qvar[:, :], qstat[:, 16:32], R64,
                                                       ascr[:, 0:16], ALU.mult, ALU.subtract
                                                       ).then_inc(S["sV"], 1)
                                v.tensor_reduce(kstat[:, 0:4],
                                                kvp[:, 0:256].rearrange("p (s f) -> p s f", f=64),
                                                AX.X, ALU.add)
                                v.drain()
                                v.wait_ge(S["sA"], sa_b[bt]["a4"])
                                v.tensor_reduce(kstat[:, 4:8],
                                                ks2[:, :].rearrange("p (s f) -> p s f", f=64),
                                                AX.X, ALU.add)
                                v.drain()
                                v.tensor_scalar(kmu[:, :], kstat[:, 0:4], R64, None,
                                                ALU.mult, ALU.bypass)
                                v.drain()
                                v.tensor_tensor(ascr[:, 16:20], kmu[:, :], kmu[:, :], ALU.mult)
                                v.drain()
                                v.scalar_tensor_tensor(kvar[:, :], kstat[:, 4:8], R64,
                                                       ascr[:, 16:20], ALU.mult, ALU.subtract
                                                       ).then_inc(S["sV"], 1)
                                v.wait_ge(S["sA"], sa_b[bt]["a5"])
                                v.reciprocal(qrr[:, :], qstd[:, :])
                                v.drain()
                                v.scalar_tensor_tensor(nmr[:, 0:16], qmu[:, :], -1.0,
                                                       qrr[:, :], ALU.mult, ALU.mult
                                                       ).then_inc(S["sV"], 1)
                                v.wait_ge(S["sA"], sa_b[bt]["a6"])
                                v.reciprocal(krr[:, :], kstd[:, :])
                                v.drain()
                                v.scalar_tensor_tensor(nmr[:, 16:20], kmu[:, :], -1.0,
                                                       krr[:, :], ALU.mult, ALU.mult
                                                       ).then_inc(S["sV"], 1)
                                v.wait_ge(S["sA"], sa_b[bt]["a7"])
                                v.tensor_tensor(qtmp[:, :], qaff[:, :], qnsc_f[:, :], ALU.mult)
                                v.drain()
                                v.tensor_tensor(qhat[:, :], qtmp[:, :], qnb_f[:, :],
                                                ALU.add).then_inc(S["sV"], 1)
                                v.drain()
                                v.wait_ge(S["sA"], sa_b[bt]["a8"])
                                v.tensor_tensor(qtmp[:, 0:256], kaff[:, :], knsc_f[:, :],
                                                ALU.mult)
                                v.drain()
                                v.tensor_tensor(khat[:, :], qtmp[:, 0:256], knb_f[:, :],
                                                ALU.add).then_inc(S["sV"], 1)
                                v.wait_ge(S["sA"], sa_b[bt]["a9"])
                                v.tensor_reduce(qsq[:, :],
                                                qsqf[:, :].rearrange("p (s f) -> p s f", f=64),
                                                AX.X, ALU.add)
                                v.drain()
                                v.tensor_max(qmaxsq[:, :], qmaxsq[:, :], qsq[:, :])
                                v.wait_ge(S["sA"], sa_b[bt]["a10"])
                                v.tensor_reduce(ksq[:, :],
                                                ksqf[:, :].rearrange("p (s f) -> p s f", f=64),
                                                AX.X, ALU.add)
                                v.drain()
                                v.tensor_max(kmaxsq[:, :], kmaxsq[:, :], ksq[:, :])
                                v.tensor_copy(
                                    v_sb[:, :].rearrange("p (g c s) -> p g c s",
                                                         g=GROUPS, c=NCH)[:, :, bt, 0:64],
                                    kvp[:, 256:512].rearrange("p (g s) -> p g s", g=GROUPS))
                                if bt >= 2:
                                    v.wait_ge(S["dQA%d" % (bt % 2)], 16 * (bt // 2))
                                v.wait_ge(S["sP"], sp_qkT[bt])
                                v.tensor_copy(
                                    q_stage[:, (bt % 2) * NPAIR * 128:
                                            (bt % 2 + 1) * NPAIR * 128],
                                    tq_ps[:, 0:1024])
                                v.tensor_copy(
                                    k_sb[:, :].rearrange("p (g t) -> p g t",
                                                         g=GROUPS)[:, :, bt * 128:(bt + 1) * 128],
                                    tq_ps[:, 1024:1536].rearrange("p (g t) -> p g t", g=GROUPS)
                                ).then_inc(S["sV"], 1)

                    @blk.scalar
                    def _(sc):
                        sc.preload_act_table(AF.Sqrt)
                        for it in range(NTT + 1):
                            if it < NTT:
                                sc.wait_ge(S["sV"], sv_f[it][1])
                                sc.activation(sd1[:, 0:1], st1[:, 1:2], AF.Sqrt,
                                              bias=epsv[:, 0:1], scale=1.0).then_inc(S["sA"], 1)
                                sc.wait_ge(S["sV"], sv_f[it][2])
                                sc.activation(sd1[:, 1:2], st1[:, 1:2], AF.Sqrt,
                                              bias=epsv[:, 2:3],
                                              scale=1.0 + EPS2).then_inc(S["sA"], 1)
                            if it >= 1:
                                bt = it - 1
                                bpar = bt % 2
                                qp = q_ps[:, bpar * DM:(bpar + 1) * DM]
                                kvp = kv_ps[:, bpar * 512:(bpar + 1) * 512]
                                sc.wait_ge(S["sP"], sp_proj[bt])
                                sc.activation(qs2[:, :], qp, AF.Square).then_inc(S["sA"], 1)
                                sc.activation(ks2[:, :], kvp[:, 0:256], AF.Square
                                              ).then_inc(S["sA"], 1)
                                sc.wait_ge(S["sV"], sv_b[bt]["B5"])
                                sc.activation(qstd[:, :], qvar[:, :], AF.Sqrt,
                                              bias=epsv[:, 1:2], scale=1.0).then_inc(S["sA"], 1)
                                sc.wait_ge(S["sV"], sv_b[bt]["B6"])
                                sc.activation(kstd[:, :], kvar[:, :], AF.Sqrt,
                                              bias=epsv[:, 1:2], scale=1.0).then_inc(S["sA"], 1)
                                sc.wait_ge(S["sV"], sv_b[bt]["B7"])
                                for s_ in range(16):
                                    ins = sc.activation(qaff[:, 64 * s_:64 * s_ + 64],
                                                        qp[:, 64 * s_:64 * s_ + 64], AF.Identity,
                                                        bias=nmr[:, s_:s_ + 1],
                                                        scale=qrr[:, s_:s_ + 1])
                                ins.then_inc(S["sA"], 1)
                                sc.wait_ge(S["sV"], sv_b[bt]["B8"])
                                for s_ in range(4):
                                    ins = sc.activation(kaff[:, 64 * s_:64 * s_ + 64],
                                                        kvp[:, 64 * s_:64 * s_ + 64], AF.Identity,
                                                        bias=nmr[:, 16 + s_:17 + s_],
                                                        scale=krr[:, s_:s_ + 1])
                                ins.then_inc(S["sA"], 1)
                                sc.wait_ge(S["sV"], sv_b[bt]["B9"])
                                sc.activation(qsqf[:, :], qhat[:, :], AF.Square
                                              ).then_inc(S["sA"], 1)
                                sc.wait_ge(S["sV"], sv_b[bt]["B10"])
                                sc.activation(ksqf[:, :], khat[:, :], AF.Square
                                              ).then_inc(S["sA"], 1)

                    @blk.tensor
                    def _(t):
                        for it in range(NTT + 1):
                            if it < NTT:
                                par = it % 2
                                t.wait_ge(S["sV"], sv_f[it][3])
                                for dc in range(8):
                                    ins = t.transpose(tq_ps[:, dc * 128:(dc + 1) * 128],
                                                      xn2_t[:, dc * 128:(dc + 1) * 128],
                                                      id_bf[:, :])
                                ins.then_inc(S["sP"], 1)
                                t.wait_ge(S["sV"], sv_f[it][4])
                                for dc in range(8):
                                    lhs = xn2f_t[:, dc * 128:(dc + 1) * 128]
                                    t.matmul(q_ps[:, par * DM:par * DM + 512], lhs,
                                             wq_sb[:, dc * 1024:dc * 1024 + 512],
                                             start=(dc == 0), stop=(dc == 7))
                                    t.matmul(q_ps[:, par * DM + 512:par * DM + 1024], lhs,
                                             wq_sb[:, dc * 1024 + 512:(dc + 1) * 1024],
                                             start=(dc == 0), stop=(dc == 7))
                                    ins = t.matmul(kv_ps[:, par * 512:(par + 1) * 512], lhs,
                                                   wkv_sb[:, dc * 512:(dc + 1) * 512],
                                                   start=(dc == 0), stop=(dc == 7))
                                ins.then_inc(S["sP"], 2)
                            if it >= 1:
                                bt = it - 1
                                t.wait_ge(S["sV"], sv_b[bt]["B10"])
                                for r in range(NPAIR):
                                    t.transpose(tq_ps[0:64, r * 128:(r + 1) * 128],
                                                qhat[:, (2 * r) * 64:(2 * r + 1) * 64],
                                                id_bf[:, :])
                                    t.transpose(tq_ps[64:128, r * 128:(r + 1) * 128],
                                                qhat[:, (2 * r + 1) * 64:(2 * r + 2) * 64],
                                                id_bf[:, :], tile_position=(0, 64))
                                for gi in range(GROUPS):
                                    t.transpose(tq_ps[0:64, 1024 + gi * 128:1024 + (gi + 1) * 128],
                                                khat[:, gi * 64:(gi + 1) * 64], id_bf[:, :])
                                    ins = t.transpose(
                                        tq_ps[64:128, 1024 + gi * 128:1024 + (gi + 1) * 128],
                                        khat[:, gi * 64:(gi + 1) * 64], id_bf[:, :],
                                        tile_position=(0, 64))
                                ins.then_inc(S["sP"], 4)

                    @blk.sync
                    def _(sy):
                        for tt in range(NTT):
                            if tt >= 2:
                                sy.wait_ge(S["sXF"], tt - 1)
                            sy.dma_start(xt[:, (tt % 2) * DM:(tt % 2 + 1) * DM],
                                         x_h[tt * 128:(tt + 1) * 128, :]
                                         ).then_inc(S["dA%d" % (tt % 2)], 16)
                            if tt >= 2:
                                sy.wait_ge(S["sV"], sv_b[tt - 2]["B12"])
                                sy.dma_start(
                                    q_scr[(tt - 2) // 4].rearrange(
                                        "p (r q) -> p r q", r=NPAIR
                                    )[:, :, ((tt - 2) % 4) * 128:((tt - 2) % 4 + 1) * 128],
                                    q_stage[:, ((tt - 2) % 2) * NPAIR * 128:
                                            ((tt - 2) % 2 + 1) * NPAIR * 128
                                            ].rearrange("p (r f) -> p r f", r=NPAIR)
                                ).then_inc(S["dQA%d" % ((tt - 2) % 2)], 16)
                        for tt in (NTT - 2, NTT - 1):
                            sy.wait_ge(S["sV"], sv_b[tt]["B12"])
                            sy.dma_start(
                                q_scr[tt // 4].rearrange("p (r q) -> p r q", r=NPAIR
                                                         )[:, :, (tt % 4) * 128:(tt % 4 + 1) * 128],
                                q_stage[:, (tt % 2) * NPAIR * 128:(tt % 2 + 1) * NPAIR * 128
                                        ].rearrange("p (r f) -> p r f", r=NPAIR)
                            ).then_inc(S["dQA%d" % (tt % 2)], 16)

                with nc.Block() as blk:
                    @blk.tensor
                    def _(t):
                        t.transpose(kv_ps[0:16, 0:128], qmaxsq[:, :], id_f[:, :])
                        t.transpose(kv_ps[0:4, 128:256], kmaxsq[:, :], id_f[:, :])
                with nc.Block() as blk:
                    @blk.vector
                    def _(v):
                        v.reduce_max(maxrow[0:16, 0:1], kv_ps[0:16, 0:128], AX.X)
                        v.reduce_max(maxrow[0:4, 1:2], kv_ps[0:4, 128:256], AX.X)
                with nc.Block() as blk:
                    @blk.scalar
                    def _(sc):
                        sc.activation(normc[0:16, 0:1], maxrow[0:16, 0:1], AF.Sqrt)
                        sc.activation(normc[0:4, 1:2], maxrow[0:4, 1:2], AF.Sqrt)
                with nc.Block() as blk:
                    @blk.tensor
                    def _(t):
                        t.transpose(kv_ps[0:1, 256:272], normc[0:16, 0:1], id_f[0:16, 0:16])
                        t.transpose(kv_ps[0:1, 272:276], normc[0:4, 1:2], id_f[0:4, 0:4])
                with nc.Block() as blk:
                    @blk.vector
                    def _(v):
                        v.tensor_copy(crow[0:1, 0:16], kv_ps[0:1, 256:272])
                        v.tensor_copy(crow[0:1, 32:36], kv_ps[0:1, 272:276])
                with nc.Block() as blk:
                    @blk.vector
                    def _(v):
                        for h in range(HEADS):
                            v.tensor_copy(crow[0:1, 16 + h:17 + h],
                                          crow[0:1, 32 + h // 4:33 + h // 4])
                with nc.Block() as blk:
                    @blk.vector
                    def _(v):
                        v.tensor_tensor(negrow[0:1, :], crow[0:1, 0:16], crow[0:1, 16:32],
                                        ALU.mult)
                        v.drain()
                        v.tensor_scalar(negrow[0:1, :], negrow[0:1, :], -1.0, None,
                                        ALU.mult, ALU.bypass)
                with nc.Block() as blk:
                    @blk.tensor
                    def _(t):
                        t.matmul(kv_ps[:, 0:16], ones1f[0:1, 0:128], negrow[0:1, :],
                                 start=True, stop=True)
                with nc.Block() as blk:
                    @blk.vector
                    def _(v):
                        v.tensor_copy(negc_sb[:, :], kv_ps[:, 0:16])

        if any(k in tp for k in ("ksb", "vsb", "negc", "stats")):
            with nc.Block() as blk:
                @blk.sync
                def _(sy):
                    for name, t_ in [("ksb", k_sb), ("vsb", v_sb),
                                     ("negc", negc_sb), ("stats", stats_sb)]:
                        if name in tp:
                            sy.dma_start(tp[name], t_[:, :]).then_inc(S["dS"], 16)
                            nS[0] += 16
                    sy.wait_ge(S["dS"], nS[0])

        with ExitStack() as pB:
          if "B" in phases:
            EB = pB.enter_context
            wout_sb = EB(nc.sbuf_tensor("wout_sb", [128, 8 * 1024], BF16))
            pbuf = EB(nc.sbuf_tensor("pbuf", [128, 2 * DEPTH * QTB], BF16))
            qbuf = EB(nc.sbuf_tensor("qbuf", [128, 2 * NPAIR * QTB], BF16))
            att_sb = EB(nc.sbuf_tensor("att_sb", [128, NPAIR * QTB], BF16))
            s_sb = EB(nc.sbuf_tensor("s_sb", [128, 2 * 8 * QTB], BF16))
            recb = EB(nc.sbuf_tensor("recb", [64, 2 * QTB], F32))
            rec_sb = EB(nc.sbuf_tensor("rec_sb", [64, 2 * QTB], F32))
            av_sb = EB(nc.sbuf_tensor("av_sb", [128, 2 * QTB], F32))
            atmp = EB(nc.sbuf_tensor("atmp", [64, QTB], BF16))
            rscr = EB(nc.sbuf_tensor("rscr", [1, QTB], F32))
            st_ps = EB(nc.psum_tensor("st_ps", [128, 2048], F32))
            av_ps = EB(nc.psum_tensor("av_ps", [128, 2048], F32))

            with nc.Block() as blk:
                @blk.gpsimd
                def _(g):
                    if "A" in phases:
                        g.wait_ge(S["dQA0"], 16 * (NTT // 2))
                        g.wait_ge(S["dQA1"], 16 * (NTT // 2))
                    g.dma_start(wout_sb[:, :].rearrange("p (c n) -> p c n", c=8),
                                wout_h.rearrange("(c p) n -> p c n", p=128)).then_inc(S["dW"], 16)
                    nW[0] += 16
                    g.memset(recb[:, :], 0.0)
                    g.wait_ge(S["dW"], nW[0])

            def pslot(h, cc):
                c0 = (h * DEPTH + cc % DEPTH) * QTB
                return pbuf[:, c0:c0 + QTB]

            with nc.Block() as blk:
                @blk.tensor
                def _(t):
                    def emit_qk_pair(u, p):
                        qtb, pr = divmod(u, NPAIR)
                        gg = pr // 2
                        qc0 = (qtb % 2) * NPAIR * QTB + pr * QTB
                        if p == 0:
                            t.wait_ge(S["dQ%d" % (qtb % 2)], 16 * (qtb // 2 + 1))
                        for cc in (2 * p, 2 * p + 1):
                            g_ = u * NCH + cc
                            if g_ >= 2:
                                t.wait_ge(S["sEXP"], g_ // 2)
                            kc = gg * SEQ + cc * 128
                            t.matmul(st_ps[:, (cc % 2) * 512:(cc % 2) * 512 + QTB],
                                     k_sb[0:64, kc:kc + 128], qbuf[0:64, qc0:qc0 + QTB],
                                     start=True, stop=True, tile_position=(0, 0)
                                     ).then_inc(S["sQK"], 1)
                        for cc in (2 * p, 2 * p + 1):
                            g_ = u * NCH + cc
                            if g_ >= 2:
                                t.wait_ge(S["sEXP1"], g_ // 2)
                            kc = gg * SEQ + cc * 128
                            t.matmul(st_ps[:, 1024 + (cc % 2) * 512:1024 + (cc % 2) * 512 + QTB],
                                     k_sb[64:128, kc:kc + 128], qbuf[64:128, qc0:qc0 + QTB],
                                     start=True, stop=True, tile_position=(64, 0)
                                     ).then_inc(S["sQK1"], 1)

                    def norm_bc(up):
                        if up >= NPAIR and up % NPAIR == 0:
                            t.wait_ge(S["sSE"], 8 * (up // NPAIR))
                        for h in range(2):
                            t.wait_ge(S["sRCP"], 2 * up + h + 1)
                            if up >= 1 and h == 0:
                                t.wait_ge(S["sMV2"], up)
                            t.matmul(av_ps[0:64, 1024 + h * 512:1024 + h * 512 + QTB],
                                     ones64f[:, :], recb[:, h * QTB:(h + 1) * QTB],
                                     start=True, stop=True,
                                     tile_position=(0, 0)).then_inc(S["sBC"], 1)

                    def norm_id(up):
                        if up >= NPAIR and up % NPAIR == 0:
                            t.wait_ge(S["sSE"], 8 * (up // NPAIR))
                        t.wait_ge(S["sNRM"], 2 * up + 2)
                        t.matmul(av_ps[64:128, 1024:1024 + QTB], id_bf[0:64, 0:64],
                                 atmp[:, :], start=True, stop=True,
                                 tile_position=(0, 64)).then_inc(S["sMV"], 1)

                    def wout_dc(qp, dc):
                        if dc == 0:
                            t.wait_ge(S["sNRM"], 16 * (qp + 1))
                            t.wait_ge(S["sMV2"], 8 * (qp + 1))
                        if dc >= 2:
                            t.wait_ge(S["sSE"], 8 * qp + dc - 1)
                        for ic in range(8):
                            ins_w = t.matmul(
                                av_ps[:, 1024 + (dc % 2) * 512:1024 + (dc % 2) * 512 + QTB],
                                wout_sb[:, ic * 1024 + dc * 128:ic * 1024 + (dc + 1) * 128],
                                att_sb[:, ic * QTB:(ic + 1) * QTB],
                                start=(ic == 0), stop=(ic == 7))
                        ins_w.then_inc(S["sWO"], 1)

                    def wout(qp):
                        for dc in range(8):
                            wout_dc(qp, dc)

                    av_cur = [0]

                    def emit_av_upto(limit):
                        while av_cur[0] < min(limit, NU * NCH):
                            a = av_cur[0]
                            uu, cc = divmod(a, NCH)
                            gg = (uu % NPAIR) // 2
                            if uu >= 1:
                                if cc == 16:
                                    norm_bc(uu - 1)
                                elif cc == 24:
                                    norm_id(uu - 1)
                                elif (cc in (25, 27, 29, 31)
                                      and (uu - 1) % NPAIR == NPAIR - 1):
                                    wout_dc((uu - 1) // NPAIR, cc - 25)
                                    wout_dc((uu - 1) // NPAIR, cc - 24)
                            t.wait_ge(S["sEXP"], a // 2 + 1)
                            t.wait_ge(S["sEXP1"], a // 2 + 1)
                            if cc == 0:
                                if uu >= 1:
                                    t.wait_ge(S["sAVC"], uu)
                            vcol = (gg * NCH + cc) * 65
                            t.matmul(av_ps[0:65, 0:512], v_sb[:, vcol:vcol + 65],
                                     pslot(0, cc), start=(cc == 0), stop=(cc == NCH - 1))
                            t.matmul(av_ps[0:65, 512:1024], v_sb[:, vcol:vcol + 65],
                                     pslot(1, cc), start=(cc == 0), stop=(cc == NCH - 1)
                                     ).then_inc(S["sAV"], 1)
                            av_cur[0] += 1

                    for P in range(NU * NCH // 2):
                        u, p = divmod(P, NCH // 2)
                        emit_av_upto(2 * P - 14)
                        emit_qk_pair(u, p)
                    emit_av_upto(NU * NCH)
                    norm_bc(NU - 1)
                    norm_id(NU - 1)
                    wout(NQTB - 1)

                @blk.scalar
                def _(sc):
                    sc.add_instruction(mybir.InstLoadActFuncSet(
                        name=nc.get_next_instruction_name(), act_func_set_id=6,
                        ins=[], outs=[]))
                    for u in range(NU):
                        qtb, pr = divmod(u, NPAIR)
                        for p in range(NCH // 2):
                            g1 = u * NCH + 2 * p + 1
                            if g1 >= DEPTH:
                                sc.wait_ge(S["sAV"], g1 - DEPTH + 1)
                            for h in range(2):
                                sc.wait_ge(S["sQK%s" % ("" if h == 0 else "1")],
                                           u * NCH + 2 * p + 2)
                                c0 = (h * DEPTH + (2 * p) % DEPTH) * QTB
                                sc.activation(pbuf[:, c0:c0 + 2 * QTB],
                                              st_ps[:, h * 1024:(h + 1) * 1024],
                                              AF.Exp, bias=negc_sb[:, 2 * pr + h:2 * pr + h + 1],
                                              scale=1.0).then_inc(
                                    S["sEXP%s" % ("" if h == 0 else "1")], 1)

                @blk.vector
                def _(v):
                    def emit_rec(ru):
                        for h in range(2):
                            if ru >= 1:
                                v.wait_ge(S["sBC"], 2 * (ru - 1) + h + 1)
                            v.reciprocal(recb[0:1, h * QTB:(h + 1) * QTB],
                                         av_sb[64:65, h * 512:h * 512 + QTB]
                                         ).then_inc(S["sRCP"], 1)
                            v.drain()

                    for u in range(NU):
                        qtb, pr = divmod(u, NPAIR)
                        v.wait_ge(S["sAV"], NCH * (u + 1))
                        v.tensor_copy(av_sb[0:65, :], av_ps[0:65, 0:1024]
                                      ).then_inc(S["sAVC"], 1)
                        v.drain()
                        emit_rec(u)
                        for h in range(2):
                            v.wait_ge(S["sBC"], 2 * u + h + 1)
                            v.tensor_copy(rec_sb[:, h * QTB:(h + 1) * QTB],
                                          av_ps[0:64, 1024 + h * 512:1024 + h * 512 + QTB])
                        v.wait_ge(S["sWO"], 8 * qtb)
                        v.drain()
                        v.tensor_tensor(att_sb[0:64, pr * QTB:(pr + 1) * QTB],
                                        av_sb[0:64, 0:512], rec_sb[:, 0:QTB], ALU.mult
                                        ).then_inc(S["sNRM"], 1)
                        if u >= 1:
                            v.wait_ge(S["sMV"], u)
                        v.tensor_tensor(atmp[:, :], av_sb[0:64, 512:1024],
                                        rec_sb[:, QTB:2 * QTB], ALU.mult
                                        ).then_inc(S["sNRM"], 1)
                        v.wait_ge(S["sMV"], u + 1)
                        v.tensor_copy(att_sb[64:128, pr * QTB:(pr + 1) * QTB],
                                      av_ps[64:128, 1024:1024 + QTB]).then_inc(S["sMV2"], 1)
                        if pr == NPAIR - 1:
                            qp = qtb
                            par = qp % 2
                            if qp >= 2:
                                v.wait_ge(S["dSp%d" % par], 16 * (qp // 2))
                            for dc in range(8):
                                v.wait_ge(S["sWO"], 8 * qp + dc + 1)
                                v.tensor_copy(s_sb[:, (par * 8 + dc) * QTB:(par * 8 + dc + 1) * QTB],
                                              av_ps[:, 1024 + (dc % 2) * 512:
                                                    1024 + (dc % 2) * 512 + QTB]
                                              ).then_inc(S["sSE"], 1)

                @blk.sync
                def _(sy):
                    sy.wait_ge(S["dQA0"], 16 * (NTT // 2))
                    sy.wait_ge(S["dQA1"], 16 * (NTT // 2))
                    for qtb in range(NQTB):
                        if qtb >= 2:
                            sy.wait_ge(S["sQK"], NCH * NPAIR * (qtb - 1))
                            sy.wait_ge(S["sQK1"], NCH * NPAIR * (qtb - 1))
                        sy.dma_start(qbuf[:, (qtb % 2) * NPAIR * QTB:(qtb % 2 + 1) * NPAIR * QTB],
                                     q_scr[qtb]).then_inc(S["dQ%d" % (qtb % 2)], 16)
                        if qtb >= 1:
                            par = (qtb - 1) % 2
                            sy.wait_ge(S["sSE"], 8 * qtb)
                            sy.dma_start(
                                s_scr[:, :, (qtb - 1) * QTB:qtb * QTB].rearrange(
                                    "dc p q -> p dc q"),
                                s_sb[:, par * 8 * QTB:(par + 1) * 8 * QTB].rearrange(
                                    "p (dc q) -> p dc q", dc=8)).then_inc(S["dSp%d" % par], 16)
                    par = (NQTB - 1) % 2
                    sy.wait_ge(S["sSE"], 8 * NQTB)
                    sy.dma_start(
                        s_scr[:, :, (NQTB - 1) * QTB:NQTB * QTB].rearrange("dc p q -> p dc q"),
                        s_sb[:, par * 8 * QTB:(par + 1) * 8 * QTB].rearrange(
                            "p (dc q) -> p dc q", dc=8)).then_inc(S["dSp%d" % par], 16)

        ab.close()

        if "B" in phases and "C" in phases:
            with nc.Block() as blk:
                @blk.sync
                def _(sy):
                    sy.wait_ge(S["dSp0"], 16 * (NQTB // 2))
                    sy.wait_ge(S["dSp1"], 16 * (NQTB // 2))

        with ExitStack() as pC:
          if "C" in phases:
            EC = pC.enter_context
            wdT_sb = EC(nc.sbuf_tensor("wdT_sb", [128, NHC * 8 * 128], BF16))
            hbuf = EC(nc.sbuf_tensor("hbuf", [128, NHC * TC], BF16))
            xnf_c = EC(nc.sbuf_tensor("xnf_c", [128, 8 * TC], BF16))
            g_c = EC(nc.sbuf_tensor("g_c", [128, 8 * TC], BF16))
            s_c = EC(nc.sbuf_tensor("s_c", [128, 2 * 8 * TC], BF16))
            xc = EC(nc.sbuf_tensor("xc", [128, 3 * 4 * DM], F32))
            wsl = EC(nc.sbuf_tensor("wsl", [128, 2 * 2 * 2 * 8 * 128], BF16))
            sg_c = EC(nc.sbuf_tensor("sg_c", [128, 2 * TC], BF16))
            xn_c = EC(nc.sbuf_tensor("xn_c", [128, DM], BF16))
            pt_c = EC(nc.sbuf_tensor("pt_c", [128, 8 * TC], F32))
            gu_ps = EC(nc.psum_tensor("gu_ps", [128, 2 * TC], F32))
            up_ps = EC(nc.psum_tensor("up_ps", [128, 2 * TC], F32))
            dn_ps = EC(nc.psum_tensor("dn_ps", [128, 2 * TC], F32))
            xT_ps = EC(nc.psum_tensor("xT_ps", [128, DM], BF16))
            pT_ps = EC(nc.psum_tensor("pT_ps", [128, TC], F32))

            SLABW = 2 * 8 * 128

            def wslab(buf, mat, hcs, dc):
                c0 = buf * 2 * SLABW + mat * SLABW + (hcs * 8 + dc) * 128
                return wsl[:, c0:c0 + 128]

            if cmode == "gate":
                _skip_c = True
            else:
                _skip_c = False
            if not _skip_c and cmode != "full":
                with nc.Block() as blk:
                    @blk.gpsimd
                    def _(g):
                        g.memset(g_c[:, :], 1.0)

            if _skip_c:
                with nc.Block() as blk:
                    @blk.sync
                    def _(sy):
                        pass
            if not _skip_c:
              with nc.Block() as blk:
                @blk.scalar
                def _(sc):
                    sc.preload_act_table(AF.Silu)

                @blk.sync
                def _(sy):
                    if cmode == "full":
                        sy.dma_start(wdT_sb[:, :].rearrange("p (hc dc f) -> p hc dc f",
                                                            hc=NHC, dc=8),
                                     wdT_scr).then_inc(S["dStr"], 16)
                    def _emit_out(oc):
                        sy.wait_ge(S["sAD"], 8 * (oc + 1))
                        sy.dma_start(out_h[oc * TC:(oc + 1) * TC, :].rearrange(
                                         "(tt p) f -> p tt f", p=128),
                                     xc[:, (oc % 3) * 4 * DM:(oc % 3 + 1) * 4 * DM
                                        ].rearrange("p (tt f) -> p tt f", tt=4)
                                     ).then_inc(S["dOUT%d" % (oc % 2)], 16)

                    for tc in range(NTC):
                        if cmode != "cout":
                            if tc >= 3 and cmode not in ("cx",):
                                sy.wait_ge(S["dOUT%d" % ((tc - 3) % 2)],
                                           16 * ((tc - 3) // 2 + 1))
                            sy.dma_start(xc[:, (tc % 3) * 4 * DM:(tc % 3 + 1) * 4 * DM
                                            ].rearrange("p (tt f) -> p tt f", tt=4),
                                         x_h[tc * TC:(tc + 1) * TC, :].rearrange(
                                             "(tt p) f -> p tt f", p=128)
                                         ).then_inc(S["dXC%d" % (tc % 3)], 16)
                        if tc >= 2 and cmode not in ("cx",):
                            _emit_out(tc - 2)
                    if cmode not in ("cx",):
                        _emit_out(NTC - 2)
                        _emit_out(NTC - 1)
                    if cmode not in ("cx",):
                        sy.wait_ge(S["dOUT0"], 16 * (NTC // 2))
                        sy.wait_ge(S["dOUT1"], 16 * (NTC // 2))
                    else:
                        sy.wait_ge(S["dXC0"], 16 * (NTC // 2))
                        sy.wait_ge(S["dXC1"], 16 * (NTC // 2))

                @blk.gpsimd
                def _(g):
                    for tc in range(NTC if cmode != "cxout" else 0):
                        if "B" in phases:
                            g.wait_ge(S["dSp%d" % (tc % 2)], 16 * (tc // 2 + 1))
                        if tc >= 2:
                            g.wait_ge(S["sPm"], 8 * (tc - 1))
                        g.dma_start(s_c[:, (tc % 2) * 8 * TC:(tc % 2 + 1) * 8 * TC
                                        ].rearrange("p (dc q) -> p dc q", dc=8),
                                    s_scr[:, :, tc * TC:(tc + 1) * TC].rearrange(
                                        "dc p q -> p dc q")).then_inc(S["dSr%d" % (tc % 2)], 16)
                        for sl_ in range(NSLAB if cmode == "full" else 0):
                            gs = tc * NSLAB + sl_
                            if gs >= 2:
                                g.wait_ge(S["sGU"], 2 * (gs - 2) + 2)
                            buf = gs % 2
                            g.dma_start(
                                wsl[:, buf * 2 * SLABW:buf * 2 * SLABW + SLABW
                                    ].rearrange("p (hc dc f) -> p hc dc f", hc=2, dc=8),
                                wgT_scr[:, sl_ * 2:(sl_ + 1) * 2]
                                ).then_inc(S["dSl%d" % (gs % 2)], 16)
                            g.dma_start(
                                wsl[:, buf * 2 * SLABW + SLABW:buf * 2 * SLABW + 2 * SLABW
                                    ].rearrange("p (hc dc f) -> p hc dc f", hc=2, dc=8),
                                wuT_scr[:, sl_ * 2:(sl_ + 1) * 2]
                                ).then_inc(S["dSl%d" % (gs % 2)], 16)

                def _dve_combine(v, ct):
                    for dc in range(8):
                        D = 8 * ct + dc
                        if cmode in ("motion", "cxout", "cx", "cout"):
                            if cmode == "motion":
                                v.wait_ge(S["dSr%d" % (ct % 2)], 16 * (ct // 2 + 1))
                            v.engine_nop().then_inc(S["sPm"], 1)
                            v.engine_nop().then_inc(S["sAD"], 1)
                            continue
                        if cmode == "full":
                            v.wait_ge(S["sG"], D + 1)
                        v.wait_ge(S["dSr%d" % (ct % 2)], 16 * (ct // 2 + 1))
                        if D >= 2:
                            v.wait_ge(S["sPT2"], D - 1)
                        v.tensor_tensor(pt_c[:, dc * TC:(dc + 1) * TC],
                                        s_c[:, ((ct % 2) * 8 + dc) * TC:
                                            ((ct % 2) * 8 + dc + 1) * TC],
                                        g_c[:, dc * TC:(dc + 1) * TC],
                                        ALU.mult).then_inc(S["sPm"], 1)
                        v.wait_ge(S["sPT2"], D + 1)
                        v.tensor_tensor(
                            xc[:, (ct % 3) * 4 * DM:(ct % 3 + 1) * 4 * DM
                               ].rearrange("p (tt f) -> p tt f", tt=4
                                           )[:, :, dc * 128:(dc + 1) * 128],
                            xc[:, (ct % 3) * 4 * DM:(ct % 3 + 1) * 4 * DM
                               ].rearrange("p (tt f) -> p tt f", tt=4
                                           )[:, :, dc * 128:(dc + 1) * 128],
                            pT_ps[:, :].rearrange("p (tt f) -> p tt f", tt=4),
                            ALU.add).then_inc(S["sAD"], 1)

                def _dve_pt(v, ct):
                    for dc in range(8):
                        v.wait_ge(S["sG"], 8 * ct + dc + 1)
                        v.wait_ge(S["dSr%d" % (ct % 2)], 16 * (ct // 2 + 1))
                        if ct >= 1:
                            v.wait_ge(S["sPT2"], 8 * (ct - 1) + dc + 1)
                        v.tensor_tensor(pt_c[:, dc * TC:(dc + 1) * TC],
                                        s_c[:, ((ct % 2) * 8 + dc) * TC:
                                            ((ct % 2) * 8 + dc + 1) * TC],
                                        g_c[:, dc * TC:(dc + 1) * TC],
                                        ALU.mult).then_inc(S["sPm"], 1)

                def _add1(v, ct, dc):
                    v.wait_ge(S["sPT2"], 8 * ct + dc + 1)
                    v.tensor_tensor(
                        xc[:, (ct % 3) * 4 * DM:(ct % 3 + 1) * 4 * DM
                           ].rearrange("p (tt f) -> p tt f", tt=4
                                       )[:, :, dc * 128:(dc + 1) * 128],
                        xc[:, (ct % 3) * 4 * DM:(ct % 3 + 1) * 4 * DM
                           ].rearrange("p (tt f) -> p tt f", tt=4
                                       )[:, :, dc * 128:(dc + 1) * 128],
                        pT_ps[:, :].rearrange("p (tt f) -> p tt f", tt=4),
                        ALU.add).then_inc(S["sAD"], 1)

                def _dve_add(v, ct):
                    for dc in range(8):
                        _add1(v, ct, dc)

                def _xnf_ts(v, tcx, t4):
                    tt = tcx * 4 + t4
                    v.wait_ge(S["dXC%d" % (tcx % 3)], 16 * (tcx // 3 + 1))
                    if tcx >= 1 and t4 == 0:
                        v.wait_ge(S["sGU"], NHC * tcx)
                    v.tensor_scalar(xn_c[:, :],
                                    xc[:, ((tcx % 3) * 4 + t4) * DM:
                                       ((tcx % 3) * 4 + t4 + 1) * DM],
                                    stats_sb[:, tt:tt + 1],
                                    stats_sb[:, NTT + tt:NTT + tt + 1],
                                    ALU.subtract, ALU.mult).then_inc(S["sXN"], 1)

                def _xnf_copy(v, tcx, t4):
                    tt = tcx * 4 + t4
                    v.wait_ge(S["sP"], C_SP_BASE + tt + 1)
                    v.tensor_copy(xnf_c[:, :].rearrange("p (dc f) -> p dc f", dc=8
                                                        )[:, :, t4 * 128:(t4 + 1) * 128],
                                  xT_ps[:, :].rearrange("p (dc f) -> p dc f", dc=8)
                                  ).then_inc(S["sXNF"], 1)

                def _xnf_prod(v, tcx):
                    for t4 in range(4):
                        _xnf_ts(v, tcx, t4)
                        _xnf_copy(v, tcx, t4)

                @blk.vector
                def _(v):
                    for tc in range(NTC):
                        if cmode == "full":
                            if tc == 0:
                                _xnf_prod(v, 0)
                            if tc >= 1:
                                _dve_pt(v, tc - 1)
                        for hc in range(NHC if cmode == "full" else 0):
                            H = NHC * tc + hc
                            v.wait_ge(S["sSI"], H + 1)
                            if tc >= 1 and hc == 0:
                                v.wait_ge(S["sDN"], 8 * tc)
                            v.tensor_tensor(hbuf[:, hc * TC:(hc + 1) * TC],
                                            sg_c[:, (hc % 2) * TC:(hc % 2 + 1) * TC],
                                            up_ps[:, (hc % 2) * TC:(hc % 2 + 1) * TC],
                                            ALU.mult).then_inc(S["sH"], 1)
                        if cmode == "full":
                            if tc + 1 < NTC and tc >= 1:
                                _xnf_ts(v, tc + 1, 0)
                                for dc in range(4):
                                    _add1(v, tc - 1, dc)
                                _xnf_copy(v, tc + 1, 0)
                                for t4 in range(1, 4):
                                    _xnf_ts(v, tc + 1, t4)
                                    _add1(v, tc - 1, 3 + t4)
                                    _xnf_copy(v, tc + 1, t4)
                                _add1(v, tc - 1, 7)
                            elif tc + 1 < NTC:
                                _xnf_prod(v, tc + 1)
                            elif tc >= 1:
                                _dve_add(v, tc - 1)
                        elif tc >= 1:
                            _dve_combine(v, tc - 1)

                    if cmode == "full":
                        _dve_pt(v, NTC - 1)
                        _dve_add(v, NTC - 1)
                    else:
                        _dve_combine(v, NTC - 1)

                @blk.scalar
                def _(sc):
                    for tc in range(NTC if cmode == "full" else 0):
                        for hc in range(NHC):
                            H = NHC * tc + hc
                            sc.wait_ge(S["sGU"], H + 1)
                            if H >= 1:
                                sc.wait_ge(S["sH"], H - 1)
                            sc.activation(sg_c[:, (hc % 2) * TC:(hc % 2 + 1) * TC],
                                          gu_ps[:, (hc % 2) * TC:(hc % 2 + 1) * TC],
                                          AF.Silu, scale=1.0 / 127.0).then_inc(S["sSI"], 1)
                        for dc in range(8):
                            D = 8 * tc + dc
                            sc.wait_ge(S["sDN"], D + 1)
                            if tc >= 1:
                                sc.wait_ge(S["sPm"], 8 * (tc - 1) + dc + 1)
                            sc.activation(g_c[:, dc * TC:(dc + 1) * TC],
                                          dn_ps[:, (dc % 2) * TC:(dc % 2 + 1) * TC],
                                          AF.Silu, scale=DQ2).then_inc(S["sG"], 1)

                def _pe_pt(t, ct, dc):
                    if cmode == "motion":
                        return
                    D = 8 * ct + dc
                    t.wait_ge(S["sPm"], D + 1)
                    if D >= 1:
                        t.wait_ge(S["sAD"], D)
                    for t4 in range(4):
                        ins = t.transpose(pT_ps[:, t4 * 128:(t4 + 1) * 128],
                                          pt_c[:, dc * TC + t4 * 128:
                                               dc * TC + (t4 + 1) * 128],
                                          id_f[:, :])
                    ins.then_inc(S["sPT2"], 1)

                def _xt_group(t, tcx, t4):
                    tt = tcx * 4 + t4
                    t.wait_ge(S["sXN"], tt + 1)
                    if tt >= 1:
                        t.wait_ge(S["sXNF"], tt)
                    for dc in range(8):
                        ins = t.transpose(xT_ps[:, dc * 128:(dc + 1) * 128],
                                          xn_c[:, dc * 128:(dc + 1) * 128], id_bf[:, :])
                    ins.then_inc(S["sP"], 1)

                @blk.tensor
                def _(t):
                    for tc in range(NTC):
                        if cmode == "full" and tc == 0:
                            for t4 in range(4):
                                _xt_group(t, 0, t4)
                        if cmode == "full":
                            t.wait_ge(S["sXNF"], 4 * (tc + 1))
                        for hc in range(NHC if cmode == "full" else 0):
                            H = NHC * tc + hc
                            gs = tc * NSLAB + hc // 2
                            t.wait_ge(S["dSl%d" % (gs % 2)], 32 * (gs // 2 + 1))
                            if H >= 2:
                                t.wait_ge(S["sSI"], H - 1)
                                t.wait_ge(S["sH"], H - 1)
                            buf = gs % 2
                            for dc in range(8):
                                t.matmul(gu_ps[:, (hc % 2) * TC:(hc % 2 + 1) * TC],
                                         wslab(buf, 0, hc % 2, dc),
                                         xnf_c[:, dc * TC:(dc + 1) * TC],
                                         start=(dc == 0), stop=(dc == 7))
                            for dc in range(8):
                                ins = t.matmul(up_ps[:, (hc % 2) * TC:(hc % 2 + 1) * TC],
                                               wslab(buf, 1, hc % 2, dc),
                                               xnf_c[:, dc * TC:(dc + 1) * TC],
                                               start=(dc == 0), stop=(dc == 7))
                            ins.then_inc(S["sGU"], 1)
                        if cmode == "full":
                            t.wait_ge(S["sH"], NHC * (tc + 1))
                        for dc in range(8 if cmode == "full" else 0):
                            D = 8 * tc + dc
                            if D == 0:
                                t.wait_ge(S["dStr"], 16)
                            if tc >= 1:
                                _pe_pt(t, tc - 1, dc)
                            if tc + 1 < NTC and dc >= 4:
                                _xt_group(t, tc + 1, dc - 4)
                            if D >= 2:
                                t.wait_ge(S["sG"], D - 1)
                            for hc in range(NHC):
                                ins = t.matmul(dn_ps[:, (dc % 2) * TC:(dc % 2 + 1) * TC],
                                               wdT_sb[:, (hc * 8 + dc) * 128:
                                                      (hc * 8 + dc + 1) * 128],
                                               hbuf[:, hc * TC:(hc + 1) * TC],
                                               start=(hc == 0), stop=(hc == NHC - 1))
                            ins.then_inc(S["sDN"], 1)
                        if cmode != "full" and tc >= 1:
                            for dc in range(8):
                                _pe_pt(t, tc - 1, dc)
                    if cmode != "motion":
                        for dc in range(8):
                            _pe_pt(t, NTC - 1, dc)

        if "C" in phases and cmode != "gate":
            if any(k in tp for k in ("gg", "hb", "xnf")):
                with nc.Block() as blk:
                    @blk.sync
                    def _(sy):
                        for name, t_ in [("gg", g_c), ("hb", hbuf), ("xnf", xnf_c)]:
                            if name in tp:
                                sy.dma_start(tp[name], t_[:, :]).then_inc(S["dS"], 16)
                                nS[0] += 16
                        sy.wait_ge(S["dS"], nS[0])
            if cmode not in ("cx",):
                with nc.Block() as blk:
                    @blk.sync
                    def _(sy):
                        sy.wait_ge(S["dOUT0"], 16 * (NTC // 2))
                        sy.wait_ge(S["dOUT1"], 16 * (NTC // 2))

    return nc


_CACHE = {}


def kernel(L0_attn_x, L0_attn_ln_scale, L0_attn_ln_bias, L0_attn_Wq, L0_attn_Wkv,
           L0_attn_qn_scale, L0_attn_qn_bias, L0_attn_kn_scale, L0_attn_kn_bias,
           L0_attn_Wout, L0_ffn_w_gate, L0_ffn_w_up, L0_ffn_w_down):
    x = np.ascontiguousarray(np.asarray(L0_attn_x, dtype=np.float32))
    shared = {
        "wq": np.ascontiguousarray(np.asarray(L0_attn_Wq, np.float32)),
        "wkv": np.ascontiguousarray(np.asarray(L0_attn_Wkv, np.float32)),
        "wout": np.ascontiguousarray(np.asarray(L0_attn_Wout, np.float32)),
        "wg": np.ascontiguousarray(np.asarray(L0_ffn_w_gate, np.int32)),
        "wu": np.ascontiguousarray(np.asarray(L0_ffn_w_up, np.int32)),
        "wd": np.ascontiguousarray(np.asarray(L0_ffn_w_down, np.int32)),
        "ln_sc": np.ascontiguousarray(np.asarray(L0_attn_ln_scale, np.float32)),
        "ln_b": np.ascontiguousarray(np.asarray(L0_attn_ln_bias, np.float32)),
        "qn_sc": np.ascontiguousarray(np.asarray(L0_attn_qn_scale, np.float32)),
        "qn_b": np.ascontiguousarray(np.asarray(L0_attn_qn_bias, np.float32)),
        "kn_sc": np.ascontiguousarray(np.asarray(L0_attn_kn_scale, np.float32)),
        "kn_b": np.ascontiguousarray(np.asarray(L0_attn_kn_bias, np.float32)),
    }
    if "nc" not in _CACHE:
        _CACHE["nc"] = build()
    nc = _CACHE["nc"]
    in_maps = [dict(shared, x=np.ascontiguousarray(x[b])) for b in range(8)]
    res = run_bass_kernel_spmd(nc, in_maps, core_ids=list(range(8)))
    return np.stack([res.results[b]["out"] for b in range(8)], axis=0)


# version 30
# speedup vs baseline: 1.0128x; 1.0100x over previous
import numpy as np
from contextlib import ExitStack

import concourse.bass as bass
import concourse.mybir as mybir
from concourse.bass_utils import run_bass_kernel_spmd

F32 = mybir.dt.float32
BF16 = mybir.dt.bfloat16
I32 = mybir.dt.int32
AF = mybir.ActivationFunctionType
ALU = mybir.AluOpType
AX = mybir.AxisListType

SEQ, DM, HEADS, GROUPS, DH, HID = 4096, 1024, 16, 4, 64, 3072
NTT = SEQ // 128
QTB = 512
NQTB = SEQ // QTB
DEPTH = 16
NPAIR = HEADS // 2
NCH = SEQ // 128
NU = NQTB * NPAIR
TC = 512
NTC = SEQ // TC
NHC = HID // 128
NSLAB = NHC // 2
EPS1, EPS2 = 1e-5, 1e-6
QS = 0.125
DQ2 = 1.0 / (127.0 * 127.0)

AV_, AA_, AP_ = 10, 6, 7
C_SP_BASE = AP_ * NTT


class _SkipPhase(Exception):
    pass


def build(taps=(), phases="0ABC", cmode="full"):
    nc = bass.Bass("TRN2", target_bir_lowering=False)

    x_h = nc.dram_tensor("x", [SEQ, DM], F32, kind="ExternalInput").ap()
    wq_h = nc.dram_tensor("wq", [DM, HEADS * DH], F32, kind="ExternalInput").ap()
    wkv_h = nc.dram_tensor("wkv", [DM, 2 * GROUPS * DH], F32, kind="ExternalInput").ap()
    wout_h = nc.dram_tensor("wout", [HEADS * DH, DM], F32, kind="ExternalInput").ap()
    wg_h = nc.dram_tensor("wg", [HID, DM], I32, kind="ExternalInput").ap()
    wu_h = nc.dram_tensor("wu", [HID, DM], I32, kind="ExternalInput").ap()
    wd_h = nc.dram_tensor("wd", [DM, HID], I32, kind="ExternalInput").ap()
    lnsc_h = nc.dram_tensor("ln_sc", [DM], F32, kind="ExternalInput").ap()
    lnb_h = nc.dram_tensor("ln_b", [DM], F32, kind="ExternalInput").ap()
    qnsc_h = nc.dram_tensor("qn_sc", [DH], F32, kind="ExternalInput").ap()
    qnb_h = nc.dram_tensor("qn_b", [DH], F32, kind="ExternalInput").ap()
    knsc_h = nc.dram_tensor("kn_sc", [DH], F32, kind="ExternalInput").ap()
    knb_h = nc.dram_tensor("kn_b", [DH], F32, kind="ExternalInput").ap()
    out_h = nc.dram_tensor("out", [SEQ, DM], F32, kind="ExternalOutput").ap()

    _wk = "ExternalOutput" if "wscr" in taps else "Internal"
    wgT_scr = nc.dram_tensor("wgT_scr", [128, NHC, 8, 128], BF16, kind=_wk).ap()
    wuT_scr = nc.dram_tensor("wuT_scr", [128, NHC, 8, 128], BF16, kind=_wk).ap()
    wdT_scr = nc.dram_tensor("wdT_scr", [128, NHC, 8, 128], BF16, kind=_wk).ap()
    s_scr = nc.dram_tensor("s_scr", [8, 128, SEQ], BF16,
                           kind="ExternalOutput" if "ssc" in taps else "Internal").ap()
    q_scr = nc.dram_tensor("q_scr", [NQTB, 128, NPAIR * QTB], BF16,
                           kind="ExternalOutput" if "qscr" in taps else "Internal").ap()

    tp = {}
    for name, shape, dt in [
        ("ksb", [128, GROUPS * SEQ], BF16),
        ("vsb", [128, GROUPS * NCH * 65], BF16), ("negc", [128, HEADS], F32),
        ("stats", [128, 2 * NTT], F32), ("gg", [128, 8 * TC], BF16),
        ("hb", [128, NHC * TC], BF16), ("xnf", [128, 8 * TC], BF16),
    ]:
        if name in taps:
            tp[name] = nc.dram_tensor("tap_" + name, shape, dt, kind="ExternalOutput").ap()

    with ExitStack() as top:
        E = top.enter_context
        stats_sb = E(nc.sbuf_tensor("stats_sb", [128, 2 * NTT], F32))
        negc_sb = E(nc.sbuf_tensor("negc_sb", [128, HEADS], F32))
        id_bf = E(nc.sbuf_tensor("id_bf", [128, 128], BF16))
        id_f = E(nc.sbuf_tensor("id_f", [128, 128], F32))
        ones1f = E(nc.sbuf_tensor("ones1f", [1, 128], F32))
        ones64f = E(nc.sbuf_tensor("ones64f", [64, 64], F32))
        ones64b = E(nc.sbuf_tensor("ones64b", [64, 64], BF16))
        epsv = E(nc.sbuf_tensor("epsv", [128, 3], F32))

        S = {}
        for s_ in ["dA0", "dA1", "dW", "dS", "dSp0", "dSp1", "dSr0", "dSr1",
                   "dStr", "dSl0", "dSl1", "dXC0", "dXC1", "dOUT0", "dOUT1",
                   "dQ0", "dQ1", "dQA0", "dQA1", "sMV2", "sEXP1", "sQK1",
                   "dS0", "dS1", "s0P", "s0V", "d0L0", "d0L1", "sAVC", "sXF",
                   "dXC2",
                   "sP", "sV", "sA", "sQK", "sEXP", "sAV", "sRCP", "sNRM",
                   "sAN", "sBC", "sMV", "sSE", "sWO", "sPT2",
                   "sGU", "sSI", "sH", "sDN", "sG", "sPm", "sAD",
                   "sXN", "sXNF"]:
            S[s_] = nc.alloc_semaphore(name=s_)
        nW = [0]
        nS = [0]

        ab = top.enter_context(ExitStack())
        EAB = ab.enter_context
        k_sb = EAB(nc.sbuf_tensor("k_sb", [128, GROUPS * SEQ], BF16))
        v_sb = EAB(nc.sbuf_tensor("v_sb", [128, GROUPS * NCH * 65], BF16))

        with ExitStack() as p0:
            E0 = p0.enter_context
            wq_sb = E0(nc.sbuf_tensor("wq_sb", [128, 8 * 1024], BF16))
            wkv_sb = E0(nc.sbuf_tensor("wkv_sb", [128, 8 * 512], BF16))
            vec_sb = E0(nc.sbuf_tensor("vec_sb", [1, 2 * DM + 4 * DH], F32))
            lnsc_f = E0(nc.sbuf_tensor("lnsc_f", [128, DM], F32))
            lnb_f = E0(nc.sbuf_tensor("lnb_f", [128, DM], F32))
            qnsc_f = E0(nc.sbuf_tensor("qnsc_f", [128, DM], F32))
            qnb_f = E0(nc.sbuf_tensor("qnb_f", [128, DM], F32))
            knsc_f = E0(nc.sbuf_tensor("knsc_f", [128, GROUPS * DH], F32))
            knb_f = E0(nc.sbuf_tensor("knb_f", [128, GROUPS * DH], F32))

            with nc.Block() as blk:
                @blk.gpsimd
                def _(g):
                    g.dma_start(wq_sb[:, :].rearrange("p (c n) -> p c n", c=8),
                                wq_h.rearrange("(c p) n -> p c n", p=128)).then_inc(S["dW"], 16)
                    g.dma_start(wkv_sb[:, :].rearrange("p (c n) -> p c n", c=8),
                                wkv_h.rearrange("(c p) n -> p c n", p=128)).then_inc(S["dW"], 16)
                    for i, h in enumerate([lnsc_h, lnb_h]):
                        g.dma_start(vec_sb[0:1, i * DM:(i + 1) * DM], h[None, :]
                                    ).then_inc(S["dW"], 16)
                    for i, h in enumerate([qnsc_h, qnb_h, knsc_h, knb_h]):
                        g.dma_start(vec_sb[0:1, 2 * DM + i * DH:2 * DM + (i + 1) * DH],
                                    h[None, :]).then_inc(S["dW"], 16)
                    nW[0] += 8 * 16
                    for t_ in (id_bf, id_f):
                        g.memset(t_[:, :], 0.0)
                    g.memset(ones1f[:, :], 1.0)
                    g.memset(ones64f[:, :], 1.0)
                    g.memset(ones64b[:, :], 1.0)
                    g.memset(epsv[:, 0:1], EPS1)
                    g.memset(epsv[:, 1:2], EPS2)
                    g.memset(epsv[:, 2:3], EPS1 * EPS2)
                    g.memset(v_sb[:, :].rearrange("p (c s) -> p c s", s=65)[:, :, 64:65], 1.0)
                    g.wait_ge(S["dW"], nW[0])

            with nc.Block() as blk:
                @blk.gpsimd
                def _(g):
                    for t_ in (id_bf, id_f):
                        g.affine_select(out=t_[:, :], in_=t_[:, :],
                                        compare_op=ALU.not_equal, fill=1.0,
                                        base=0, pattern=[[-1, 128]], channel_multiplier=1)

            with ExitStack() as bc:
                bps = bc.enter_context(nc.psum_tensor("bps", [128, DM], F32))
                for (dst_fill, src_lo, src_n) in [
                        ("ln", 0, DM), ("ln2", DM, DM), ("hd", 2 * DM, 4 * DH)]:
                    with nc.Block() as blk:
                        @blk.tensor
                        def _(t, lo=src_lo, n=src_n):
                            for o in range(0, n, 512):
                                w = min(512, n - o)
                                t.matmul(bps[:, o:o + w], ones1f[0:1, 0:128],
                                         vec_sb[0:1, lo + o:lo + o + w],
                                         start=True, stop=True)
                    with nc.Block() as blk:
                        @blk.vector
                        def _(v, tag=dst_fill, n=src_n):
                            if tag == "ln":
                                v.tensor_copy(lnsc_f[:, :], bps[:, 0:DM])
                            elif tag == "ln2":
                                v.tensor_copy(lnb_f[:, :], bps[:, 0:DM])
                            else:
                                for r in range(HEADS):
                                    v.tensor_scalar(qnsc_f[:, r * DH:(r + 1) * DH],
                                                    bps[:, 0:DH], QS, None,
                                                    ALU.mult, ALU.bypass)
                                    v.tensor_scalar(qnb_f[:, r * DH:(r + 1) * DH],
                                                    bps[:, DH:2 * DH], QS, None,
                                                    ALU.mult, ALU.bypass)
                                for r in range(GROUPS):
                                    v.tensor_copy(knsc_f[:, r * DH:(r + 1) * DH],
                                                  bps[:, 2 * DH:3 * DH])
                                    v.tensor_copy(knb_f[:, r * DH:(r + 1) * DH],
                                                  bps[:, 3 * DH:4 * DH])

            with ExitStack() as p0b:
                E0b = p0b.enter_context
                wint = E0b(nc.sbuf_tensor("wint", [128, 2 * HID], BF16))
                wstg = E0b(nc.sbuf_tensor("wstg", [128, 2 * HID], BF16))
                tps = E0b(nc.psum_tensor("tps", [128, 1024], BF16))
                jobs = []
                for srch, dsth in [(wg_h, wgT_scr), (wu_h, wuT_scr)]:
                    for hc in range(NHC):
                        jobs.append(("gu", srch, dsth, hc))
                for dc in range(8):
                    jobs.append(("wd", wd_h, wdT_scr, dc))
                rounds = [1 if j[0] == "gu" else 3 for j in jobs]
                rtot = [0]
                for r_ in rounds:
                    rtot.append(rtot[-1] + r_)
                NJ = len(jobs)

                with nc.Block() as blk:
                    @blk.gpsimd
                    def _(g):
                        nl = [0, 0]
                        for j, (kind, srch, dsth, idx) in enumerate(jobs):
                            pj = j % 2
                            if j >= 2:
                                g.wait_ge(S["s0P"], rtot[j - 1])
                            ncols = 1024 if kind == "gu" else HID
                            g.dma_start(wint[:, pj * HID:pj * HID + ncols],
                                        srch[idx * 128:(idx + 1) * 128, :]
                                        ).then_inc(S["d0L%d" % pj], 16)

                    @blk.tensor
                    def _(t):
                        nload = [0, 0]
                        for j, (kind, srch, dsth, idx) in enumerate(jobs):
                            pj = j % 2
                            nload[pj] += 16
                            t.wait_ge(S["d0L%d" % pj], nload[pj])
                            for r_ in range(rounds[j]):
                                if rtot[j] + r_ >= 1:
                                    t.wait_ge(S["s0V"], rtot[j] + r_)
                                for i in range(8):
                                    ins = t.transpose(
                                        tps[:, i * 128:(i + 1) * 128],
                                        wint[:, pj * HID + (r_ * 8 + i) * 128:
                                             pj * HID + (r_ * 8 + i + 1) * 128],
                                        id_bf[:, :])
                                ins.then_inc(S["s0P"], 1)

                    @blk.vector
                    def _(v):
                        nst = [0, 0]
                        for j, (kind, srch, dsth, idx) in enumerate(jobs):
                            pj = j % 2
                            if j >= 2:
                                nst[pj] += 16
                                v.wait_ge(S["dS%d" % pj], nst[pj])
                            for r_ in range(rounds[j]):
                                v.wait_ge(S["s0P"], rtot[j] + r_ + 1)
                                v.tensor_copy(wstg[:, pj * HID + r_ * 1024:
                                                   pj * HID + (r_ + 1) * 1024],
                                              tps[:, :]).then_inc(S["s0V"], 1)

                    @blk.sync
                    def _(sy):
                        for j, (kind, srch, dsth, idx) in enumerate(jobs):
                            pj = j % 2
                            sy.wait_ge(S["s0V"], rtot[j + 1])
                            if kind == "gu":
                                sy.dma_start(dsth[:, idx],
                                             wstg[:, pj * HID:pj * HID + 1024
                                                  ].rearrange("p (dc f) -> p dc f", dc=8)
                                             ).then_inc(S["dS%d" % pj], 16)
                            else:
                                sy.dma_start(dsth[:, :, idx],
                                             wstg[:, pj * HID:(pj + 1) * HID
                                                  ].rearrange("p (hc f) -> p hc f", hc=NHC)
                                             ).then_inc(S["dS%d" % pj], 16)
                        sy.wait_ge(S["dS0"], 16 * (NJ - NJ // 2))
                        sy.wait_ge(S["dS1"], 16 * (NJ // 2))

            with ExitStack() as pA:
              if "A" in phases:
                EA = pA.enter_context
                xt = EA(nc.sbuf_tensor("xt", [128, 2 * DM], F32))
                xn_t = EA(nc.sbuf_tensor("xn_t", [128, DM], F32))
                xn2_t = EA(nc.sbuf_tensor("xn2_t", [128, DM], BF16))
                xn2f_t = EA(nc.sbuf_tensor("xn2f_t", [128, DM], BF16))
                qtmp = EA(nc.sbuf_tensor("qtmp", [128, DM], F32))
                qs2 = EA(nc.sbuf_tensor("qs2", [128, DM], F32))
                ks2 = EA(nc.sbuf_tensor("ks2", [128, 256], F32))
                qaff = EA(nc.sbuf_tensor("qaff", [128, DM], F32))
                kaff = EA(nc.sbuf_tensor("kaff", [128, 256], F32))
                qsqf = EA(nc.sbuf_tensor("qsqf", [128, DM], F32))
                ksqf = EA(nc.sbuf_tensor("ksqf", [128, 256], F32))
                qhat = EA(nc.sbuf_tensor("qhat", [128, DM], BF16))
                khat = EA(nc.sbuf_tensor("khat", [128, GROUPS * DH], BF16))
                bnb = EA(nc.sbuf_tensor("bnb", [128, 12], F32))
                qstat = EA(nc.sbuf_tensor("qstat", [128, 32], F32))
                kstat = EA(nc.sbuf_tensor("kstat", [128, 8], F32))
                st1 = EA(nc.sbuf_tensor("st1", [128, 2], F32))
                st2 = EA(nc.sbuf_tensor("st2", [128, 2], F32))
                sd1 = EA(nc.sbuf_tensor("sd1", [128, 4], F32))
                qmu = EA(nc.sbuf_tensor("qmu", [128, 16], F32))
                qstd = EA(nc.sbuf_tensor("qstd", [128, 16], F32))
                kstd = EA(nc.sbuf_tensor("kstd", [128, 4], F32))
                qrr = EA(nc.sbuf_tensor("qrr", [128, 16], F32))
                kmu = EA(nc.sbuf_tensor("kmu", [128, 4], F32))
                krr = EA(nc.sbuf_tensor("krr", [128, 4], F32))
                qvar = EA(nc.sbuf_tensor("qvar", [128, 16], F32))
                kvar = EA(nc.sbuf_tensor("kvar", [128, 4], F32))
                nmr = EA(nc.sbuf_tensor("nmr", [128, 20], F32))
                ascr = EA(nc.sbuf_tensor("ascr", [128, 20], F32))
                q_stage = EA(nc.sbuf_tensor("q_stage", [128, 2 * NPAIR * 128], BF16))
                qsq = EA(nc.sbuf_tensor("qsq", [128, 16], F32))
                ksq = EA(nc.sbuf_tensor("ksq", [128, 4], F32))
                qmaxsq = EA(nc.sbuf_tensor("qmaxsq", [128, 16], F32))
                kmaxsq = EA(nc.sbuf_tensor("kmaxsq", [128, 4], F32))
                maxrow = EA(nc.sbuf_tensor("maxrow", [16, 2], F32))
                normc = EA(nc.sbuf_tensor("normc", [16, 2], F32))
                crow = EA(nc.sbuf_tensor("crow", [1, 40], F32))
                negrow = EA(nc.sbuf_tensor("negrow", [1, 16], F32))
                q_ps = EA(nc.psum_tensor("q_ps", [128, 2 * DM], F32))
                kv_ps = EA(nc.psum_tensor("kv_ps", [128, 2 * 512], F32))
                tq_ps = EA(nc.psum_tensor("tq_ps", [128, 1536], BF16))

                with nc.Block() as blk:
                    @blk.gpsimd
                    def _(g):
                        g.memset(qmaxsq[:, :], 0.0)
                        g.memset(kmaxsq[:, :], 0.0)

                sv_f = [[0] * 5 for _ in range(NTT)]
                sv_b = [dict() for _ in range(NTT)]
                sa_f = [dict() for _ in range(NTT)]
                sa_b = [dict() for _ in range(NTT)]
                cnt = 0
                acnt = 0
                for it in range(NTT + 1):
                    if it < NTT:
                        for k in (1, 2, 3, 4):
                            cnt += 1
                            sv_f[it][k] = cnt
                    if it >= 1:
                        for k in ("B5", "B6", "B7", "B8", "B9", "B10", "B12"):
                            cnt += 1
                            sv_b[it - 1][k] = cnt
                    if it < NTT:
                        for k in ("a1", "a2"):
                            acnt += 1
                            sa_f[it][k] = acnt
                    if it >= 1:
                        for k in ("a3", "a4", "a5", "a6", "a7", "a8", "a9", "a10"):
                            acnt += 1
                            sa_b[it - 1][k] = acnt
                sp_x2T = [1 if it == 0 else 7 * it - 3 for it in range(NTT)]
                sp_proj = [3 if it == 0 else 7 * it - 1 for it in range(NTT)]
                sp_qkT = [7 * bt + 10 for bt in range(NTT)]
                sp_qkT[NTT - 1] = 7 * NTT
                R64 = 1.0 / 64.0

                with nc.Block() as blk:
                    @blk.vector
                    def _(v):
                        for it in range(NTT + 1):
                            if it < NTT:
                                par = it % 2
                                xv = xt[:, par * DM:(par + 1) * DM]
                                v.wait_ge(S["dA%d" % par], 16 * (it // 2 + 1))
                                v.bn_stats(bnb[:, 0:6], xv[:, 0:512])
                                v.bn_stats(bnb[:, 6:12], xv[:, 512:1024])
                                v.drain()
                                v.bn_aggr(st1[:, :], bnb[:, 0:12])
                                v.drain()
                                v.tensor_copy(stats_sb[:, it:it + 1], st1[:, 0:1]
                                              ).then_inc(S["sV"], 1)
                                v.scalar_tensor_tensor(qtmp[:, :], xv, st1[:, 0:1],
                                                       lnsc_f[:, :], ALU.subtract, ALU.mult
                                                       ).then_inc(S["sXF"], 1)
                                v.drain()
                                v.wait_ge(S["sA"], sa_f[it]["a1"])
                                v.reciprocal(stats_sb[:, NTT + it:NTT + it + 1],
                                             sd1[:, 0:1]).then_inc(S["sV"], 1)
                                v.wait_ge(S["sA"], sa_f[it]["a2"])
                                v.reciprocal(sd1[:, 3:4], sd1[:, 1:2])
                                v.drain()
                                if it >= 1:
                                    v.wait_ge(S["sP"], sp_x2T[it - 1])
                                v.scalar_tensor_tensor(xn2_t[:, :], qtmp[:, :], sd1[:, 3:4],
                                                       lnb_f[:, :], ALU.mult, ALU.add
                                                       ).then_inc(S["sV"], 1)
                                v.wait_ge(S["sP"], sp_x2T[it])
                                v.tensor_copy(xn2f_t[:, :], tq_ps[:, 0:DM]
                                              ).then_inc(S["sV"], 1)
                            if it >= 1:
                                bt = it - 1
                                bpar = bt % 2
                                qp = q_ps[:, bpar * DM:(bpar + 1) * DM]
                                kvp = kv_ps[:, bpar * 512:(bpar + 1) * 512]
                                v.wait_ge(S["sP"], sp_proj[bt])
                                v.tensor_reduce(qstat[:, 0:16],
                                                qp.rearrange("p (s f) -> p s f", f=64),
                                                AX.X, ALU.add)
                                v.drain()
                                v.wait_ge(S["sA"], sa_b[bt]["a3"])
                                v.tensor_reduce(qstat[:, 16:32],
                                                qs2[:, :].rearrange("p (s f) -> p s f", f=64),
                                                AX.X, ALU.add)
                                v.drain()
                                v.tensor_scalar(qmu[:, :], qstat[:, 0:16], R64, None,
                                                ALU.mult, ALU.bypass)
                                v.drain()
                                v.tensor_tensor(ascr[:, 0:16], qmu[:, :], qmu[:, :], ALU.mult)
                                v.drain()
                                v.scalar_tensor_tensor(qvar[:, :], qstat[:, 16:32], R64,
                                                       ascr[:, 0:16], ALU.mult, ALU.subtract
                                                       ).then_inc(S["sV"], 1)
                                v.tensor_reduce(kstat[:, 0:4],
                                                kvp[:, 0:256].rearrange("p (s f) -> p s f", f=64),
                                                AX.X, ALU.add)
                                v.drain()
                                v.wait_ge(S["sA"], sa_b[bt]["a4"])
                                v.tensor_reduce(kstat[:, 4:8],
                                                ks2[:, :].rearrange("p (s f) -> p s f", f=64),
                                                AX.X, ALU.add)
                                v.drain()
                                v.tensor_scalar(kmu[:, :], kstat[:, 0:4], R64, None,
                                                ALU.mult, ALU.bypass)
                                v.drain()
                                v.tensor_tensor(ascr[:, 16:20], kmu[:, :], kmu[:, :], ALU.mult)
                                v.drain()
                                v.scalar_tensor_tensor(kvar[:, :], kstat[:, 4:8], R64,
                                                       ascr[:, 16:20], ALU.mult, ALU.subtract
                                                       ).then_inc(S["sV"], 1)
                                v.wait_ge(S["sA"], sa_b[bt]["a5"])
                                v.reciprocal(qrr[:, :], qstd[:, :])
                                v.drain()
                                v.scalar_tensor_tensor(nmr[:, 0:16], qmu[:, :], -1.0,
                                                       qrr[:, :], ALU.mult, ALU.mult
                                                       ).then_inc(S["sV"], 1)
                                v.wait_ge(S["sA"], sa_b[bt]["a6"])
                                v.reciprocal(krr[:, :], kstd[:, :])
                                v.drain()
                                v.scalar_tensor_tensor(nmr[:, 16:20], kmu[:, :], -1.0,
                                                       krr[:, :], ALU.mult, ALU.mult
                                                       ).then_inc(S["sV"], 1)
                                v.wait_ge(S["sA"], sa_b[bt]["a7"])
                                v.tensor_tensor(qtmp[:, :], qaff[:, :], qnsc_f[:, :], ALU.mult)
                                v.drain()
                                v.tensor_tensor(qhat[:, :], qtmp[:, :], qnb_f[:, :],
                                                ALU.add).then_inc(S["sV"], 1)
                                v.drain()
                                v.wait_ge(S["sA"], sa_b[bt]["a8"])
                                v.tensor_tensor(qtmp[:, 0:256], kaff[:, :], knsc_f[:, :],
                                                ALU.mult)
                                v.drain()
                                v.tensor_tensor(khat[:, :], qtmp[:, 0:256], knb_f[:, :],
                                                ALU.add).then_inc(S["sV"], 1)
                                v.wait_ge(S["sA"], sa_b[bt]["a9"])
                                v.tensor_reduce(qsq[:, :],
                                                qsqf[:, :].rearrange("p (s f) -> p s f", f=64),
                                                AX.X, ALU.add)
                                v.drain()
                                v.tensor_max(qmaxsq[:, :], qmaxsq[:, :], qsq[:, :])
                                v.wait_ge(S["sA"], sa_b[bt]["a10"])
                                v.tensor_reduce(ksq[:, :],
                                                ksqf[:, :].rearrange("p (s f) -> p s f", f=64),
                                                AX.X, ALU.add)
                                v.drain()
                                v.tensor_max(kmaxsq[:, :], kmaxsq[:, :], ksq[:, :])
                                v.tensor_copy(
                                    v_sb[:, :].rearrange("p (g c s) -> p g c s",
                                                         g=GROUPS, c=NCH)[:, :, bt, 0:64],
                                    kvp[:, 256:512].rearrange("p (g s) -> p g s", g=GROUPS))
                                if bt >= 2:
                                    v.wait_ge(S["dQA%d" % (bt % 2)], 16 * (bt // 2))
                                v.wait_ge(S["sP"], sp_qkT[bt])
                                v.tensor_copy(
                                    q_stage[:, (bt % 2) * NPAIR * 128:
                                            (bt % 2 + 1) * NPAIR * 128],
                                    tq_ps[:, 0:1024])
                                v.tensor_copy(
                                    k_sb[:, :].rearrange("p (g t) -> p g t",
                                                         g=GROUPS)[:, :, bt * 128:(bt + 1) * 128],
                                    tq_ps[:, 1024:1536].rearrange("p (g t) -> p g t", g=GROUPS)
                                ).then_inc(S["sV"], 1)

                    @blk.scalar
                    def _(sc):
                        sc.preload_act_table(AF.Sqrt)
                        for it in range(NTT + 1):
                            if it < NTT:
                                sc.wait_ge(S["sV"], sv_f[it][1])
                                sc.activation(sd1[:, 0:1], st1[:, 1:2], AF.Sqrt,
                                              bias=epsv[:, 0:1], scale=1.0).then_inc(S["sA"], 1)
                                sc.wait_ge(S["sV"], sv_f[it][2])
                                sc.activation(sd1[:, 1:2], st1[:, 1:2], AF.Sqrt,
                                              bias=epsv[:, 2:3],
                                              scale=1.0 + EPS2).then_inc(S["sA"], 1)
                            if it >= 1:
                                bt = it - 1
                                bpar = bt % 2
                                qp = q_ps[:, bpar * DM:(bpar + 1) * DM]
                                kvp = kv_ps[:, bpar * 512:(bpar + 1) * 512]
                                sc.wait_ge(S["sP"], sp_proj[bt])
                                sc.activation(qs2[:, :], qp, AF.Square).then_inc(S["sA"], 1)
                                sc.activation(ks2[:, :], kvp[:, 0:256], AF.Square
                                              ).then_inc(S["sA"], 1)
                                sc.wait_ge(S["sV"], sv_b[bt]["B5"])
                                sc.activation(qstd[:, :], qvar[:, :], AF.Sqrt,
                                              bias=epsv[:, 1:2], scale=1.0).then_inc(S["sA"], 1)
                                sc.wait_ge(S["sV"], sv_b[bt]["B6"])
                                sc.activation(kstd[:, :], kvar[:, :], AF.Sqrt,
                                              bias=epsv[:, 1:2], scale=1.0).then_inc(S["sA"], 1)
                                sc.wait_ge(S["sV"], sv_b[bt]["B7"])
                                for s_ in range(16):
                                    ins = sc.activation(qaff[:, 64 * s_:64 * s_ + 64],
                                                        qp[:, 64 * s_:64 * s_ + 64], AF.Identity,
                                                        bias=nmr[:, s_:s_ + 1],
                                                        scale=qrr[:, s_:s_ + 1])
                                ins.then_inc(S["sA"], 1)
                                sc.wait_ge(S["sV"], sv_b[bt]["B8"])
                                for s_ in range(4):
                                    ins = sc.activation(kaff[:, 64 * s_:64 * s_ + 64],
                                                        kvp[:, 64 * s_:64 * s_ + 64], AF.Identity,
                                                        bias=nmr[:, 16 + s_:17 + s_],
                                                        scale=krr[:, s_:s_ + 1])
                                ins.then_inc(S["sA"], 1)
                                sc.wait_ge(S["sV"], sv_b[bt]["B9"])
                                sc.activation(qsqf[:, :], qhat[:, :], AF.Square
                                              ).then_inc(S["sA"], 1)
                                sc.wait_ge(S["sV"], sv_b[bt]["B10"])
                                sc.activation(ksqf[:, :], khat[:, :], AF.Square
                                              ).then_inc(S["sA"], 1)

                    @blk.tensor
                    def _(t):
                        for it in range(NTT + 1):
                            if it < NTT:
                                par = it % 2
                                t.wait_ge(S["sV"], sv_f[it][3])
                                for dc in range(8):
                                    ins = t.transpose(tq_ps[:, dc * 128:(dc + 1) * 128],
                                                      xn2_t[:, dc * 128:(dc + 1) * 128],
                                                      id_bf[:, :])
                                ins.then_inc(S["sP"], 1)
                                t.wait_ge(S["sV"], sv_f[it][4])
                                for dc in range(8):
                                    lhs = xn2f_t[:, dc * 128:(dc + 1) * 128]
                                    t.matmul(q_ps[:, par * DM:par * DM + 512], lhs,
                                             wq_sb[:, dc * 1024:dc * 1024 + 512],
                                             start=(dc == 0), stop=(dc == 7))
                                    t.matmul(q_ps[:, par * DM + 512:par * DM + 1024], lhs,
                                             wq_sb[:, dc * 1024 + 512:(dc + 1) * 1024],
                                             start=(dc == 0), stop=(dc == 7))
                                    ins = t.matmul(kv_ps[:, par * 512:(par + 1) * 512], lhs,
                                                   wkv_sb[:, dc * 512:(dc + 1) * 512],
                                                   start=(dc == 0), stop=(dc == 7))
                                ins.then_inc(S["sP"], 2)
                            if it >= 1:
                                bt = it - 1
                                t.wait_ge(S["sV"], sv_b[bt]["B10"])
                                for r in range(NPAIR):
                                    t.transpose(tq_ps[0:64, r * 128:(r + 1) * 128],
                                                qhat[:, (2 * r) * 64:(2 * r + 1) * 64],
                                                id_bf[:, :])
                                    t.transpose(tq_ps[64:128, r * 128:(r + 1) * 128],
                                                qhat[:, (2 * r + 1) * 64:(2 * r + 2) * 64],
                                                id_bf[:, :], tile_position=(0, 64))
                                for gi in range(GROUPS):
                                    t.transpose(tq_ps[0:64, 1024 + gi * 128:1024 + (gi + 1) * 128],
                                                khat[:, gi * 64:(gi + 1) * 64], id_bf[:, :])
                                    ins = t.transpose(
                                        tq_ps[64:128, 1024 + gi * 128:1024 + (gi + 1) * 128],
                                        khat[:, gi * 64:(gi + 1) * 64], id_bf[:, :],
                                        tile_position=(0, 64))
                                ins.then_inc(S["sP"], 4)

                    @blk.sync
                    def _(sy):
                        for tt in range(NTT):
                            if tt >= 2:
                                sy.wait_ge(S["sXF"], tt - 1)
                            sy.dma_start(xt[:, (tt % 2) * DM:(tt % 2 + 1) * DM],
                                         x_h[tt * 128:(tt + 1) * 128, :]
                                         ).then_inc(S["dA%d" % (tt % 2)], 16)
                            if tt >= 2:
                                sy.wait_ge(S["sV"], sv_b[tt - 2]["B12"])
                                sy.dma_start(
                                    q_scr[(tt - 2) // 4].rearrange(
                                        "p (r q) -> p r q", r=NPAIR
                                    )[:, :, ((tt - 2) % 4) * 128:((tt - 2) % 4 + 1) * 128],
                                    q_stage[:, ((tt - 2) % 2) * NPAIR * 128:
                                            ((tt - 2) % 2 + 1) * NPAIR * 128
                                            ].rearrange("p (r f) -> p r f", r=NPAIR)
                                ).then_inc(S["dQA%d" % ((tt - 2) % 2)], 16)
                        for tt in (NTT - 2, NTT - 1):
                            sy.wait_ge(S["sV"], sv_b[tt]["B12"])
                            sy.dma_start(
                                q_scr[tt // 4].rearrange("p (r q) -> p r q", r=NPAIR
                                                         )[:, :, (tt % 4) * 128:(tt % 4 + 1) * 128],
                                q_stage[:, (tt % 2) * NPAIR * 128:(tt % 2 + 1) * NPAIR * 128
                                        ].rearrange("p (r f) -> p r f", r=NPAIR)
                            ).then_inc(S["dQA%d" % (tt % 2)], 16)

                with nc.Block() as blk:
                    @blk.tensor
                    def _(t):
                        t.transpose(kv_ps[0:16, 0:128], qmaxsq[:, :], id_f[:, :])
                        t.transpose(kv_ps[0:4, 128:256], kmaxsq[:, :], id_f[:, :])
                with nc.Block() as blk:
                    @blk.vector
                    def _(v):
                        v.reduce_max(maxrow[0:16, 0:1], kv_ps[0:16, 0:128], AX.X)
                        v.reduce_max(maxrow[0:4, 1:2], kv_ps[0:4, 128:256], AX.X)
                with nc.Block() as blk:
                    @blk.scalar
                    def _(sc):
                        sc.activation(normc[0:16, 0:1], maxrow[0:16, 0:1], AF.Sqrt)
                        sc.activation(normc[0:4, 1:2], maxrow[0:4, 1:2], AF.Sqrt)
                with nc.Block() as blk:
                    @blk.tensor
                    def _(t):
                        t.transpose(kv_ps[0:1, 256:272], normc[0:16, 0:1], id_f[0:16, 0:16])
                        t.transpose(kv_ps[0:1, 272:276], normc[0:4, 1:2], id_f[0:4, 0:4])
                with nc.Block() as blk:
                    @blk.vector
                    def _(v):
                        v.tensor_copy(crow[0:1, 0:16], kv_ps[0:1, 256:272])
                        v.tensor_copy(crow[0:1, 32:36], kv_ps[0:1, 272:276])
                with nc.Block() as blk:
                    @blk.vector
                    def _(v):
                        for h in range(HEADS):
                            v.tensor_copy(crow[0:1, 16 + h:17 + h],
                                          crow[0:1, 32 + h // 4:33 + h // 4])
                with nc.Block() as blk:
                    @blk.vector
                    def _(v):
                        v.tensor_tensor(negrow[0:1, :], crow[0:1, 0:16], crow[0:1, 16:32],
                                        ALU.mult)
                        v.drain()
                        v.tensor_scalar(negrow[0:1, :], negrow[0:1, :], -1.0, None,
                                        ALU.mult, ALU.bypass)
                with nc.Block() as blk:
                    @blk.tensor
                    def _(t):
                        t.matmul(kv_ps[:, 0:16], ones1f[0:1, 0:128], negrow[0:1, :],
                                 start=True, stop=True)
                with nc.Block() as blk:
                    @blk.vector
                    def _(v):
                        v.tensor_copy(negc_sb[:, :], kv_ps[:, 0:16])

        if any(k in tp for k in ("ksb", "vsb", "negc", "stats")):
            with nc.Block() as blk:
                @blk.sync
                def _(sy):
                    for name, t_ in [("ksb", k_sb), ("vsb", v_sb),
                                     ("negc", negc_sb), ("stats", stats_sb)]:
                        if name in tp:
                            sy.dma_start(tp[name], t_[:, :]).then_inc(S["dS"], 16)
                            nS[0] += 16
                    sy.wait_ge(S["dS"], nS[0])

        with ExitStack() as pB:
          if "B" in phases:
            EB = pB.enter_context
            wout_sb = EB(nc.sbuf_tensor("wout_sb", [128, 8 * 1024], BF16))
            pbuf = EB(nc.sbuf_tensor("pbuf", [128, 2 * DEPTH * QTB], BF16))
            qbuf = EB(nc.sbuf_tensor("qbuf", [128, 2 * NPAIR * QTB], BF16))
            att_sb = EB(nc.sbuf_tensor("att_sb", [128, NPAIR * QTB], BF16))
            s_sb = EB(nc.sbuf_tensor("s_sb", [128, 2 * 8 * QTB], BF16))
            recb = EB(nc.sbuf_tensor("recb", [64, 2 * QTB], F32))
            rec_sb = EB(nc.sbuf_tensor("rec_sb", [64, 2 * QTB], F32))
            av_sb = EB(nc.sbuf_tensor("av_sb", [128, 2 * QTB], F32))
            atmp = EB(nc.sbuf_tensor("atmp", [64, QTB], BF16))
            rscr = EB(nc.sbuf_tensor("rscr", [1, QTB], F32))
            st_ps = EB(nc.psum_tensor("st_ps", [128, 2048], F32))
            av_ps = EB(nc.psum_tensor("av_ps", [128, 2048], F32))

            with nc.Block() as blk:
                @blk.gpsimd
                def _(g):
                    if "A" in phases:
                        g.wait_ge(S["dQA0"], 16 * (NTT // 2))
                        g.wait_ge(S["dQA1"], 16 * (NTT // 2))
                    g.dma_start(wout_sb[:, :].rearrange("p (c n) -> p c n", c=8),
                                wout_h.rearrange("(c p) n -> p c n", p=128)).then_inc(S["dW"], 16)
                    nW[0] += 16
                    g.memset(recb[:, :], 0.0)
                    g.wait_ge(S["dW"], nW[0])

            def pslot(h, cc):
                c0 = (h * DEPTH + cc % DEPTH) * QTB
                return pbuf[:, c0:c0 + QTB]

            with nc.Block() as blk:
                @blk.tensor
                def _(t):
                    def emit_qk_pair(u, p):
                        qtb, pr = divmod(u, NPAIR)
                        gg = pr // 2
                        qc0 = (qtb % 2) * NPAIR * QTB + pr * QTB
                        if p == 0:
                            t.wait_ge(S["dQ%d" % (qtb % 2)], 16 * (qtb // 2 + 1))
                        for cc in (2 * p, 2 * p + 1):
                            g_ = u * NCH + cc
                            if g_ >= 2:
                                t.wait_ge(S["sEXP"], g_ // 2)
                            kc = gg * SEQ + cc * 128
                            t.matmul(st_ps[:, (cc % 2) * 512:(cc % 2) * 512 + QTB],
                                     k_sb[0:64, kc:kc + 128], qbuf[0:64, qc0:qc0 + QTB],
                                     start=True, stop=True, tile_position=(0, 0)
                                     ).then_inc(S["sQK"], 1)
                        for cc in (2 * p, 2 * p + 1):
                            g_ = u * NCH + cc
                            if g_ >= 2:
                                t.wait_ge(S["sEXP1"], g_ // 2)
                            kc = gg * SEQ + cc * 128
                            t.matmul(st_ps[:, 1024 + (cc % 2) * 512:1024 + (cc % 2) * 512 + QTB],
                                     k_sb[64:128, kc:kc + 128], qbuf[64:128, qc0:qc0 + QTB],
                                     start=True, stop=True, tile_position=(64, 0)
                                     ).then_inc(S["sQK1"], 1)

                    def norm_bc(up):
                        if up >= NPAIR and up % NPAIR == 0:
                            t.wait_ge(S["sSE"], 8 * (up // NPAIR))
                        for h in range(2):
                            t.wait_ge(S["sRCP"], 2 * up + h + 1)
                            if up >= 1 and h == 0:
                                t.wait_ge(S["sMV2"], up)
                            t.matmul(av_ps[0:64, 1024 + h * 512:1024 + h * 512 + QTB],
                                     ones64f[:, :], recb[:, h * QTB:(h + 1) * QTB],
                                     start=True, stop=True,
                                     tile_position=(0, 0)).then_inc(S["sBC"], 1)

                    def norm_id(up):
                        if up >= NPAIR and up % NPAIR == 0:
                            t.wait_ge(S["sSE"], 8 * (up // NPAIR))
                        t.wait_ge(S["sNRM"], 2 * up + 2)
                        t.matmul(av_ps[64:128, 1024:1024 + QTB], id_bf[0:64, 0:64],
                                 atmp[:, :], start=True, stop=True,
                                 tile_position=(0, 64)).then_inc(S["sMV"], 1)

                    def wout_dc(qp, dc):
                        if dc == 0:
                            t.wait_ge(S["sNRM"], 16 * (qp + 1))
                            t.wait_ge(S["sMV2"], 8 * (qp + 1))
                        if dc >= 2:
                            t.wait_ge(S["sSE"], 8 * qp + dc - 1)
                        for ic in range(8):
                            ins_w = t.matmul(
                                av_ps[:, 1024 + (dc % 2) * 512:1024 + (dc % 2) * 512 + QTB],
                                wout_sb[:, ic * 1024 + dc * 128:ic * 1024 + (dc + 1) * 128],
                                att_sb[:, ic * QTB:(ic + 1) * QTB],
                                start=(ic == 0), stop=(ic == 7))
                        ins_w.then_inc(S["sWO"], 1)

                    def wout(qp):
                        for dc in range(8):
                            wout_dc(qp, dc)

                    av_cur = [0]

                    def emit_av_upto(limit):
                        while av_cur[0] < min(limit, NU * NCH):
                            a = av_cur[0]
                            uu, cc = divmod(a, NCH)
                            gg = (uu % NPAIR) // 2
                            if uu >= 1:
                                if cc == 14:
                                    norm_bc(uu - 1)
                                elif cc == 24:
                                    norm_id(uu - 1)
                                elif (cc in (25, 27, 29, 31)
                                      and (uu - 1) % NPAIR == NPAIR - 1):
                                    wout_dc((uu - 1) // NPAIR, cc - 25)
                                    wout_dc((uu - 1) // NPAIR, cc - 24)
                            t.wait_ge(S["sEXP"], a // 2 + 1)
                            t.wait_ge(S["sEXP1"], a // 2 + 1)
                            if cc == 0:
                                if uu >= 1:
                                    t.wait_ge(S["sAVC"], uu)
                            vcol = (gg * NCH + cc) * 65
                            t.matmul(av_ps[0:65, 0:512], v_sb[:, vcol:vcol + 65],
                                     pslot(0, cc), start=(cc == 0), stop=(cc == NCH - 1))
                            t.matmul(av_ps[0:65, 512:1024], v_sb[:, vcol:vcol + 65],
                                     pslot(1, cc), start=(cc == 0), stop=(cc == NCH - 1)
                                     ).then_inc(S["sAV"], 1)
                            av_cur[0] += 1

                    for P in range(NU * NCH // 2):
                        u, p = divmod(P, NCH // 2)
                        emit_av_upto(2 * P - 14)
                        emit_qk_pair(u, p)
                    emit_av_upto(NU * NCH)
                    norm_bc(NU - 1)
                    norm_id(NU - 1)
                    wout(NQTB - 1)

                @blk.scalar
                def _(sc):
                    sc.add_instruction(mybir.InstLoadActFuncSet(
                        name=nc.get_next_instruction_name(), act_func_set_id=6,
                        ins=[], outs=[]))
                    for u in range(NU):
                        qtb, pr = divmod(u, NPAIR)
                        for p in range(NCH // 2):
                            g1 = u * NCH + 2 * p + 1
                            if g1 >= DEPTH:
                                sc.wait_ge(S["sAV"], g1 - DEPTH + 1)
                            for h in range(2):
                                sc.wait_ge(S["sQK%s" % ("" if h == 0 else "1")],
                                           u * NCH + 2 * p + 2)
                                c0 = (h * DEPTH + (2 * p) % DEPTH) * QTB
                                sc.activation(pbuf[:, c0:c0 + 2 * QTB],
                                              st_ps[:, h * 1024:(h + 1) * 1024],
                                              AF.Exp, bias=negc_sb[:, 2 * pr + h:2 * pr + h + 1],
                                              scale=1.0).then_inc(
                                    S["sEXP%s" % ("" if h == 0 else "1")], 1)

                @blk.vector
                def _(v):
                    def emit_rec(ru):
                        for h in range(2):
                            if ru >= 1:
                                v.wait_ge(S["sBC"], 2 * (ru - 1) + h + 1)
                            v.reciprocal(recb[0:1, h * QTB:(h + 1) * QTB],
                                         av_sb[64:65, h * 512:h * 512 + QTB]
                                         ).then_inc(S["sRCP"], 1)
                            v.drain()

                    for u in range(NU):
                        qtb, pr = divmod(u, NPAIR)
                        v.wait_ge(S["sAV"], NCH * (u + 1))
                        v.tensor_copy(av_sb[0:65, :], av_ps[0:65, 0:1024]
                                      ).then_inc(S["sAVC"], 1)
                        v.drain()
                        emit_rec(u)
                        for h in range(2):
                            v.wait_ge(S["sBC"], 2 * u + h + 1)
                            v.tensor_copy(rec_sb[:, h * QTB:(h + 1) * QTB],
                                          av_ps[0:64, 1024 + h * 512:1024 + h * 512 + QTB])
                        v.wait_ge(S["sWO"], 8 * qtb)
                        v.drain()
                        v.tensor_tensor(att_sb[0:64, pr * QTB:(pr + 1) * QTB],
                                        av_sb[0:64, 0:512], rec_sb[:, 0:QTB], ALU.mult
                                        ).then_inc(S["sNRM"], 1)
                        if u >= 1:
                            v.wait_ge(S["sMV"], u)
                        v.tensor_tensor(atmp[:, :], av_sb[0:64, 512:1024],
                                        rec_sb[:, QTB:2 * QTB], ALU.mult
                                        ).then_inc(S["sNRM"], 1)
                        v.wait_ge(S["sMV"], u + 1)
                        v.tensor_copy(att_sb[64:128, pr * QTB:(pr + 1) * QTB],
                                      av_ps[64:128, 1024:1024 + QTB]).then_inc(S["sMV2"], 1)
                        if pr == NPAIR - 1:
                            qp = qtb
                            par = qp % 2
                            if qp >= 2:
                                v.wait_ge(S["dSp%d" % par], 16 * (qp // 2))
                            for dc in range(8):
                                v.wait_ge(S["sWO"], 8 * qp + dc + 1)
                                v.tensor_copy(s_sb[:, (par * 8 + dc) * QTB:(par * 8 + dc + 1) * QTB],
                                              av_ps[:, 1024 + (dc % 2) * 512:
                                                    1024 + (dc % 2) * 512 + QTB]
                                              ).then_inc(S["sSE"], 1)

                @blk.sync
                def _(sy):
                    sy.wait_ge(S["dQA0"], 16 * (NTT // 2))
                    sy.wait_ge(S["dQA1"], 16 * (NTT // 2))
                    for qtb in range(NQTB):
                        if qtb >= 2:
                            sy.wait_ge(S["sQK"], NCH * NPAIR * (qtb - 1))
                            sy.wait_ge(S["sQK1"], NCH * NPAIR * (qtb - 1))
                        sy.dma_start(qbuf[:, (qtb % 2) * NPAIR * QTB:(qtb % 2 + 1) * NPAIR * QTB],
                                     q_scr[qtb]).then_inc(S["dQ%d" % (qtb % 2)], 16)
                        if qtb >= 1:
                            par = (qtb - 1) % 2
                            sy.wait_ge(S["sSE"], 8 * qtb)
                            sy.dma_start(
                                s_scr[:, :, (qtb - 1) * QTB:qtb * QTB].rearrange(
                                    "dc p q -> p dc q"),
                                s_sb[:, par * 8 * QTB:(par + 1) * 8 * QTB].rearrange(
                                    "p (dc q) -> p dc q", dc=8)).then_inc(S["dSp%d" % par], 16)
                    par = (NQTB - 1) % 2
                    sy.wait_ge(S["sSE"], 8 * NQTB)
                    sy.dma_start(
                        s_scr[:, :, (NQTB - 1) * QTB:NQTB * QTB].rearrange("dc p q -> p dc q"),
                        s_sb[:, par * 8 * QTB:(par + 1) * 8 * QTB].rearrange(
                            "p (dc q) -> p dc q", dc=8)).then_inc(S["dSp%d" % par], 16)

        ab.close()

        if "B" in phases and "C" in phases:
            with nc.Block() as blk:
                @blk.sync
                def _(sy):
                    sy.wait_ge(S["dSp0"], 16 * (NQTB // 2))
                    sy.wait_ge(S["dSp1"], 16 * (NQTB // 2))

        with ExitStack() as pC:
          if "C" in phases:
            EC = pC.enter_context
            wdT_sb = EC(nc.sbuf_tensor("wdT_sb", [128, NHC * 8 * 128], BF16))
            hbuf = EC(nc.sbuf_tensor("hbuf", [128, NHC * TC], BF16))
            xnf_c = EC(nc.sbuf_tensor("xnf_c", [128, 8 * TC], BF16))
            g_c = EC(nc.sbuf_tensor("g_c", [128, 8 * TC], BF16))
            s_c = EC(nc.sbuf_tensor("s_c", [128, 2 * 8 * TC], BF16))
            xc = EC(nc.sbuf_tensor("xc", [128, 3 * 4 * DM], F32))
            wsl = EC(nc.sbuf_tensor("wsl", [128, 2 * 2 * 2 * 8 * 128], BF16))
            sg_c = EC(nc.sbuf_tensor("sg_c", [128, 2 * TC], BF16))
            xn_c = EC(nc.sbuf_tensor("xn_c", [128, DM], BF16))
            pt_c = EC(nc.sbuf_tensor("pt_c", [128, 8 * TC], F32))
            gu_ps = EC(nc.psum_tensor("gu_ps", [128, 2 * TC], F32))
            up_ps = EC(nc.psum_tensor("up_ps", [128, 2 * TC], F32))
            dn_ps = EC(nc.psum_tensor("dn_ps", [128, 2 * TC], F32))
            xT_ps = EC(nc.psum_tensor("xT_ps", [128, DM], BF16))
            pT_ps = EC(nc.psum_tensor("pT_ps", [128, TC], F32))

            SLABW = 2 * 8 * 128

            def wslab(buf, mat, hcs, dc):
                c0 = buf * 2 * SLABW + mat * SLABW + (hcs * 8 + dc) * 128
                return wsl[:, c0:c0 + 128]

            if cmode == "gate":
                _skip_c = True
            else:
                _skip_c = False
            if not _skip_c and cmode != "full":
                with nc.Block() as blk:
                    @blk.gpsimd
                    def _(g):
                        g.memset(g_c[:, :], 1.0)

            if _skip_c:
                with nc.Block() as blk:
                    @blk.sync
                    def _(sy):
                        pass
            if not _skip_c:
              with nc.Block() as blk:
                @blk.scalar
                def _(sc):
                    sc.preload_act_table(AF.Silu)
                    if cmode == "full":
                        if "B" in phases:
                            sc.wait_ge(S["dSp0"], 16 * (NQTB // 2))
                            sc.wait_ge(S["dSp1"], 16 * (NQTB // 2))
                        sc.dma_start(wdT_sb[:, :].rearrange("p (hc dc f) -> p hc dc f",
                                                            hc=NHC, dc=8),
                                     wdT_scr).then_inc(S["dStr"], 16)

                @blk.sync
                def _(sy):
                    def _emit_out(oc):
                        sy.wait_ge(S["sAD"], 8 * (oc + 1))
                        sy.dma_start(out_h[oc * TC:(oc + 1) * TC, :].rearrange(
                                         "(tt p) f -> p tt f", p=128),
                                     xc[:, (oc % 3) * 4 * DM:(oc % 3 + 1) * 4 * DM
                                        ].rearrange("p (tt f) -> p tt f", tt=4)
                                     ).then_inc(S["dOUT%d" % (oc % 2)], 16)

                    for tc in range(NTC):
                        if cmode != "cout":
                            if tc >= 3 and cmode not in ("cx",):
                                sy.wait_ge(S["dOUT%d" % ((tc - 3) % 2)],
                                           16 * ((tc - 3) // 2 + 1))
                            sy.dma_start(xc[:, (tc % 3) * 4 * DM:(tc % 3 + 1) * 4 * DM
                                            ].rearrange("p (tt f) -> p tt f", tt=4),
                                         x_h[tc * TC:(tc + 1) * TC, :].rearrange(
                                             "(tt p) f -> p tt f", p=128)
                                         ).then_inc(S["dXC%d" % (tc % 3)], 16)
                        if tc >= 2 and cmode not in ("cx",):
                            _emit_out(tc - 2)
                        if cmode != "cxout":
                            if tc >= 2:
                                sy.wait_ge(S["sPm"], 8 * (tc - 1))
                            sy.dma_start(s_c[:, (tc % 2) * 8 * TC:(tc % 2 + 1) * 8 * TC
                                              ].rearrange("p (dc q) -> p dc q", dc=8),
                                         s_scr[:, :, tc * TC:(tc + 1) * TC].rearrange(
                                             "dc p q -> p dc q")
                                         ).then_inc(S["dSr%d" % (tc % 2)], 16)
                    if cmode not in ("cx",):
                        _emit_out(NTC - 2)
                        _emit_out(NTC - 1)
                    if cmode not in ("cx",):
                        sy.wait_ge(S["dOUT0"], 16 * (NTC // 2))
                        sy.wait_ge(S["dOUT1"], 16 * (NTC // 2))
                    else:
                        sy.wait_ge(S["dXC0"], 16 * (NTC // 2))
                        sy.wait_ge(S["dXC1"], 16 * (NTC // 2))

                @blk.gpsimd
                def _(g):
                    if "B" in phases and cmode != "cxout":
                        g.wait_ge(S["dSp0"], 16 * (NQTB // 2))
                        g.wait_ge(S["dSp1"], 16 * (NQTB // 2))
                    for tc in range(NTC if cmode != "cxout" else 0):
                        for sl_ in range(NSLAB if cmode == "full" else 0):
                            gs = tc * NSLAB + sl_
                            if gs >= 2:
                                g.wait_ge(S["sGU"], 2 * (gs - 2) + 2)
                            buf = gs % 2
                            g.dma_start(
                                wsl[:, buf * 2 * SLABW:buf * 2 * SLABW + SLABW
                                    ].rearrange("p (hc dc f) -> p hc dc f", hc=2, dc=8),
                                wgT_scr[:, sl_ * 2:(sl_ + 1) * 2]
                                ).then_inc(S["dSl%d" % (gs % 2)], 16)
                            g.dma_start(
                                wsl[:, buf * 2 * SLABW + SLABW:buf * 2 * SLABW + 2 * SLABW
                                    ].rearrange("p (hc dc f) -> p hc dc f", hc=2, dc=8),
                                wuT_scr[:, sl_ * 2:(sl_ + 1) * 2]
                                ).then_inc(S["dSl%d" % (gs % 2)], 16)

                def _dve_combine(v, ct):
                    for dc in range(8):
                        D = 8 * ct + dc
                        if cmode in ("motion", "cxout", "cx", "cout"):
                            if cmode == "motion":
                                v.wait_ge(S["dSr%d" % (ct % 2)], 16 * (ct // 2 + 1))
                            v.engine_nop().then_inc(S["sPm"], 1)
                            v.engine_nop().then_inc(S["sAD"], 1)
                            continue
                        if cmode == "full":
                            v.wait_ge(S["sG"], D + 1)
                        v.wait_ge(S["dSr%d" % (ct % 2)], 16 * (ct // 2 + 1))
                        if D >= 2:
                            v.wait_ge(S["sPT2"], D - 1)
                        v.tensor_tensor(pt_c[:, dc * TC:(dc + 1) * TC],
                                        s_c[:, ((ct % 2) * 8 + dc) * TC:
                                            ((ct % 2) * 8 + dc + 1) * TC],
                                        g_c[:, dc * TC:(dc + 1) * TC],
                                        ALU.mult).then_inc(S["sPm"], 1)
                        v.wait_ge(S["sPT2"], D + 1)
                        v.tensor_tensor(
                            xc[:, (ct % 3) * 4 * DM:(ct % 3 + 1) * 4 * DM
                               ].rearrange("p (tt f) -> p tt f", tt=4
                                           )[:, :, dc * 128:(dc + 1) * 128],
                            xc[:, (ct % 3) * 4 * DM:(ct % 3 + 1) * 4 * DM
                               ].rearrange("p (tt f) -> p tt f", tt=4
                                           )[:, :, dc * 128:(dc + 1) * 128],
                            pT_ps[:, :].rearrange("p (tt f) -> p tt f", tt=4),
                            ALU.add).then_inc(S["sAD"], 1)

                def _dve_pt(v, ct):
                    for dc in range(8):
                        v.wait_ge(S["sG"], 8 * ct + dc + 1)
                        v.wait_ge(S["dSr%d" % (ct % 2)], 16 * (ct // 2 + 1))
                        if ct >= 1:
                            v.wait_ge(S["sPT2"], 8 * (ct - 1) + dc + 1)
                        v.tensor_tensor(pt_c[:, dc * TC:(dc + 1) * TC],
                                        s_c[:, ((ct % 2) * 8 + dc) * TC:
                                            ((ct % 2) * 8 + dc + 1) * TC],
                                        g_c[:, dc * TC:(dc + 1) * TC],
                                        ALU.mult).then_inc(S["sPm"], 1)

                def _add1(v, ct, dc):
                    v.wait_ge(S["sPT2"], 8 * ct + dc + 1)
                    v.tensor_tensor(
                        xc[:, (ct % 3) * 4 * DM:(ct % 3 + 1) * 4 * DM
                           ].rearrange("p (tt f) -> p tt f", tt=4
                                       )[:, :, dc * 128:(dc + 1) * 128],
                        xc[:, (ct % 3) * 4 * DM:(ct % 3 + 1) * 4 * DM
                           ].rearrange("p (tt f) -> p tt f", tt=4
                                       )[:, :, dc * 128:(dc + 1) * 128],
                        pT_ps[:, :].rearrange("p (tt f) -> p tt f", tt=4),
                        ALU.add).then_inc(S["sAD"], 1)

                def _dve_add(v, ct):
                    for dc in range(8):
                        _add1(v, ct, dc)

                def _xnf_ts(v, tcx, t4):
                    tt = tcx * 4 + t4
                    v.wait_ge(S["dXC%d" % (tcx % 3)], 16 * (tcx // 3 + 1))
                    if tcx >= 1 and t4 == 0:
                        v.wait_ge(S["sGU"], NHC * tcx)
                    v.tensor_scalar(xn_c[:, :],
                                    xc[:, ((tcx % 3) * 4 + t4) * DM:
                                       ((tcx % 3) * 4 + t4 + 1) * DM],
                                    stats_sb[:, tt:tt + 1],
                                    stats_sb[:, NTT + tt:NTT + tt + 1],
                                    ALU.subtract, ALU.mult).then_inc(S["sXN"], 1)

                def _xnf_copy(v, tcx, t4):
                    tt = tcx * 4 + t4
                    v.wait_ge(S["sP"], C_SP_BASE + tt + 1)
                    v.tensor_copy(xnf_c[:, :].rearrange("p (dc f) -> p dc f", dc=8
                                                        )[:, :, t4 * 128:(t4 + 1) * 128],
                                  xT_ps[:, :].rearrange("p (dc f) -> p dc f", dc=8)
                                  ).then_inc(S["sXNF"], 1)

                def _xnf_prod(v, tcx):
                    for t4 in range(4):
                        _xnf_ts(v, tcx, t4)
                        _xnf_copy(v, tcx, t4)

                @blk.vector
                def _(v):
                    for tc in range(NTC):
                        if cmode == "full":
                            if tc == 0:
                                _xnf_prod(v, 0)
                            if tc >= 1:
                                _dve_pt(v, tc - 1)
                        for hc in range(NHC if cmode == "full" else 0):
                            H = NHC * tc + hc
                            v.wait_ge(S["sSI"], H + 1)
                            if tc >= 1 and hc == 0:
                                v.wait_ge(S["sDN"], 8 * tc)
                            v.tensor_tensor(hbuf[:, hc * TC:(hc + 1) * TC],
                                            sg_c[:, (hc % 2) * TC:(hc % 2 + 1) * TC],
                                            up_ps[:, (hc % 2) * TC:(hc % 2 + 1) * TC],
                                            ALU.mult).then_inc(S["sH"], 1)
                        if cmode == "full":
                            if tc + 1 < NTC and tc >= 1:
                                _xnf_ts(v, tc + 1, 0)
                                for dc in range(4):
                                    _add1(v, tc - 1, dc)
                                _xnf_copy(v, tc + 1, 0)
                                for t4 in range(1, 4):
                                    _xnf_ts(v, tc + 1, t4)
                                    _add1(v, tc - 1, 3 + t4)
                                    _xnf_copy(v, tc + 1, t4)
                                _add1(v, tc - 1, 7)
                            elif tc + 1 < NTC:
                                _xnf_prod(v, tc + 1)
                            elif tc >= 1:
                                _dve_add(v, tc - 1)
                        elif tc >= 1:
                            _dve_combine(v, tc - 1)

                    if cmode == "full":
                        _dve_pt(v, NTC - 1)
                        _dve_add(v, NTC - 1)
                    else:
                        _dve_combine(v, NTC - 1)

                @blk.scalar
                def _(sc):
                    for tc in range(NTC if cmode == "full" else 0):
                        for hc in range(NHC):
                            H = NHC * tc + hc
                            sc.wait_ge(S["sGU"], H + 1)
                            if H >= 1:
                                sc.wait_ge(S["sH"], H - 1)
                            sc.activation(sg_c[:, (hc % 2) * TC:(hc % 2 + 1) * TC],
                                          gu_ps[:, (hc % 2) * TC:(hc % 2 + 1) * TC],
                                          AF.Silu, scale=1.0 / 127.0).then_inc(S["sSI"], 1)
                        for dc in range(8):
                            D = 8 * tc + dc
                            sc.wait_ge(S["sDN"], D + 1)
                            if tc >= 1:
                                sc.wait_ge(S["sPm"], 8 * (tc - 1) + dc + 1)
                            sc.activation(g_c[:, dc * TC:(dc + 1) * TC],
                                          dn_ps[:, (dc % 2) * TC:(dc % 2 + 1) * TC],
                                          AF.Silu, scale=DQ2).then_inc(S["sG"], 1)

                def _pe_pt(t, ct, dc):
                    if cmode == "motion":
                        return
                    D = 8 * ct + dc
                    t.wait_ge(S["sPm"], D + 1)
                    if D >= 1:
                        t.wait_ge(S["sAD"], D)
                    for t4 in range(4):
                        ins = t.transpose(pT_ps[:, t4 * 128:(t4 + 1) * 128],
                                          pt_c[:, dc * TC + t4 * 128:
                                               dc * TC + (t4 + 1) * 128],
                                          id_f[:, :])
                    ins.then_inc(S["sPT2"], 1)

                def _xt_group(t, tcx, t4):
                    tt = tcx * 4 + t4
                    t.wait_ge(S["sXN"], tt + 1)
                    if tt >= 1:
                        t.wait_ge(S["sXNF"], tt)
                    for dc in range(8):
                        ins = t.transpose(xT_ps[:, dc * 128:(dc + 1) * 128],
                                          xn_c[:, dc * 128:(dc + 1) * 128], id_bf[:, :])
                    ins.then_inc(S["sP"], 1)

                @blk.tensor
                def _(t):
                    for tc in range(NTC):
                        if cmode == "full" and tc == 0:
                            for t4 in range(4):
                                _xt_group(t, 0, t4)
                        if cmode == "full":
                            t.wait_ge(S["sXNF"], 4 * (tc + 1))
                        for hc in range(NHC if cmode == "full" else 0):
                            H = NHC * tc + hc
                            gs = tc * NSLAB + hc // 2
                            t.wait_ge(S["dSl%d" % (gs % 2)], 32 * (gs // 2 + 1))
                            if H >= 2:
                                t.wait_ge(S["sSI"], H - 1)
                                t.wait_ge(S["sH"], H - 1)
                            buf = gs % 2
                            for dc in range(8):
                                t.matmul(gu_ps[:, (hc % 2) * TC:(hc % 2 + 1) * TC],
                                         wslab(buf, 0, hc % 2, dc),
                                         xnf_c[:, dc * TC:(dc + 1) * TC],
                                         start=(dc == 0), stop=(dc == 7))
                            for dc in range(8):
                                ins = t.matmul(up_ps[:, (hc % 2) * TC:(hc % 2 + 1) * TC],
                                               wslab(buf, 1, hc % 2, dc),
                                               xnf_c[:, dc * TC:(dc + 1) * TC],
                                               start=(dc == 0), stop=(dc == 7))
                            ins.then_inc(S["sGU"], 1)
                        if cmode == "full":
                            t.wait_ge(S["sH"], NHC * (tc + 1))
                        for dc in range(8 if cmode == "full" else 0):
                            D = 8 * tc + dc
                            if D == 0:
                                t.wait_ge(S["dStr"], 16)
                            if tc >= 1:
                                _pe_pt(t, tc - 1, dc)
                            if tc + 1 < NTC and dc >= 4:
                                _xt_group(t, tc + 1, dc - 4)
                            if D >= 2:
                                t.wait_ge(S["sG"], D - 1)
                            for hc in range(NHC):
                                ins = t.matmul(dn_ps[:, (dc % 2) * TC:(dc % 2 + 1) * TC],
                                               wdT_sb[:, (hc * 8 + dc) * 128:
                                                      (hc * 8 + dc + 1) * 128],
                                               hbuf[:, hc * TC:(hc + 1) * TC],
                                               start=(hc == 0), stop=(hc == NHC - 1))
                            ins.then_inc(S["sDN"], 1)
                        if cmode != "full" and tc >= 1:
                            for dc in range(8):
                                _pe_pt(t, tc - 1, dc)
                    if cmode != "motion":
                        for dc in range(8):
                            _pe_pt(t, NTC - 1, dc)

        if "C" in phases and cmode != "gate":
            if any(k in tp for k in ("gg", "hb", "xnf")):
                with nc.Block() as blk:
                    @blk.sync
                    def _(sy):
                        for name, t_ in [("gg", g_c), ("hb", hbuf), ("xnf", xnf_c)]:
                            if name in tp:
                                sy.dma_start(tp[name], t_[:, :]).then_inc(S["dS"], 16)
                                nS[0] += 16
                        sy.wait_ge(S["dS"], nS[0])
            if cmode not in ("cx",):
                with nc.Block() as blk:
                    @blk.sync
                    def _(sy):
                        sy.wait_ge(S["dOUT0"], 16 * (NTC // 2))
                        sy.wait_ge(S["dOUT1"], 16 * (NTC // 2))

    return nc


_CACHE = {}


def kernel(L0_attn_x, L0_attn_ln_scale, L0_attn_ln_bias, L0_attn_Wq, L0_attn_Wkv,
           L0_attn_qn_scale, L0_attn_qn_bias, L0_attn_kn_scale, L0_attn_kn_bias,
           L0_attn_Wout, L0_ffn_w_gate, L0_ffn_w_up, L0_ffn_w_down):
    x = np.ascontiguousarray(np.asarray(L0_attn_x, dtype=np.float32))
    shared = {
        "wq": np.ascontiguousarray(np.asarray(L0_attn_Wq, np.float32)),
        "wkv": np.ascontiguousarray(np.asarray(L0_attn_Wkv, np.float32)),
        "wout": np.ascontiguousarray(np.asarray(L0_attn_Wout, np.float32)),
        "wg": np.ascontiguousarray(np.asarray(L0_ffn_w_gate, np.int32)),
        "wu": np.ascontiguousarray(np.asarray(L0_ffn_w_up, np.int32)),
        "wd": np.ascontiguousarray(np.asarray(L0_ffn_w_down, np.int32)),
        "ln_sc": np.ascontiguousarray(np.asarray(L0_attn_ln_scale, np.float32)),
        "ln_b": np.ascontiguousarray(np.asarray(L0_attn_ln_bias, np.float32)),
        "qn_sc": np.ascontiguousarray(np.asarray(L0_attn_qn_scale, np.float32)),
        "qn_b": np.ascontiguousarray(np.asarray(L0_attn_qn_bias, np.float32)),
        "kn_sc": np.ascontiguousarray(np.asarray(L0_attn_kn_scale, np.float32)),
        "kn_b": np.ascontiguousarray(np.asarray(L0_attn_kn_bias, np.float32)),
    }
    if "nc" not in _CACHE:
        _CACHE["nc"] = build()
    nc = _CACHE["nc"]
    in_maps = [dict(shared, x=np.ascontiguousarray(x[b])) for b in range(8)]
    res = run_bass_kernel_spmd(nc, in_maps, core_ids=list(range(8)))
    return np.stack([res.results[b]["out"] for b in range(8)], axis=0)
